# version 01
# baseline (speedup 1.0000x reference)
import math
import numpy as np

import concourse.bass as bass
import concourse.mybir as mybir
from concourse.bass_utils import run_bass_kernel_spmd
from concourse.bass_types import AP

F32 = mybir.dt.float32
BF16 = mybir.dt.bfloat16
AF = mybir.ActivationFunctionType
ALU = mybir.AluOpType
AX = mybir.AxisListType.X

NCORES = 8
D = 1024
T = 2048
NH, DH = 16, 64
JC = 512
NJC = T // JC
NCH = D // 128
FF = 2048
PW = 2560
PPAD = 4608
PTRIM = 4480

HPERM = []
for _c in range(8):
    HPERM += [_c, _c + 8]
GORDER = [0, 2, 1, 3]

ENGS = ("sy", "gp", "pe", "ac", "ve")
K = 1024



def _host_layout(inputs):
    f32 = lambda a: np.ascontiguousarray(np.asarray(a), dtype=np.float32)
    col8 = lambda a: f32(np.asarray(a, np.float32).reshape(8, 128).T)
    w = {}
    w["ident"] = np.eye(128, dtype=np.float32)

    for nm, key in (("W_q", "aq"), ("W_k", "ak"), ("W_v", "av"), ("W_o", "ao")):
        w[key] = f32(inputs[f"L0_attn_a_{nm}"])
    cent = np.asarray(inputs["L0_attn_a_centers"], np.float32)
    cp = np.zeros((128, 8, 64), np.float32)
    for p in range(8):
        cp[0:64, p, :] = cent[2 * p].T
        cp[64:128, p, :] = cent[2 * p + 1].T
    w["acent"] = f32(cp)
    w["als"] = f32(inputs["L0_attn_a_log_scales"])
    w["aamp"] = f32(inputs["L0_attn_a_amplitudes"])

    qkv = np.asarray(inputs["L0_ffn_qkv_weight"], np.float32)
    qw = qkv[:NH * DH].reshape(NH, DH, D)[HPERM].reshape(NH * DH, D)
    kw = qkv[NH * DH:(NH + 4) * DH].reshape(4, DH, D)[GORDER].reshape(4 * DH, D)
    vw = qkv[(NH + 4) * DH:].reshape(4, DH, D)[GORDER].reshape(4 * DH, D)
    w["fqkv"] = f32(np.concatenate([qw, kw, vw], 0).T)
    pj = np.asarray(inputs["L0_ffn_proj_weight"], np.float32)
    w["fproj"] = f32(pj.T.reshape(NH, DH, D)[HPERM].reshape(D, D))
    w["frout"] = f32(np.asarray(inputs["L0_ffn_router_weight"], np.float32).T)
    w["fw1"] = f32(inputs["L0_ffn_moe_w1"])
    w["fw2"] = f32(inputs["L0_ffn_moe_w2"])
    w["fln1w"] = col8(inputs["L0_ffn_ln1_weight"])
    w["fln1b"] = col8(inputs["L0_ffn_ln1_bias"])
    w["fln2w"] = col8(inputs["L0_ffn_ln2_weight"])
    w["fln2b"] = col8(inputs["L0_ffn_ln2_bias"])

    for nm, key in (("Wq", "bwq"), ("Wk", "bwk"), ("Wv", "bwv"),
                    ("Wout", "bwo"), ("Wpos", "bwp")):
        w[key] = f32(np.asarray(inputs[f"L0_attn_b_{nm}"], np.float32).T)
    w["bbq"] = col8(inputs["L0_attn_b_bq"])
    w["bbk"] = col8(inputs["L0_attn_b_bk"])
    w["bbout"] = col8(inputs["L0_attn_b_bout"])
    w["bg"] = col8(inputs["L0_attn_b_ln_gamma"])
    w["bb"] = col8(inputs["L0_attn_b_ln_beta"])
    w["bbv_row"] = f32(np.asarray(inputs["L0_attn_b_bv"], np.float32).reshape(1, D))
    w["bu"] = col8(np.asarray(inputs["L0_attn_b_u_bias"], np.float32).reshape(D))
    w["bv"] = col8(np.asarray(inputs["L0_attn_b_v_bias"], np.float32).reshape(D))

    pos = np.arange(T, dtype=np.float64)[:, None]
    div = np.exp(np.arange(0, D, 2, dtype=np.float64) * (-math.log(10000.0) / D))
    ang = pos * div
    pe_pos = np.stack([np.sin(ang), np.cos(ang)], -1).reshape(T, D)
    pe_neg = np.stack([np.sin(-ang), np.cos(-ang)], -1).reshape(T, D)
    pe = np.concatenate([pe_pos[::-1], pe_neg[1:]], 0)
    pet = np.zeros((D, PPAD), np.float64)
    pet[:, :2 * T - 1] = pe.T
    w["pet"] = f32(pet)

    m16 = np.zeros((128, 8, 16), np.float32)
    for c in range(8):
        m16[0:64, c, 2 * c] = 0.125
        m16[64:128, c, 2 * c + 1] = 0.125
    w["mask16"] = f32(m16)
    s16 = np.zeros((16, 8, 128), np.float32)
    for c in range(8):
        s16[2 * c, c, 0:64] = 1.0
        s16[2 * c + 1, c, 64:128] = 1.0
    w["sel16"] = f32(s16)
    d8 = np.zeros((8, 8, 128), np.float32)
    for e in range(8):
        d8[e, e, :] = 1.0
    w["delta8"] = f32(d8)
    return w


def _core_inputs(shared, inputs, core):
    d = dict(shared)
    x = np.asarray(inputs["L0_attn_a_x"], np.float32)[core]
    d["x"] = np.ascontiguousarray(x.T)
    tm = np.zeros((16, 8), np.float32)
    tm[:, :core + 1] = 1.0
    d["tmask8"] = tm
    return d



class _Step:
    __slots__ = ("eng", "fn", "after", "tgt", "kind")

    def __init__(self, eng, fn, after, kind="op"):
        self.eng = eng
        self.fn = fn
        self.after = list(after)
        self.tgt = None
        self.kind = kind


class _StepCtx:

    def __init__(self, sched, eng):
        self.sched = sched
        self.eng = eng
        self.ndma = 0

    def dma(self, inst):
        self.sched.dcount[self.eng] += 16
        inst.then_inc(self.sched.dsem[self.eng], 16)
        self.ndma += 1
        return inst

    def sync(self, e, inst):
        self.sched.count[self.eng] += 1
        inst.then_inc(self.sched.sem[self.eng], 1)
        e.wait_ge(self.sched.sem[self.eng], self.sched.count[self.eng])
        return inst


class Sched:
    def __init__(self, nc):
        self.nc = nc
        self.sem = {e: nc.alloc_semaphore("ec_" + e) for e in ENGS}
        self.dsem = {e: nc.alloc_semaphore("ed_" + e) for e in ENGS}
        self.count = {e: 0 for e in ENGS}
        self.dcount = {e: 0 for e in ENGS}
        self.pending = []

    def step(self, eng, fn, after=(), kind="op"):
        st = _Step(eng, fn, after, kind)
        self.pending.append(st)
        return st

    def barrier(self):
        self.flush()
        snapc = dict(self.count)
        snapd = dict(self.dcount)
        toks = []
        for e in ENGS:
            waits = []
            for f in ENGS:
                if f != e:
                    waits.append((self.sem[f], snapc[f]))
                    waits.append((self.dsem[f], snapd[f]))
            waits.append((self.dsem[e], snapd[e]))
            toks.append(self.step(e, None, after=waits, kind="bar"))
        self.flush()
        return toks

    def flush(self):
        if not self.pending:
            return
        steps = self.pending
        self.pending = []
        with self.nc.Block() as blk:
            bodies = {"sy": blk.sync, "gp": blk.gpsimd, "pe": blk.tensor,
                      "ac": blk.scalar, "ve": blk.vector}
            i = 0
            while i < len(steps):
                j = i
                eng = steps[i].eng
                while j < len(steps) and steps[j].eng == eng:
                    j += 1
                group = steps[i:j]

                def run(e, group=group):
                    for st in group:
                        for a in st.after:
                            if isinstance(a, tuple):
                                sh, val = a
                            else:
                                assert a.tgt is not None, "dep on later step"
                                sh, val = a.tgt
                            if val > 0:
                                e.wait_ge(sh, val)
                        ctx = _StepCtx(self, st.eng)
                        if st.kind == "bar":
                            last = e.nop()
                        else:
                            last = st.fn(e, ctx)
                        if last is not None:
                            assert ctx.ndma == 0, "step mixes dma and compute"
                            self.count[st.eng] += 1
                            last.then_inc(self.sem[st.eng], 1)
                            st.tgt = (self.sem[st.eng], self.count[st.eng])
                        elif ctx.ndma > 0:
                            st.tgt = (self.dsem[st.eng], self.dcount[st.eng])
                        else:
                            self.count[st.eng] += 1
                            e.sem_inc(self.sem[st.eng], 1)
                            st.tgt = (self.sem[st.eng], self.count[st.eng])
                bodies[eng](run)
                i = j



def build(stage=4, sub=99):
    nc = bass.Bass("TRN2", target_bir_lowering=False, debug=False,
                   detect_race_conditions=False)
    sch = Sched(nc)
    g = {}
    sb = {}
    ps = {}

    for name, shape in (
        ("x", (D, T)), ("ident", (128, 128)),
        ("aq", (D, D)), ("ak", (D, D)), ("av", (D, D)), ("ao", (D, D)),
        ("acent", (128, 8, 64)), ("als", (16, 64)), ("aamp", (16, 64)),
        ("fqkv", (D, 1536)), ("fproj", (D, D)), ("frout", (D, 8)),
        ("fw1", (8, D, FF)), ("fw2", (8, FF, D)),
        ("fln1w", (128, 8)), ("fln1b", (128, 8)),
        ("fln2w", (128, 8)), ("fln2b", (128, 8)),
        ("bwq", (D, D)), ("bwk", (D, D)), ("bwv", (D, D)), ("bwo", (D, D)),
        ("bwp", (D, D)),
        ("bbq", (128, 8)), ("bbk", (128, 8)), ("bbout", (128, 8)),
        ("bg", (128, 8)), ("bb", (128, 8)), ("bbv_row", (1, D)),
        ("bu", (128, 8)), ("bv", (128, 8)),
        ("pet", (D, PPAD)),
        ("mask16", (128, 8, 16)), ("sel16", (16, 8, 128)),
        ("delta8", (8, 8, 128)), ("tmask8", (16, 8)),
    ):
        g[name] = nc.dram_tensor(name, list(shape), F32, kind="ExternalInput").ap()
    y_d = nc.dram_tensor("y", [D, T], F32, kind="ExternalOutput").ap()

    g["xspill"] = nc.dram_tensor("xspill", [D, T], F32, kind="Internal").ap()
    g["ps_scr"] = nc.dram_tensor("ps_scr", [16, 16, 128, PW], BF16, kind="Internal").ap()
    g["p_scr"] = nc.dram_tensor("p_scr", [16, T, T], BF16, kind="Internal").ap()
    g["pd"] = nc.dram_tensor("pd", [8, 128, PTRIM], BF16, kind="Internal").ap()
    g["agin"] = nc.dram_tensor("agin", [512, T], BF16, kind="Internal")
    g["agout"] = nc.dram_tensor("agout", [8 * 512, T], BF16, kind="Internal")

    def asb(name, shape, dt=F32):
        sb[name] = nc.alloc_sbuf_tensor("s_" + name, list(shape), dt)
        return sb[name]

    asb("xb", (128, NCH, T), BF16)
    asb("xb_stat", (128, NCH, JC), BF16)
    asb("lnt", (128, JC))
    asb("rowA", (1, T)); asb("rowB", (1, T)); asb("rowC", (1, T))
    asb("lnw", (128, 8)); asb("lnb", (128, 8))
    asb("ones_col", (128, 1), BF16)
    asb("onescol_f", (128, 1))
    asb("ones_row", (1, 128), BF16)
    asb("ones_rowf", (1, 128), F32)
    asb("ones512", (1, JC), BF16)
    asb("identf", (128, 128), F32)
    asb("cdb", (1, 4))
    nc.const_aps.register(F32, 1e-5, sb["cdb"][0:1, 0:1])
    nc.const_aps.register(F32, 1e-6, sb["cdb"][0:1, 1:2])
    asb("mask16b", (128, 8, 16), BF16)
    asb("tmask", (16, 8))
    asb("sel16b", (16, 8, 128), BF16)
    asb("nmcol", (128, 1)); asb("sumcol", (128, 1))
    asb("accA", (128, 1)); asb("accB", (128, 1))
    asb("bias1", (128, 8)); asb("bias2", (128, 8))

    arena = nc.alloc_sbuf_tensor("s_arena", [128, 130 * K], mybir.dt.uint8)
    AR0 = nc.lookup_mloc(arena).addr
    _pl = {}

    def place(name, shape, dt, off):
        key = (name, off)
        if key not in _pl:
            _pl[key] = nc.alloc_sbuf_tensor_at(
                "a_" + name, list(shape), dt, offset=AR0 + off)
        return _pl[key]

    x_sb = place("x", (128, NCH, T), F32, 0)

    ps["A"] = nc.alloc_psum_tensor("psA", [128, 2048], F32)
    ps["B"] = nc.alloc_psum_tensor("psB", [128, 512], F32)
    ps["C"] = nc.alloc_psum_tensor("psC", [128, 512], F32)
    ps["D"] = nc.alloc_psum_tensor("psD", [128, 512], F32)
    ps["E"] = nc.alloc_psum_tensor("psE", [128, 512], F32)
    psA4 = [ps["A"][:, i * JC:(i + 1) * JC] for i in range(4)]

    step = sch.step
    bar = sch.barrier
    _gate_store = [None]

    def _init(e, st):
        st.dma(e.dma_start(out=sb["identf"][:, :], in_=g["ident"][:, :]))
        st.dma(e.dma_start(out=x_sb[:, :, :],
                           in_=g["x"].rearrange("(c p) t -> p c t", p=128)))
        st.dma(e.dma_start(out=sb["tmask"][:, :], in_=g["tmask8"][:, :]))
        return None
    ld0 = step("sy", _init)

    def _ldb16(e, st):
        st.dma(e.dma_start(out=sb["mask16b"][:, :, :], in_=g["mask16"][:, :, :]))
        st.dma(e.dma_start(out=sb["sel16b"][:, :, :], in_=g["sel16"][:, :, :]))
        return None
    ldb = step("gp", _ldb16)

    def _consts(e, st):
        e.memset(sb["cdb"][0:1, 0:1], 1e-5)
        e.memset(sb["cdb"][0:1, 1:2], 1e-6)
        e.memset(sb["ones_col"][:, :], 1.0)
        e.memset(sb["onescol_f"][:, :], 1.0)
        e.memset(sb["ones_row"][:, :], 1.0)
        e.memset(sb["ones_rowf"][:, :], 1.0)
        return e.memset(sb["ones512"][:, :], 1.0)
    c0 = step("ve", _consts)

    def emit_ln(src3, out3, kind="plain", eps=1e-5, dep=()):
        stats, evacs = [], []
        prev_cast = None
        for jc in range(NJC):
            def _cast(e, st, jc=jc):
                last = None
                for c in range(NCH):
                    last = e.tensor_copy(sb["xb_stat"][:, c, :],
                                         src3[:, c, jc * JC:(jc + 1) * JC])
                return last
            depc = list(dep) if jc == 0 else [stats[jc - 1][1]]
            t_cast = step("ve", _cast, after=depc)

            def _sum(e, st, jc=jc):
                last = None
                for c in range(NCH):
                    last = e.matmul(ps["B"][0:1, :], sb["ones_col"][:, :],
                                    sb["xb_stat"][:, c, :],
                                    start=(c == 0), stop=(c == NCH - 1))
                return last
            deps = [t_cast] if jc == 0 else [t_cast, evacs[jc - 1]]
            t_sum = step("pe", _sum, after=deps)

            def _sq(e, st, jc=jc):
                return e.activation(sb["xb_stat"][:, :, :], sb["xb_stat"][:, :, :],
                                    AF.Square)
            t_sq = step("ac", _sq, after=[t_sum])

            def _sum2(e, st, jc=jc):
                last = None
                for c in range(NCH):
                    last = e.matmul(ps["C"][0:1, :], sb["ones_col"][:, :],
                                    sb["xb_stat"][:, c, :],
                                    start=(c == 0), stop=(c == NCH - 1))
                return last
            t_sum2 = step("pe", _sum2, after=[t_sq])
            stats.append((t_sum, t_sum2))

            def _evac(e, st, jc=jc):
                e.tensor_copy(sb["rowA"][0:1, jc * JC:(jc + 1) * JC], ps["B"][0:1, :])
                return e.tensor_copy(sb["rowB"][0:1, jc * JC:(jc + 1) * JC],
                                     ps["C"][0:1, :])
            evacs.append(step("ve", _evac, after=[t_sum2]))

        def _rows1(e, st):
            e.tensor_scalar(sb["rowA"][0:1, :], sb["rowA"][0:1, :], 1.0 / D, None, ALU.mult)
            e.tensor_scalar(sb["rowB"][0:1, :], sb["rowB"][0:1, :], 1.0 / D, None, ALU.mult)
            e.tensor_mul(sb["rowC"][0:1, :], sb["rowA"][0:1, :], sb["rowA"][0:1, :])
            return e.tensor_sub(sb["rowB"][0:1, :], sb["rowB"][0:1, :], sb["rowC"][0:1, :])
        rows1 = step("ve", _rows1, after=[evacs[-1]])

        def _rows2(e, st):
            e.activation(sb["rowB"][0:1, :], sb["rowB"][0:1, :], AF.Ln, bias=eps)
            return e.activation(sb["rowB"][0:1, :], sb["rowB"][0:1, :], AF.Exp, scale=-0.5)
        rows2 = step("ac", _rows2, after=[rows1])

        def _rows3(e, st):
            return e.scalar_tensor_tensor(sb["rowA"][0:1, :], sb["rowA"][0:1, :], -1.0,
                                          sb["rowB"][0:1, :], ALU.mult, ALU.mult)
        rows3 = step("ve", _rows3, after=[rows2])

        prev_app = None
        for jc in range(NJC):
            def _bc(e, st, jc=jc):
                e.matmul(ps["D"][:, :], sb["ones_rowf"][:, :],
                         sb["rowB"][0:1, jc * JC:(jc + 1) * JC], start=True, stop=True)
                return e.matmul(ps["E"][:, :], sb["ones_rowf"][:, :],
                                sb["rowA"][0:1, jc * JC:(jc + 1) * JC], start=True, stop=True)
            depb = [rows3] if prev_app is None else [rows3, prev_app]
            t_bc = step("pe", _bc, after=depb)

            def _app(e, st, jc=jc):
                last = None
                for c in range(NCH):
                    xs = src3[:, c, jc * JC:(jc + 1) * JC]
                    ov = out3[:, c, jc * JC:(jc + 1) * JC]
                    e.tensor_mul(sb["lnt"][:, :], xs, ps["D"][:, :])
                    if kind == "plain":
                        last = e.tensor_add(ov, sb["lnt"][:, :], ps["E"][:, :])
                    else:
                        e.tensor_add(sb["lnt"][:, :], sb["lnt"][:, :], ps["E"][:, :])
                        last = e.tensor_scalar(ov, sb["lnt"][:, :],
                                               sb["lnw"][:, c:c + 1],
                                               sb["lnb"][:, c:c + 1], ALU.mult, ALU.add)
                return last
            prev_app = step("ve", _app, after=[t_bc])
        return prev_app

    def spill_x(dep):
        def _sp(e, st):
            st.dma(e.dma_start(out=g["xspill"].rearrange("(c p) t -> p c t", p=128),
                               in_=x_sb[:, :, :]))
            return None
        return step("sy", _sp, after=dep)

    def reload_x(dep):
        def _rl(e, st):
            st.dma(e.dma_start(out=x_sb[:, :, :],
                               in_=g["xspill"].rearrange("(c p) t -> p c t", p=128)))
            return None
        return step("sy", _rl, after=dep)

    def attn_a():
        pre_o = place("pre_o", (128, 8, T), BF16, 66 * K)
        q_pair = place("q_pair", (128, T), BF16, 99 * K)
        k_pair = place("k_pair", (128, T), BF16, 103 * K)
        v_pair = place("v_pair", (128, 16, 128), BF16, 107 * K)
        eq_pair = place("eq_pair", (128, T), BF16, 111 * K)
        ek_pair = place("ek_pair", (128, 16, 128), BF16, 115 * K)
        aq_p = place("aq_p", (128, NCH, 128), BF16, 119 * K)
        ak_p = place("ak_p", (128, NCH, 128), BF16, 121 * K)
        av_p = place("av_p", (128, NCH, 128), BF16, 123 * K)
        cent_b = place("cent_b", (128, 8, 64), BF16, 125 * K)
        csq = place("csq", (128, 8, 64), F32, 126 * K)
        lsamp = place("lsamp", (16, 128), F32, 128 * K)
        inv_sc = place("inv_sc", (16, 64), F32, int(128.5 * K))
        amp2_sc = place("amp2_sc", (16, 64), F32, int(128.75 * K))
        amp2T = place("amp2T", (128, 16), F32, 129 * K)
        vagg_b = place("vagg_b", (128, 64), BF16, int(129.25 * K))
        ks_col = place("ks_col", (128, 2), BF16, int(129.5 * K))
        invrow = place("invrow", (1, D), F32, 0)
        amp2row = place("amp2row", (1, D), F32, 4 * K)
        c2row = place("c2row", (1, D), F32, 8 * K)
        negc2i = place("negc2i", (1, D), BF16, 12 * K)
        neginvrow = place("neginvrow", (1, D), BF16, 14 * K)
        q2row = place("q2row", (1, 2, T), BF16, 16 * K)
        k2row = place("k2row", (1, 2, T), BF16, 24 * K)
        denA = place("denA", (1, T), F32, 32 * K)
        denB = place("denB", (1, T), F32, 40 * K)
        num_sb = place("num_sb", (128, JC), F32, 48 * K)
        ksrow = place("ksrow", (1, 128), F32, 50 * K)
        invb = place("invb", (128, 64), BF16, 51 * K)
        awo = place("awo", (128, NCH, D), BF16, 99 * K)

        t_ln = emit_ln(x_sb[:, :, :], sb["xb"][:, :, :], "plain", dep=[ld0, c0])
        t_spill = spill_x([t_ln])
        if sub < 1:
            reload_x([t_spill])
            return

        def _ldsc(e, st):
            st.dma(e.dma_start(out=lsamp[:, 0:64], in_=g["als"][:, :]))
            st.dma(e.dma_start(out=lsamp[:, 64:128], in_=g["aamp"][:, :]))
            st.dma(e.dma_start(out=cent_b[:, :, :], in_=g["acent"][:, :, :]))
            return None
        t_ld = step("gp", _ldsc, after=[t_spill])

        t_exp = step("ac", lambda e, st: e.activation(inv_sc[:, :], lsamp[:, 0:64],
                                                      AF.Exp), after=[t_ld])

        def _inv(e, st):
            e.tensor_scalar(inv_sc[:, :], inv_sc[:, :], 1e-3, None, ALU.max)
            e.tensor_mul(inv_sc[:, :], inv_sc[:, :], inv_sc[:, :])
            st.sync(e, e.reciprocal(inv_sc[:, :], inv_sc[:, :]))
            e.tensor_scalar(inv_sc[:, :], inv_sc[:, :], 0.5, None, ALU.mult)
            return e.tensor_mul(amp2_sc[:, :], lsamp[:, 64:128], lsamp[:, 64:128])
        t_inv = step("ve", _inv, after=[t_exp])
        if sub == 11:
            reload_x([t_inv])
            return

        def _rowize(e, st):
            st.dma(e.dma_start(out=invrow[0:1, :], in_=inv_sc[:, :]))
            st.dma(e.dma_start(out=amp2row[0:1, :], in_=amp2_sc[:, :]))
            return None
        t_rz = step("gp", _rowize, after=[t_inv])
        if sub == 12:
            reload_x([t_rz])
            return

        t_a2t = step("pe", lambda e, st: e.matmul(ps["B"][0:64, 0:16], amp2_sc[:, :],
                                                  sb["identf"][0:16, 0:16],
                                                  start=True, stop=True),
                     after=[t_inv])
        t_a2te = step("ve", lambda e, st: e.tensor_copy(amp2T[0:64, :],
                                                        ps["B"][0:64, 0:16]),
                      after=[t_a2t])
        t_csq = step("ac", lambda e, st: e.activation(csq[:, :, :], cent_b[:, :, :],
                                                      AF.Square), after=[t_ld])
        if sub == 13:
            reload_x([t_csq, t_a2te])
            return

        prev = None
        for p in range(8):
            def _c2(e, st, p=p):
                e.matmul(ps["C"][0:1, 0:64], sb["onescol_f"][0:64, 0:1],
                         csq[0:64, p, :], start=True, stop=True)
                return e.matmul(ps["D"][0:1, 0:64], sb["onescol_f"][64:128, 0:1],
                                csq[64:128, p, :], start=True, stop=True)
            t_c2 = step("pe", _c2, after=[t_csq] if prev is None else [prev])

            def _c2e(e, st, p=p):
                e.tensor_copy(c2row[0:1, p * 128:p * 128 + 64], ps["C"][0:1, 0:64])
                return e.tensor_copy(c2row[0:1, p * 128 + 64:(p + 1) * 128],
                                     ps["D"][0:1, 0:64])
            prev = step("ve", _c2e, after=[t_c2])
            if sub == 130 + p:
                reload_x([prev])
                return

        def _fin(e, st):
            e.tensor_mul(c2row[0:1, :], c2row[0:1, :], invrow[0:1, :])
            e.tensor_scalar(negc2i[0:1, :], c2row[0:1, :], -1.0, None, ALU.mult)
            return e.tensor_scalar(neginvrow[0:1, :], invrow[0:1, :], -1.0, None, ALU.mult)
        t_fin = step("ve", _fin, after=[prev, t_rz, t_a2te])
        if sub == 14:
            reload_x([t_fin])
            return

        prev = t_fin
        for p in range(8):
            def _ib(e, st, p=p):
                e.matmul(ps["D"][0:64, 0:64], sb["ones_rowf"][0:1, 0:64],
                         invrow[0:1, (2 * p) * 64:(2 * p) * 64 + 64], start=True, stop=True)
                return e.matmul(ps["D"][64:128, 0:64], sb["ones_rowf"][0:1, 0:64],
                                invrow[0:1, (2 * p + 1) * 64:(2 * p + 1) * 64 + 64],
                                start=True, stop=True, tile_position=(0, 64))
            t_ib = step("pe", _ib, after=[prev])

            def _ibe(e, st, p=p):
                e.tensor_copy(invb[:, :], ps["D"][:, 0:64])
                return e.scalar_tensor_tensor(cent_b[:, p, :], invb[:, :], 2.0,
                                              cent_b[:, p, :], ALU.mult, ALU.mult)
            prev = step("ve", _ibe, after=[t_ib])
        bar()
        if sub < 2:
            reload_x([])
            return

        for p in range(8):
            if (30 <= sub <= 40) or (130 <= sub <= 140) or (330 <= sub <= 340):
                _attn_a_pair(p, q_pair, k_pair, v_pair, eq_pair, ek_pair,
                             aq_p, ak_p, av_p, cent_b, invrow, amp2row, c2row,
                             negc2i, neginvrow, q2row, k2row, denA, denB,
                             num_sb, ksrow, vagg_b, ks_col, amp2T, pre_o, sub)
                bar()
                reload_x([])
                return
            if p >= max(1, sub - 2):
                step("ve", lambda e, st, p=p: e.memset(pre_o[:, p, :], 0.0))
                continue
            _attn_a_pair(p, q_pair, k_pair, v_pair, eq_pair, ek_pair,
                         aq_p, ak_p, av_p, cent_b, invrow, amp2row, c2row,
                         negc2i, neginvrow, q2row, k2row, denA, denB,
                         num_sb, ksrow, vagg_b, ks_col, amp2T, pre_o, 99)
            bar()
        bar()

        t_rx = reload_x([])
        t_wo = step("gp", lambda e, st: st.dma(e.dma_start(
            out=awo[:, :, :], in_=g["ao"].rearrange("(c p) m -> p c m", p=128))) and None)

        prev_add = None
        for jc in range(NJC):
            for dc in range(NCH):
                def _mm(e, st, jc=jc, dc=dc):
                    last = None
                    for pc in range(NCH):
                        last = e.matmul(ps["B"][:, :], awo[:, pc, dc * 128:(dc + 1) * 128],
                                        pre_o[:, pc, jc * JC:(jc + 1) * JC],
                                        start=(pc == 0), stop=(pc == NCH - 1))
                    return last
                depm = [t_wo] if prev_add is None else [t_wo, prev_add]
                t_mm = step("pe", _mm, after=depm)

                def _add(e, st, jc=jc, dc=dc):
                    xv = x_sb[:, dc, jc * JC:(jc + 1) * JC]
                    return e.tensor_add(xv, ps["B"][:, :], xv)
                prev_add = step("ve", _add, after=[t_mm, t_rx])
        bar()

    def _attn_a_pair(p, q_pair, k_pair, v_pair, eq_pair, ek_pair,
                     aq_p, ak_p, av_p, cent_b, invrow, amp2row, c2row,
                     negc2i, neginvrow, q2row, k2row, denA, denB,
                     num_sb, ksrow, vagg_b, ks_col, amp2T, pre_o, sub=99):
        hA, hB = 2 * p, 2 * p + 1

        def _ldw(e, st, p=p):
            for dst, wsrc in ((aq_p, "aq"), (ak_p, "ak"), (av_p, "av")):
                st.dma(e.dma_start(
                    out=dst[:, :, :],
                    in_=g[wsrc][:, p * 128:(p + 1) * 128]
                        .rearrange("(c q) m -> q c m", q=128)))
            return None
        t_ldw = step("gp", _ldw)

        def _qmm(e, st):
            last = None
            for jc in range(NJC):
                for c in range(NCH):
                    last = e.matmul(psA4[jc], aq_p[:, c, :],
                                    sb["xb"][:, c, jc * JC:(jc + 1) * JC],
                                    start=(c == 0), stop=(c == NCH - 1))
            return last
        t_qmm = step("pe", _qmm, after=[t_ldw])
        t_qev = step("ve", lambda e, st: e.tensor_copy(q_pair[:, :], ps["A"][:, :]),
                     after=[t_qmm])
        if sub == 31:
            return

        def _kmm(e, st):
            last = None
            for jc in range(NJC):
                for c in range(NCH):
                    last = e.matmul(psA4[jc], ak_p[:, c, :],
                                    sb["xb"][:, c, jc * JC:(jc + 1) * JC],
                                    start=(c == 0), stop=(c == NCH - 1))
            return last
        t_kmm = step("pe", _kmm, after=[t_qev])
        t_kev = step("ve", lambda e, st: e.tensor_copy(k_pair[:, :], ps["A"][:, :]),
                     after=[t_kmm])

        vprev = None
        for tc2 in range(8):
            def _vmm(e, st, tc2=tc2):
                last = None
                for i, tt in enumerate((2 * tc2, 2 * tc2 + 1)):
                    bank = ps["B"] if i == 0 else ps["C"]
                    for c in range(NCH):
                        last = e.matmul(bank[:, 0:128],
                                        sb["xb"][:, c, tt * 128:(tt + 1) * 128],
                                        av_p[:, c, :], start=(c == 0), stop=(c == NCH - 1))
                return last
            t_vmm = step("pe", _vmm, after=[t_ldw] if vprev is None else [vprev])

            def _vev(e, st, tc2=tc2):
                e.tensor_copy(v_pair[:, 2 * tc2, :], ps["B"][:, 0:128])
                return e.tensor_copy(v_pair[:, 2 * tc2 + 1, :], ps["C"][:, 0:128])
            vprev = step("ve", _vev, after=[t_vmm])
        t_vdone = vprev
        if sub == 32:
            return

        t_sqq = step("ac", lambda e, st: e.activation(eq_pair[:, :], q_pair[:, :],
                                                      AF.Square), after=[t_qev])
        prev = t_sqq
        for jc in range(NJC):
            def _q2jc(e, st, jc=jc):
                e.matmul(ps["D"][0:1, 0:JC], sb["ones_col"][0:64, 0:1],
                         eq_pair[0:64, jc * JC:(jc + 1) * JC], start=True, stop=True)
                return e.matmul(ps["E"][0:1, 0:JC], sb["ones_col"][64:128, 0:1],
                                eq_pair[64:128, jc * JC:(jc + 1) * JC],
                                start=True, stop=True)
            t_q2 = step("pe", _q2jc, after=[prev])

            def _q2ev(e, st, jc=jc):
                e.tensor_copy(q2row[0:1, 0, jc * JC:(jc + 1) * JC], ps["D"][0:1, 0:JC])
                return e.tensor_copy(q2row[0:1, 1, jc * JC:(jc + 1) * JC],
                                     ps["E"][0:1, 0:JC])
            prev = step("ve", _q2ev, after=[t_q2])
        t_q2done = prev
        if sub == 331:
            return

        t_sqk = step("ac", lambda e, st: e.activation(
            eq_pair[:, :], k_pair[:, :], AF.Square), after=[t_kev, t_q2done])
        prev = t_sqk
        for jc in range(NJC):
            def _k2jc(e, st, jc=jc):
                e.matmul(ps["D"][0:1, 0:JC], sb["ones_col"][0:64, 0:1],
                         eq_pair[0:64, jc * JC:(jc + 1) * JC], start=True, stop=True)
                return e.matmul(ps["E"][0:1, 0:JC], sb["ones_col"][64:128, 0:1],
                                eq_pair[64:128, jc * JC:(jc + 1) * JC],
                                start=True, stop=True)
            t_k2 = step("pe", _k2jc, after=[prev])

            def _k2ev(e, st, jc=jc):
                e.tensor_copy(k2row[0:1, 0, jc * JC:(jc + 1) * JC], ps["D"][0:1, 0:JC])
                return e.tensor_copy(k2row[0:1, 1, jc * JC:(jc + 1) * JC],
                                     ps["E"][0:1, 0:JC])
            prev = step("ve", _k2ev, after=[t_k2])
        t_k2done = prev
        if sub == 33:
            return

        eqprev = t_k2done
        for jc in range(NJC):
            def _eq(e, st, jc=jc, p=p, hA=hA, hB=hB):
                sl = slice(jc * JC, (jc + 1) * JC)
                e.matmul(ps["B"][0:64, :], cent_b[0:64, p, :], q_pair[0:64, sl],
                         start=True, stop=False)
                e.matmul(ps["B"][0:64, :], neginvrow[0:1, hA * 64:hA * 64 + 64],
                         q2row[0:1, 0, sl], start=False, stop=False)
                e.matmul(ps["B"][0:64, :], negc2i[0:1, hA * 64:hA * 64 + 64],
                         sb["ones512"][0:1, :], start=False, stop=True)
                e.matmul(ps["B"][64:128, :], cent_b[64:128, p, :], q_pair[64:128, sl],
                         start=True, stop=False, tile_position=(64, 64))
                e.matmul(ps["B"][64:128, :], neginvrow[0:1, hB * 64:hB * 64 + 64],
                         q2row[0:1, 1, sl], start=False, stop=False,
                         tile_position=(0, 64))
                return e.matmul(ps["B"][64:128, :], negc2i[0:1, hB * 64:hB * 64 + 64],
                                sb["ones512"][0:1, :], start=False, stop=True,
                                tile_position=(0, 64))
            t_eq = step("pe", _eq, after=[eqprev])
            eqprev = step("ac", lambda e, st, jc=jc: e.activation(
                eq_pair[:, jc * JC:(jc + 1) * JC], ps["B"][:, :], AF.Exp), after=[t_eq])
        t_eqmin = step("ve", lambda e, st: e.tensor_scalar(eq_pair[:, :], eq_pair[:, :],
                                                           1.0, None, ALU.min),
                       after=[eqprev])
        if sub == 34:
            return

        ekprev = t_k2done
        for tc in range(16):
            def _ek(e, st, tc=tc, p=p, hA=hA, hB=hB):
                tl = slice(tc * 128, (tc + 1) * 128)
                e.matmul(ps["C"][:, 0:64], k_pair[0:64, tl], cent_b[0:64, p, :],
                         start=True, stop=False)
                e.matmul(ps["C"][:, 0:64], k2row[0:1, 0, tl],
                         neginvrow[0:1, hA * 64:hA * 64 + 64], start=False, stop=False)
                e.matmul(ps["C"][:, 0:64], sb["ones_row"][0:1, :],
                         negc2i[0:1, hA * 64:hA * 64 + 64], start=False, stop=True)
                e.matmul(ps["D"][:, 0:64], k_pair[64:128, tl], cent_b[64:128, p, :],
                         start=True, stop=False, tile_position=(64, 0))
                e.matmul(ps["D"][:, 0:64], k2row[0:1, 1, tl],
                         neginvrow[0:1, hB * 64:hB * 64 + 64], start=False, stop=False)
                return e.matmul(ps["D"][:, 0:64], sb["ones_row"][0:1, :],
                                negc2i[0:1, hB * 64:hB * 64 + 64], start=False, stop=True)
            t_ek = step("pe", _ek, after=[ekprev, t_eqmin] if tc == 0 else [ekprev])

            def _eke(e, st, tc=tc):
                e.activation(ek_pair[:, tc, 0:64], ps["C"][:, 0:64], AF.Exp)
                return e.activation(ek_pair[:, tc, 64:128], ps["D"][:, 0:64], AF.Exp)
            ekprev = step("ac", _eke, after=[t_ek])
        t_ekmin = step("ve", lambda e, st: e.tensor_scalar(ek_pair[:, :, :],
                                                           ek_pair[:, :, :], 1.0,
                                                           None, ALU.min),
                       after=[ekprev])
        if sub == 35:
            return

        def _vagg(e, st):
            for tc in range(16):
                e.matmul(ps["B"][0:64, 0:64], ek_pair[:, tc, 0:64], v_pair[:, tc, 0:64],
                         start=(tc == 0), stop=(tc == 15))
            for tc in range(16):
                e.matmul(ps["B"][64:128, 0:64], ek_pair[:, tc, 64:128],
                         v_pair[:, tc, 64:128], start=(tc == 0), stop=(tc == 15),
                         tile_position=(0, 64))
            last = None
            for tc in range(16):
                last = e.matmul(ps["E"][0:1, 0:128], sb["ones_col"][:, :],
                                ek_pair[:, tc, :], start=(tc == 0), stop=(tc == 15))
            return last
        t_vagg = step("pe", _vagg, after=[t_ekmin, t_vdone])

        def _vev2(e, st, hA=hA, hB=hB):
            e.tensor_scalar(vagg_b[0:64, :], ps["B"][0:64, 0:64],
                            amp2T[0:64, hA:hA + 1], None, ALU.mult)
            e.tensor_scalar(vagg_b[64:128, :], ps["B"][64:128, 0:64],
                            amp2T[0:64, hB:hB + 1], None, ALU.mult)
            e.tensor_copy(ksrow[0:1, :], ps["E"][0:1, 0:128])
            return e.tensor_mul(ksrow[0:1, :], ksrow[0:1, :],
                                amp2row[0:1, hA * 64:hA * 64 + 128])
        t_vev2 = step("ve", _vev2, after=[t_vagg])

        t_kst = step("pe", lambda e, st: e.matmul(ps["C"][0:128, 0:1], ksrow[0:1, :],
                                                  sb["onescol_f"][0:1, 0:1],
                                                  start=True, stop=True),
                     after=[t_vev2])

        def _kscol(e, st):
            e.memset(ks_col[:, :], 0.0)
            e.tensor_copy(ks_col[0:64, 0:1], ps["C"][0:64, 0:1])
            return e.tensor_copy(ks_col[64:128, 1:2], ps["C"][64:128, 0:1])
        t_kscol = step("ve", _kscol, after=[t_kst])
        if sub == 36:
            return

        prev = t_kscol
        for jc in range(NJC):
            def _den(e, st, jc=jc):
                sl = slice(jc * JC, (jc + 1) * JC)
                e.matmul(ps["E"][0:1, 0:JC], ks_col[:, 0:1], eq_pair[:, sl],
                         start=True, stop=True)
                return e.matmul(ps["D"][0:1, 0:JC], ks_col[:, 1:2], eq_pair[:, sl],
                                start=True, stop=True)
            t_den = step("pe", _den, after=[prev])

            def _denev(e, st, jc=jc):
                sl = slice(jc * JC, (jc + 1) * JC)
                e.tensor_copy(denA[0:1, sl], ps["E"][0:1, 0:JC])
                return e.tensor_copy(denB[0:1, sl], ps["D"][0:1, 0:JC])
            prev = step("ve", _denev, after=[t_den])

        def _recip(e, st):
            e.tensor_scalar(denA[0:1, :], denA[0:1, :], 1e-6, None, ALU.add)
            e.tensor_scalar(denB[0:1, :], denB[0:1, :], 1e-6, None, ALU.add)
            e.reciprocal(denA[0:1, :], denA[0:1, :])
            return e.reciprocal(denB[0:1, :], denB[0:1, :])
        t_rec = step("ve", _recip, after=[prev])
        if sub == 37:
            return

        prev = t_rec
        for jc in range(NJC):
            def _num2(e, st, jc=jc):
                sl = slice(jc * JC, (jc + 1) * JC)
                e.matmul(ps["B"][0:64, :], vagg_b[0:64, :], eq_pair[0:64, sl],
                         start=True, stop=True)
                e.matmul(ps["B"][64:128, :], vagg_b[64:128, :], eq_pair[64:128, sl],
                         start=True, stop=True, tile_position=(64, 64))
                e.matmul(ps["D"][0:64, 0:JC], sb["ones_rowf"][0:1, 0:64],
                         denA[0:1, sl], start=True, stop=True)
                return e.matmul(ps["D"][64:128, 0:JC], sb["ones_rowf"][0:1, 0:64],
                                denB[0:1, sl], start=True, stop=True,
                                tile_position=(0, 64))
            t_n2 = step("pe", _num2, after=[prev])

            def _preo(e, st, jc=jc, p=p):
                e.tensor_copy(num_sb[:, :], ps["B"][:, :])
                return e.tensor_mul(pre_o[:, p, jc * JC:(jc + 1) * JC],
                                    num_sb[:, :], ps["D"][:, :])
            prev = step("ve", _preo, after=[t_n2])

    def ffn(call, sub=99):
        h = place("h", (128, NCH, T), F32, 66 * K)
        q_b = place("q_b", (128, NCH, T), BF16, 0)
        ctx_acc = place("ctx_acc", (128, NCH, JC), F32, 32 * K)
        sc8 = place("sc8", (16, 8, JC), BF16, 48 * K)
        kv_t = place("kv_t", (128, 4, JC), BF16, 56 * K)
        fpr_dc = place("fpr_dc", (128, NCH, 128), BF16, 56 * K)
        mx16 = place("mx16", (16, JC), F32, 61 * K)
        s16t = place("s16t", (16, JC), F32, 63 * K)
        prod4 = place("prod4", (128, 4, JC), BF16, int(61 * K))
        glog = place("glog", (128, 16, 8), F32, 65 * K)
        gsc1 = place("gsc1", (128, 16, 8), F32, int(65.5 * K))
        gsc2 = place("gsc2", (128, 16, 8), F32, 60 * K)
        grow = place("grow", (128, 16), F32, int(60.5 * K))
        w1h = place("w1h", (128, NCH, K), BF16, 0)
        w2h = place("w2h", (128, NCH, K), BF16, 16 * K)
        h1g = place("h1g", (128, NCH, JC), BF16, 32 * K)
        bc_sb = place("bc_sb", (128, JC), F32, 40 * K)
        tmp2 = place("tmp2", (128, JC), F32, 42 * K)
        wT = place("wT", (8, T), F32, 44 * K)
        d8_sb = place("d8_sb", (8, 8, 128), F32, 52 * K)
        rt_sb = place("rt_sb", (128, NCH, 8), BF16, 56 * K)

        t_h = emit_ln(x_sb[:, :, :], h[:, :, :], "plain", dep=[])
        t_spill = spill_x([t_h])
        if sub == 201:
            _gate_store[0] = h
            return None

        def _ldln1(e, st):
            st.dma(e.dma_start(out=sb["lnw"][:, :], in_=g["fln1w"][:, :]))
            st.dma(e.dma_start(out=sb["lnb"][:, :], in_=g["fln1b"][:, :]))
            return None
        t_l1 = step("gp", _ldln1)
        t_ln1 = emit_ln(h[:, :, :], sb["xb"][:, :, :], "affine", dep=[t_l1])
        if sub == 202:
            _gate_store[0] = h
            return None

        qkv_w = place("qkv_w", (128, NCH, 128), BF16, 56 * K)
        agv = g["agin"].ap().rearrange("(c p) t -> p c t", p=128)
        kvev = place("kvev", (128, JC), BF16, 60 * K)
        prev = t_ln1
        ag_deps = []
        for oc in range(12):
            def _lw(e, st, oc=oc):
                st.dma(e.dma_start(
                    out=qkv_w[:, :, :],
                    in_=g["fqkv"][:, oc * 128:(oc + 1) * 128]
                        .rearrange("(c q) m -> q c m", q=128)))
                return None
            t_lw = step("gp", _lw, after=[prev])

            def _mm(e, st, oc=oc):
                last = None
                for jc in range(NJC):
                    for c in range(NCH):
                        last = e.matmul(psA4[jc], qkv_w[:, c, :],
                                        sb["xb"][:, c, jc * JC:(jc + 1) * JC],
                                        start=(c == 0), stop=(c == NCH - 1))
                return last
            t_mm = step("pe", _mm, after=[t_lw])

            if oc < 8:
                prev = step("ve", lambda e, st, oc=oc: e.tensor_copy(
                    q_b[:, oc, :], ps["A"][:, :]), after=[t_mm])
            else:
                for jc in range(NJC):
                    t_ev = step("ve", lambda e, st, oc=oc, jc=jc: e.tensor_copy(
                        kvev[:, :], psA4[jc]), after=[t_mm] if jc == 0 else [prev])

                    def _ship(e, st, oc=oc, jc=jc):
                        st.dma(e.dma_start(out=agv[:, oc - 8, jc * JC:(jc + 1) * JC],
                                           in_=kvev[:, :]))
                        return None
                    t_sh = step("sy", _ship, after=[t_ev])
                    prev = t_sh
                ag_deps.append(prev)

        if sub == 203:
            _gate_store[0] = h
            return None

        def _cc(e, st):
            return e.collective_compute(
                "AllGather", ALU.bypass, replica_groups=[list(range(NCORES))],
                ins=[g["agin"].ap().opt()], outs=[g["agout"].ap().opt()])
        t_cc = step("gp", _cc, after=[prev])
        if sub == 204:
            _gate_store[0] = h
            return None

        agov = g["agout"].ap().rearrange("(t c p) u -> t p c u", p=128, c=4)
        prev_jc = t_cc
        for jc in range(NJC):
            sl = slice(jc * JC, (jc + 1) * JC)
            sc_deps = []
            prev_t = prev_jc
            for t in range(8):
                def _lkv(e, st, t=t, sl=sl):
                    st.dma(e.dma_start(out=kv_t[:, :, :], in_=agov[t, :, :, sl]))
                    return None
                t_kv = step("sy", _lkv, after=[prev_t])

                half_toks = []
                for half in range(2):
                    def _prod(e, st, half=half, sl=sl):
                        last = None
                        for i in range(4):
                            c = half * 4 + i
                            last = e.tensor_mul(prod4[:, i, :], q_b[:, c, sl],
                                                kv_t[:, 0 if c < 4 else 1, :])
                        return last
                    dp = [t_kv] if half == 0 else [half_toks[-1]]
                    t_pr = step("ve", _prod, after=dp)

                    def _smm(e, st, half=half, t=t):
                        last = None
                        for i in range(4):
                            c = half * 4 + i
                            last = e.matmul(ps["B"][0:16, :], sb["mask16b"][:, c, :],
                                            prod4[:, i, :],
                                            start=(c == 0), stop=(c == 7))
                        return last
                    t_sm = step("pe", _smm, after=[t_pr])
                    half_toks.append(t_sm)

                t_sev = step("ve", lambda e, st, t=t: e.tensor_copy(
                    sc8[:, t, :], ps["B"][0:16, :]), after=[half_toks[-1]])
                sc_deps.append(t_sev)
                prev_t = t_sev

            def _mx(e, st):
                e.tensor_max(mx16[:, :], sc8[:, 0, :], sc8[:, 1, :])
                for t in range(2, 8):
                    e.tensor_max(mx16[:, :], mx16[:, :], sc8[:, t, :])
                last = None
                for t in range(8):
                    last = e.tensor_sub(sc8[:, t, :], sc8[:, t, :], mx16[:, :])
                return last
            t_mx = step("ve", _mx, after=[sc_deps[-1]])

            t_ex = step("ac", lambda e, st: e.activation(sc8[:, :, :], sc8[:, :, :],
                                                         AF.Exp), after=[t_mx])

            def _norm(e, st):
                for t in range(8):
                    e.tensor_scalar(sc8[:, t, :], sc8[:, t, :],
                                    sb["tmask"][:, t:t + 1], None, ALU.mult)
                e.tensor_add(s16t[:, :], sc8[:, 0, :], sc8[:, 1, :])
                for t in range(2, 8):
                    e.tensor_add(s16t[:, :], s16t[:, :], sc8[:, t, :])
                st.sync(e, e.reciprocal(s16t[:, :], s16t[:, :]))
                last = None
                for t in range(8):
                    last = e.tensor_mul(sc8[:, t, :], sc8[:, t, :], s16t[:, :])
                return last
            t_nm = step("ve", _norm, after=[t_ex])

            t_z = step("ve", lambda e, st: e.memset(ctx_acc[:, :, :], 0.0), after=[t_nm])
            prev_c = t_z
            for t in range(8):
                def _lkv2(e, st, t=t, sl=sl):
                    st.dma(e.dma_start(out=kv_t[:, :, :], in_=agov[t, :, :, sl]))
                    return None
                t_kv2 = step("sy", _lkv2, after=[prev_c])
                for half in range(2):
                    def _bmm(e, st, half=half, t=t):
                        last = None
                        for i in range(4):
                            c = half * 4 + i
                            last = e.matmul(psA4[i], sb["sel16b"][:, c, :],
                                            sc8[:, t, :], start=True, stop=True)
                        return last
                    dp = [t_kv2, prev_c] if half == 0 else [prev_c]
                    t_bm = step("pe", _bmm, after=dp)

                    def _acc(e, st, half=half, t=t):
                        last = None
                        for i in range(4):
                            c = half * 4 + i
                            e.tensor_mul(sb["lnt"][:, :], psA4[i],
                                         kv_t[:, 2 + (0 if c < 4 else 1), :])
                            last = e.tensor_add(ctx_acc[:, c, :], ctx_acc[:, c, :],
                                                sb["lnt"][:, :])
                        return last
                    prev_c = step("ve", _acc, after=[t_bm])

            prev_jc = _ffn_proj(jc, prev_c, h, ctx_acc, fpr_dc)
            if sub == 205:
                _gate_store[0] = h
                return None
        if sub == 206:
            _gate_store[0] = h
            return None
        return _ffn_tail(call, h, q_b, glog, gsc1, gsc2, grow, wT, d8_sb, rt_sb,
                         w1h, w2h, h1g, bc_sb, tmp2, prev_jc, sub)

    def _ffn_proj(jc, dep, h, ctx_acc, fpr_dc):
        ctx_b8 = place("ctx_b8", (128, NCH, JC), BF16, 48 * K)

        def _cast(e, st):
            last = None
            for pc in range(NCH):
                last = e.tensor_copy(ctx_b8[:, pc, :], ctx_acc[:, pc, :])
            return last
        t_cast = step("ve", _cast, after=[dep])
        prev = t_cast
        for dc in range(NCH):
            def _lf(e, st, dc=dc):
                st.dma(e.dma_start(
                    out=fpr_dc[:, :, :],
                    in_=g["fproj"][:, dc * 128:(dc + 1) * 128]
                        .rearrange("(c q) m -> q c m", q=128)))
                return None
            t_lf = step("gp", _lf, after=[prev])

            def _pj(e, st, dc=dc):
                last = None
                for pc in range(NCH):
                    last = e.matmul(ps["B"][:, :], fpr_dc[:, pc, :], ctx_b8[:, pc, :],
                                    start=(pc == 0), stop=(pc == NCH - 1))
                return last
            t_pj = step("pe", _pj, after=[t_lf])

            def _hadd(e, st, dc=dc, jc=jc):
                hv = h[:, dc, jc * JC:(jc + 1) * JC]
                return e.tensor_add(hv, ps["B"][:, :], hv)
            prev = step("ve", _hadd, after=[t_pj])
        return prev

    def _ffn_tail(call, h, q_b, glog, gsc1, gsc2, grow, wT, d8_sb, rt_sb,
                  w1h, w2h, h1g, bc_sb, tmp2, dep, sub=99):
        def _ldln2(e, st):
            st.dma(e.dma_start(out=sb["lnw"][:, :], in_=g["fln2w"][:, :]))
            st.dma(e.dma_start(out=sb["lnb"][:, :], in_=g["fln2b"][:, :]))
            st.dma(e.dma_start(out=rt_sb[:, :, :],
                               in_=g["frout"].rearrange("(c p) m -> p c m", p=128)))
            st.dma(e.dma_start(out=d8_sb[:, :, :], in_=g["delta8"][:, :, :]))
            return None
        t_l2 = step("gp", _ldln2, after=[dep])
        t_ln2 = emit_ln(h[:, :, :], sb["xb"][:, :, :], "affine", dep=[t_l2])
        if sub == 207:
            _gate_store[0] = h
            return None
        if sub == 211:
            _gate_store[0] = ("bf16", sb["xb"])
            return None

        prev = t_ln2
        for js in range(16):
            def _rmm(e, st, js=js):
                last = None
                for c in range(NCH):
                    last = e.matmul(ps["B"][0:128, 0:8],
                                    sb["xb"][:, c, js * 128:(js + 1) * 128],
                                    rt_sb[:, c, :], start=(c == 0), stop=(c == NCH - 1))
                return last
            t_rm = step("pe", _rmm, after=[prev])
            prev = step("ve", lambda e, st, js=js: e.tensor_copy(
                glog[:, js, :], ps["B"][0:128, 0:8]), after=[t_rm])

        if sub == 213:
            _gate_store[0] = ("p128", glog)
            return None

        def _gate1(e, st):
            st.sync(e, e.reduce_max(grow[:, :], glog[:, :, :], axis=AX))
            g3 = grow[:, :].rearrange("p (a b) -> p a b", b=1).broadcast_to([128, 16, 8])
            return e.tensor_sub(glog[:, :, :], glog[:, :, :], g3)
        t_g1 = step("ve", _gate1, after=[prev])
        t_g2 = step("ac", lambda e, st: e.activation(glog[:, :, :], glog[:, :, :], AF.Exp),
                    after=[t_g1])

        def _gate3(e, st):
            st.sync(e, e.reduce_sum(grow[:, :], glog[:, :, :], axis=AX))
            st.sync(e, e.reciprocal(grow[:, :], grow[:, :]))
            g3 = grow[:, :].rearrange("p (a b) -> p a b", b=1).broadcast_to([128, 16, 8])
            e.tensor_mul(glog[:, :, :], glog[:, :, :], g3)
            st.sync(e, e.reduce_max(grow[:, :], glog[:, :, :], axis=AX))
            m3 = grow[:, :].rearrange("p (a b) -> p a b", b=1).broadcast_to([128, 16, 8])
            e.tensor_tensor(gsc1[:, :, :], glog[:, :, :], m3, ALU.is_equal)
            e.scalar_tensor_tensor(gsc2[:, :, :], gsc1[:, :, :], -2.0, glog[:, :, :],
                                   ALU.mult, ALU.add)
            st.sync(e, e.reduce_max(grow[:, :], gsc2[:, :, :], axis=AX))
            m3 = grow[:, :].rearrange("p (a b) -> p a b", b=1).broadcast_to([128, 16, 8])
            e.tensor_tensor(gsc1[:, :, :], glog[:, :, :], m3, ALU.is_ge)
            return e.tensor_mul(glog[:, :, :], glog[:, :, :], gsc1[:, :, :])
        t_g3 = step("ve", _gate3, after=[t_g2])
        if sub == 214:
            _gate_store[0] = ("p128", glog)
            return None
        if sub == 208:
            _gate_store[0] = h
            return None

        prev = t_g3
        for js in range(16):
            t_tr = step("pe", lambda e, st, js=js: e.transpose(
                ps["B"][0:8, 0:128], glog[:, js, :], sb["identf"][:, :]), after=[prev])
            prev = step("ve", lambda e, st, js=js: e.tensor_copy(
                wT[0:8, js * 128:(js + 1) * 128], ps["B"][0:8, 0:128]), after=[t_tr])
        t_wT = prev
        if sub == 209:
            _gate_store[0] = h
            return None
        if sub == 212:
            _gate_store[0] = ("wt", wT)
            return None

        fw1v = g["fw1"].rearrange("e (c p) f -> e p c f", p=128)
        fw2v = g["fw2"].rearrange("e (c p) f -> e p c f", p=128)
        prev_acc = t_wT
        for ei in range(8):
            for fh in range(2):
                def _lw(e, st, ei=ei, fh=fh):
                    st.dma(e.dma_start(out=w1h[:, :, :],
                                       in_=fw1v[ei, :, :, fh * K:(fh + 1) * K]))
                    st.dma(e.dma_start(out=w2h[:, :, :],
                                       in_=fw2v[ei, :, fh * NCH:(fh + 1) * NCH, :]))
                    return None
                t_lwm = step("gp", _lw, after=[prev_acc])

                for jc in range(NJC):
                    sl = slice(jc * JC, (jc + 1) * JC)

                    def _h1a(e, st, sl=sl):
                        last = None
                        for fc in range(4):
                            for c in range(NCH):
                                last = e.matmul(psA4[fc], w1h[:, c, fc * 128:(fc + 1) * 128],
                                                sb["xb"][:, c, sl],
                                                start=(c == 0), stop=(c == NCH - 1))
                        return last
                    t_h1a = step("pe", _h1a, after=[t_lwm] if jc == 0 else [t_lwm, prev_acc])

                    def _gel1(e, st, jc=jc):
                        last = None
                        for fc in range(4):
                            last = e.activation(h1g[:, fc, :], psA4[fc], AF.Gelu_apprx_tanh)
                        return last
                    t_g1m = step("ac", _gel1, after=[t_h1a])

                    def _h1b(e, st, sl=sl):
                        last = None
                        for fc in range(4):
                            for c in range(NCH):
                                last = e.matmul(psA4[fc],
                                                w1h[:, c, (4 + fc) * 128:(5 + fc) * 128],
                                                sb["xb"][:, c, sl],
                                                start=(c == 0), stop=(c == NCH - 1))
                        return last
                    t_h1b = step("pe", _h1b, after=[t_g1m])

                    def _gel2(e, st, jc=jc):
                        last = None
                        for fc in range(4):
                            last = e.activation(h1g[:, 4 + fc, :], psA4[fc],
                                                AF.Gelu_apprx_tanh)
                        return last
                    t_g2m = step("ac", _gel2, after=[t_h1b])

                    def _bcm(e, st, ei=ei, sl=sl):
                        return e.matmul(ps["E"][:, :], d8_sb[:, ei, :], wT[0:8, sl],
                                        start=True, stop=True)
                    t_bcm = step("pe", _bcm, after=[t_g2m])
                    t_bce = step("ve", lambda e, st: e.tensor_copy(bc_sb[:, :], ps["E"][:, :]),
                                 after=[t_bcm])

                    prev_d = t_bce
                    for dc2 in range(4):
                        def _eo(e, st, dc2=dc2):
                            for c in range(NCH):
                                e.matmul(ps["B"][:, :], w2h[:, c, (2 * dc2) * 128:(2 * dc2 + 1) * 128],
                                         h1g[:, c, :], start=(c == 0), stop=(c == NCH - 1))
                            last = None
                            for c in range(NCH):
                                last = e.matmul(ps["C"][:, :],
                                                w2h[:, c, (2 * dc2 + 1) * 128:(2 * dc2 + 2) * 128],
                                                h1g[:, c, :], start=(c == 0), stop=(c == NCH - 1))
                            return last
                        t_eo = step("pe", _eo, after=[prev_d])

                        def _accm(e, st, dc2=dc2, jc=jc):
                            e.tensor_mul(tmp2[:, :], ps["B"][:, :], bc_sb[:, :])
                            hv = h[:, 2 * dc2, jc * JC:(jc + 1) * JC]
                            e.tensor_add(hv, hv, tmp2[:, :])
                            e.tensor_mul(tmp2[:, :], ps["C"][:, :], bc_sb[:, :])
                            hv = h[:, 2 * dc2 + 1, jc * JC:(jc + 1) * JC]
                            return e.tensor_add(hv, hv, tmp2[:, :])
                        prev_d = step("ve", _accm, after=[t_eo])
                    prev_acc = prev_d
            if sub == 210:
                _gate_store[0] = h
                break

        t_rx = reload_x([prev_acc])
        prev = prev_acc
        for dc in range(NCH):
            def _xadd(e, st, dc=dc):
                last = None
                for jc in range(NJC):
                    xv = x_sb[:, dc, jc * JC:(jc + 1) * JC]
                    last = e.tensor_add(xv, xv, h[:, dc, jc * JC:(jc + 1) * JC])
                return last
            prev = step("ve", _xadd, after=[t_rx, prev])
        return prev

    def attn_b():
        tmp_h = place("tmph", (128, NCH, T), F32, 66 * K)
        pe_bf = place("pe_bf", (128, NCH, PPAD), BF16, 0)
        pbuf = place("pbuf", (128, PPAD), BF16, 72 * K)
        p_pair = place("p_pair", (128, PTRIM), BF16, 0)
        q_u = place("q_u", (128, T), BF16, 9 * K)
        q_v = place("q_v", (128, T), BF16, 13 * K)
        kb_pair = place("kb_pair", (128, T), BF16, 17 * K)
        vb_pair = place("vb_pair", (128, 16, 128), BF16, 21 * K)
        S_A = place("S_A", (128, T), F32, 25 * K)
        S_B = place("S_B", (128, T), F32, 33 * K)
        P_A = place("P_A", (128, T), BF16, 41 * K)
        P_B = place("P_B", (128, T), BF16, 45 * K)
        PT_A = place("PT_A", (128, 2, T), BF16, 49 * K)
        PT_B = place("PT_B", (128, 2, T), BF16, 57 * K)
        posbuf = place("posbuf", (128, 2, JC), BF16, 65 * K)
        wq_sl = place("wq_sl", (128, NCH, 128), BF16, 67 * K)
        wk_sl = place("wk_sl", (128, NCH, 128), BF16, 69 * K)
        wv_sl = place("wv_sl", (128, NCH, 128), BF16, 71 * K)
        bvrow = place("bvrow", (1, D), F32, 73 * K)
        ctx_all = place("ctx_all", (128, NCH, T), BF16, 77 * K)
        bwo_sb = place("bwo_sb", (128, NCH, K), BF16, 110 * K)

        t_ln = emit_ln(x_sb[:, :, :], tmp_h[:, :, :], "plain", dep=[])

        def _ldg(e, st):
            st.dma(e.dma_start(out=sb["lnw"][:, :], in_=g["bg"][:, :]))
            st.dma(e.dma_start(out=sb["lnb"][:, :], in_=g["bb"][:, :]))
            st.dma(e.dma_start(out=sb["bias1"][:, :], in_=g["bbq"][:, :]))
            st.dma(e.dma_start(out=sb["bias2"][:, :], in_=g["bbk"][:, :]))
            return None
        t_lg = step("gp", _ldg, after=[t_ln])
        t_ln2 = emit_ln(tmp_h[:, :, :], sb["xb"][:, :, :], "affine", dep=[t_lg])
        t_spill = spill_x([t_ln2])

        ubias = place("ubias", (128, 8), F32, 109 * K)
        vbias = place("vbias", (128, 8), F32, int(109.25 * K))
        kbias = place("kbias", (128, 8), F32, int(109.5 * K))

        def _ldpe(e, st):
            st.dma(e.dma_start(out=pe_bf[:, :, :],
                               in_=g["pet"].rearrange("(c p) t -> p c t", p=128)))
            return None
        t_lpe = step("gp", _ldpe, after=[t_spill])

        wp_sl = place("wp_sl", (128, NCH, 128), BF16, 82 * K)
        prev = t_lpe
        for p in range(8):
            def _lbwp(e, st, p=p):
                st.dma(e.dma_start(
                    out=wp_sl[:, :, :],
                    in_=g["bwp"][:, p * 128:(p + 1) * 128].rearrange("(c q) m -> q c m", q=128)))
                return None
            t_lw = step("gp", _lbwp, after=[prev])
            evs = []
            for wch in range(9):
                def _pmm(e, st, wch=wch):
                    bank = ps["B"] if wch % 2 == 0 else ps["C"]
                    last = None
                    for c in range(NCH):
                        last = e.matmul(bank[:, :], wp_sl[:, c, :],
                                        pe_bf[:, c, wch * JC:(wch + 1) * JC],
                                        start=(c == 0), stop=(c == NCH - 1))
                    return last
                dpp = [t_lw] if wch < 2 else [t_lw, evs[wch - 2]]
                t_pm = step("pe", _pmm, after=dpp)
                t_pe = step("ac", lambda e, st, wch=wch: e.activation(
                    pbuf[:, wch * JC:(wch + 1) * JC],
                    (ps["B"] if wch % 2 == 0 else ps["C"])[:, :], AF.Identity),
                    after=[t_pm])
                evs.append(t_pe)

            def _shipp(e, st, p=p):
                st.dma(e.dma_start(out=g["pd"][p, :, :], in_=pbuf[:, 0:PTRIM]))
                return None
            prev = step("sy", _shipp, after=[evs[-1]])
        bar()

        def _ldg2(e, st):
            st.dma(e.dma_start(out=ubias[:, :], in_=g["bu"][:, :]))
            st.dma(e.dma_start(out=vbias[:, :], in_=g["bv"][:, :]))
            st.dma(e.dma_start(out=bvrow[0:1, :], in_=g["bbv_row"][:, :]))
            return None
        t_lg2 = step("gp", _ldg2)

        def _bias(e, st):
            e.tensor_add(ubias[:, :], ubias[:, :], sb["bias1"][:, :])
            e.tensor_add(vbias[:, :], vbias[:, :], sb["bias1"][:, :])
            return e.tensor_copy(kbias[:, :], sb["bias2"][:, :])
        t_bias = step("ve", _bias, after=[t_lg2])
        bar()

        for p in range(8):
            _attn_b_pair(p, p_pair, q_u, q_v, kb_pair, vb_pair, S_A, S_B,
                         P_A, P_B, PT_A, PT_B, posbuf, wq_sl, wk_sl, wv_sl,
                         bvrow, ubias, vbias, kbias, ctx_all)
            bar()

        t_rx = reload_x([])

        def _ldwo(e, st):
            st.dma(e.dma_start(out=bwo_sb[:, :, :],
                               in_=g["bwo"].rearrange("(c p) m -> p c m", p=128)))
            st.dma(e.dma_start(out=sb["bias1"][:, :], in_=g["bbout"][:, :]))
            return None
        t_wo = step("gp", _ldwo)

        prev_add = None
        for jc in range(NJC):
            for dc in range(NCH):
                def _mm(e, st, jc=jc, dc=dc):
                    last = None
                    for pc in range(NCH):
                        last = e.matmul(ps["B"][:, :], bwo_sb[:, pc, dc * 128:(dc + 1) * 128],
                                        ctx_all[:, pc, jc * JC:(jc + 1) * JC],
                                        start=(pc == 0), stop=(pc == NCH - 1))
                    return last
                depm = [t_wo] if prev_add is None else [t_wo, prev_add]
                t_mm = step("pe", _mm, after=depm)

                def _add(e, st, jc=jc, dc=dc):
                    xv = x_sb[:, dc, jc * JC:(jc + 1) * JC]
                    return e.scalar_tensor_tensor(xv, ps["B"][:, :],
                                                  sb["bias1"][:, dc:dc + 1], xv,
                                                  ALU.add, ALU.add)
                prev_add = step("ve", _add, after=[t_mm, t_rx])
        bar()

    def _attn_b_pair(p, p_pair, q_u, q_v, kb_pair, vb_pair, S_A, S_B,
                     P_A, P_B, PT_A, PT_B, posbuf, wq_sl, wk_sl, wv_sl,
                     bvrow, ubias, vbias, kbias, ctx_all):
        hA, hB = 2 * p, 2 * p + 1

        def _ldw(e, st, p=p):
            st.dma(e.dma_start(out=p_pair[:, :], in_=g["pd"][p, :, :]))
            for dst, wsrc in ((wq_sl, "bwq"), (wk_sl, "bwk"), (wv_sl, "bwv")):
                st.dma(e.dma_start(
                    out=dst[:, :, :],
                    in_=g[wsrc][:, p * 128:(p + 1) * 128]
                        .rearrange("(c q) m -> q c m", q=128)))
            return None
        t_ldw = step("gp", _ldw)

        def _qmm(e, st):
            last = None
            for jc in range(NJC):
                for c in range(NCH):
                    last = e.matmul(psA4[jc], wq_sl[:, c, :],
                                    sb["xb"][:, c, jc * JC:(jc + 1) * JC],
                                    start=(c == 0), stop=(c == NCH - 1))
            return last
        t_qmm = step("pe", _qmm, after=[t_ldw])

        def _qev(e, st, p=p):
            e.activation(q_u[:, :], ps["A"][:, :], AF.Identity, bias=ubias[:, p:p + 1])
            return e.activation(q_v[:, :], ps["A"][:, :], AF.Identity,
                                bias=vbias[:, p:p + 1])
        t_qev = step("ac", _qev, after=[t_qmm])

        def _kmm(e, st):
            last = None
            for jc in range(NJC):
                for c in range(NCH):
                    last = e.matmul(psA4[jc], wk_sl[:, c, :],
                                    sb["xb"][:, c, jc * JC:(jc + 1) * JC],
                                    start=(c == 0), stop=(c == NCH - 1))
            return last
        t_kmm = step("pe", _kmm, after=[t_qev])
        t_kev = step("ac", lambda e, st, p=p: e.activation(
            kb_pair[:, :], ps["A"][:, :], AF.Identity, bias=kbias[:, p:p + 1]),
            after=[t_kmm])

        vprev = t_kev
        for tc2 in range(8):
            def _vmm(e, st, tc2=tc2, p=p):
                last = None
                for i, tt in enumerate((2 * tc2, 2 * tc2 + 1)):
                    bank = ps["B"] if i == 0 else ps["C"]
                    for c in range(NCH):
                        e.matmul(bank[:, 0:128], sb["xb"][:, c, tt * 128:(tt + 1) * 128],
                                 wv_sl[:, c, :], start=(c == 0), stop=False)
                    last = e.matmul(bank[:, 0:128], sb["ones_rowf"][0:1, :],
                                    bvrow[0:1, p * 128:(p + 1) * 128],
                                    start=False, stop=True)
                return last
            t_vmm = step("pe", _vmm, after=[vprev])

            def _vev(e, st, tc2=tc2):
                e.tensor_copy(vb_pair[:, 2 * tc2, :], ps["B"][:, 0:128])
                return e.tensor_copy(vb_pair[:, 2 * tc2 + 1, :], ps["C"][:, 0:128])
            vprev = step("ve", _vev, after=[t_vmm])
        t_vdone = vprev

        pprev = t_vdone
        for qt in range(16):
            w0 = 128 * (15 - qt)
            qsl = slice(qt * 128, (qt + 1) * 128)

            def _cmm(e, st, qt=qt, qsl=qsl):
                banksB = (ps["B"], ps["C"], ps["D"], ps["E"])
                last = None
                for ss in range(4):
                    e.matmul(psA4[ss], q_u[0:64, qsl],
                             kb_pair[0:64, ss * JC:(ss + 1) * JC], start=True, stop=True)
                    last = e.matmul(banksB[ss][:, :], q_u[64:128, qsl],
                                    kb_pair[64:128, ss * JC:(ss + 1) * JC],
                                    start=True, stop=True, tile_position=(64, 0))
                return last
            t_cm = step("pe", _cmm, after=[pprev])

            def _cevA(e, st, qt=qt):
                last = None
                for ss in range(4):
                    last = e.activation(S_A[:, ss * JC:(ss + 1) * JC], psA4[ss], AF.Identity)
                return last
            t_ca = step("ac", _cevA, after=[t_cm])

            def _cevB(e, st, qt=qt):
                last = None
                for ss, bank in enumerate((ps["B"], ps["C"], ps["D"], ps["E"])):
                    last = e.tensor_copy(S_B[:, ss * JC:(ss + 1) * JC], bank[:, :])
                return last
            t_cb = step("ve", _cevB, after=[t_cm])

            posev = []
            for w in range(5):
                def _pmm(e, st, w=w, w0=w0, qsl=qsl):
                    e.matmul(ps["B"][:, :], q_v[0:64, qsl],
                             p_pair[0:64, w0 + w * JC:w0 + (w + 1) * JC],
                             start=True, stop=True)
                    return e.matmul(ps["C"][:, :], q_v[64:128, qsl],
                                    p_pair[64:128, w0 + w * JC:w0 + (w + 1) * JC],
                                    start=True, stop=True, tile_position=(64, 0))
                dpp = [t_cb] if w == 0 else [posev[-1]]
                t_pm = step("pe", _pmm, after=dpp)

                def _pev(e, st, w=w):
                    e.tensor_copy(posbuf[:, 0, :], ps["B"][:, :])
                    return e.tensor_copy(posbuf[:, 1, :], ps["C"][:, :])
                t_pv = step("ve", _pev, after=[t_pm])

                def _pship(e, st, w=w, qt=qt, hA=hA, hB=hB):
                    st.dma(e.dma_start(out=g["ps_scr"][hA, qt, :, w * JC:(w + 1) * JC],
                                       in_=posbuf[:, 0, :]))
                    st.dma(e.dma_start(out=g["ps_scr"][hB, qt, :, w * JC:(w + 1) * JC],
                                       in_=posbuf[:, 1, :]))
                    return None
                t_ps = step("sy", _pship, after=[t_pv])
                posev.append(t_ps)

            pst = g["ps_scr"].tensor

            def _shear(e, st, qt=qt, hA=hA, hB=hB):
                baseA = ((hA * 16 + qt) * 128) * PW + 127
                baseB = ((hB * 16 + qt) * 128) * PW + 127
                st.dma(e.dma_start(out=S_A[:, :], in_=AP(pst, baseA, [[PW - 1, 128], [1, T]]),
                                   accum_op=ALU.add))
                st.dma(e.dma_start(out=S_B[:, :], in_=AP(pst, baseB, [[PW - 1, 128], [1, T]]),
                                   accum_op=ALU.add))
                return None
            t_sh = step("gp", _shear, after=[posev[-1], t_ca, t_cb])

            def _sm1(e, st):
                st.sync(e, e.reduce_max(sb["nmcol"][:, :], S_A[:, :], axis=AX, negate=True))
                e.tensor_scalar(sb["nmcol"][:, :], sb["nmcol"][:, :], 0.125, None, ALU.mult)
                st.sync(e, e.reduce_max(sb["sumcol"][:, :], S_B[:, :], axis=AX, negate=True))
                return e.tensor_scalar(sb["sumcol"][:, :], sb["sumcol"][:, :], 0.125,
                                       None, ALU.mult)
            t_s1 = step("ve", _sm1, after=[t_sh])

            def _sm2(e, st):
                e.activation(P_A[:, :], S_A[:, :], AF.Exp, bias=sb["nmcol"][:, :],
                             scale=0.125, accum_out=sb["accA"][:, :])
                return e.activation(P_B[:, :], S_B[:, :], AF.Exp, bias=sb["sumcol"][:, :],
                                    scale=0.125, accum_out=sb["accB"][:, :])
            t_s2 = step("ac", _sm2, after=[t_s1])

            def _sm3(e, st):
                e.reciprocal(sb["accA"][:, :], sb["accA"][:, :])
                st.sync(e, e.reciprocal(sb["accB"][:, :], sb["accB"][:, :]))
                e.tensor_scalar(P_A[:, :], P_A[:, :], sb["accA"][:, :], None, ALU.mult)
                return e.tensor_scalar(P_B[:, :], P_B[:, :], sb["accB"][:, :],
                                       None, ALU.mult)
            t_s3 = step("ve", _sm3, after=[t_s2])

            def _pship2(e, st, qt=qt, hA=hA, hB=hB):
                st.dma(e.dma_start(out=g["p_scr"][hA, qt * 128:(qt + 1) * 128, :],
                                   in_=P_A[:, :]))
                st.dma(e.dma_start(out=g["p_scr"][hB, qt * 128:(qt + 1) * 128, :],
                                   in_=P_B[:, :]))
                return None
            pprev = step("sy", _pship2, after=[t_s3])

        av_prev = pprev
        ld_toks = []
        for stp in range(16):
            par = stp % 2

            def _ldpt(e, st, stp=stp, par=par, hA=hA, hB=hB):
                st.dma(e.dma_start_transpose(
                    PT_A[:, par, :], g["p_scr"][hA, :, stp * 128:(stp + 1) * 128]))
                st.dma(e.dma_start_transpose(
                    PT_B[:, par, :], g["p_scr"][hB, :, stp * 128:(stp + 1) * 128]))
                return None
            dpl = [av_prev] if stp < 2 else [av_prev, ld_toks[stp - 2][1]]
            t_lp = step("ac", _ldpt, after=dpl)

            def _avmm(e, st, stp=stp, par=par):
                banksB = (ps["B"], ps["C"], ps["D"], ps["E"])
                last = None
                for qc in range(4):
                    e.matmul(ps["A"][0:64, qc * JC:(qc + 1) * JC],
                             vb_pair[:, stp, 0:64], PT_A[:, par, qc * JC:(qc + 1) * JC],
                             start=(stp == 0), stop=(stp == 15))
                    last = e.matmul(banksB[qc][64:128, :],
                                    vb_pair[:, stp, 64:128],
                                    PT_B[:, par, qc * JC:(qc + 1) * JC],
                                    start=(stp == 0), stop=(stp == 15),
                                    tile_position=(0, 64))
                return last
            t_av = step("pe", _avmm, after=[t_lp] if stp == 0 else [t_lp, ld_toks[-1][1]])
            ld_toks.append((t_lp, t_av))

        def _ctxev(e, st, p=p):
            banksB = (ps["B"], ps["C"], ps["D"], ps["E"])
            e.tensor_copy(ctx_all[0:64, p, :], ps["A"][0:64, :])
            last = None
            for qc in range(4):
                last = e.tensor_copy(ctx_all[64:128, p, qc * JC:(qc + 1) * JC],
                                     banksB[qc][64:128, :])
            return last
        step("ve", _ctxev, after=[ld_toks[-1][1]])

    bar()
    attn_a()
    if sub < 99 and stage == 1:
        pass
    if stage >= 2:
        _r = ffn(0, sub)
        if _r is None:
            bar()
            src = _gate_store[0] if _gate_store[0] is not None else x_sb
            if isinstance(src, tuple):
                kindg, tile = src
                if kindg == "bf16":
                    def _gstore(e, st, tile=tile):
                        st.dma(e.dma_start(out=y_d.rearrange("(c p) t -> p c t", p=128),
                                           in_=tile[:, :, :]))
                        return None
                    tg = step("gp", _gstore)
                elif kindg == "p128":
                    def _gstore(e, st, tile=tile):
                        st.dma(e.dma_start(
                            out=y_d[0:128, 0:128],
                            in_=tile[:, :, :].rearrange("p a b -> p (a b)")))
                        return None
                    tg = step("sy", _gstore)
                else:
                    def _gstore(e, st, tile=tile):
                        st.dma(e.dma_start(out=y_d[0:8, :], in_=tile[0:8, :]))
                        return None
                    tg = step("sy", _gstore)
            else:
                def _gstore(e, st, src=src):
                    st.dma(e.dma_start(out=y_d.rearrange("(c p) t -> p c t", p=128),
                                       in_=src[:, :, :]))
                    return None
                tg = step("sy", _gstore)
            step("sy", lambda e, st: e.nop(), after=[tg])
            sch.flush()
            return nc
        bar()
    if stage >= 3:
        attn_b()
    if stage >= 4:
        ffn(1)
        bar()

    def _store(e, st):
        st.dma(e.dma_start(out=y_d.rearrange("(c p) t -> p c t", p=128),
                           in_=x_sb[:, :, :]))
        return None
    t_out = step("sy", _store)
    step("sy", lambda e, st: e.nop(), after=[t_out])
    sch.flush()
    return nc



def kernel(**inputs):
    shared = _host_layout(inputs)
    nc = build(4)
    in_maps = [_core_inputs(shared, inputs, c) for c in range(NCORES)]
    res = run_bass_kernel_spmd(nc, in_maps, core_ids=list(range(NCORES)))
    out = np.zeros((NCORES, T, D), np.float32)
    for c in range(NCORES):
        out[c] = np.asarray(res.results[c]["y"], np.float32).T
    return out


# version 02
# speedup vs baseline: 1.0553x; 1.0553x over previous
import math
import numpy as np

import concourse.bass as bass
import concourse.mybir as mybir
from concourse.bass_utils import run_bass_kernel_spmd
from concourse.bass_types import AP

F32 = mybir.dt.float32
BF16 = mybir.dt.bfloat16
AF = mybir.ActivationFunctionType
ALU = mybir.AluOpType
AX = mybir.AxisListType.X

NCORES = 8
D = 1024
T = 2048
NH, DH = 16, 64
JC = 512
NJC = T // JC
NCH = D // 128
FF = 2048
PW = 2560
PPAD = 4608
PTRIM = 4480

HPERM = []
for _c in range(8):
    HPERM += [_c, _c + 8]
GORDER = [0, 2, 1, 3]

ENGS = ("sy", "gp", "pe", "ac", "ve")
K = 1024



def _host_layout(inputs):
    f32 = lambda a: np.ascontiguousarray(np.asarray(a), dtype=np.float32)
    col8 = lambda a: f32(np.asarray(a, np.float32).reshape(8, 128).T)
    w = {}
    w["ident"] = np.eye(128, dtype=np.float32)

    for nm, key in (("W_q", "aq"), ("W_k", "ak"), ("W_v", "av"), ("W_o", "ao")):
        w[key] = f32(inputs[f"L0_attn_a_{nm}"])
    cent = np.asarray(inputs["L0_attn_a_centers"], np.float32)
    cp = np.zeros((128, 8, 64), np.float32)
    for p in range(8):
        cp[0:64, p, :] = cent[2 * p].T
        cp[64:128, p, :] = cent[2 * p + 1].T
    w["acent"] = f32(cp)
    w["als"] = f32(inputs["L0_attn_a_log_scales"])
    w["aamp"] = f32(inputs["L0_attn_a_amplitudes"])

    qkv = np.asarray(inputs["L0_ffn_qkv_weight"], np.float32)
    qw = qkv[:NH * DH].reshape(NH, DH, D)[HPERM].reshape(NH * DH, D)
    kw = qkv[NH * DH:(NH + 4) * DH].reshape(4, DH, D)[GORDER].reshape(4 * DH, D)
    vw = qkv[(NH + 4) * DH:].reshape(4, DH, D)[GORDER].reshape(4 * DH, D)
    w["fqkv"] = f32(np.concatenate([qw, kw, vw], 0).T)
    pj = np.asarray(inputs["L0_ffn_proj_weight"], np.float32)
    w["fproj"] = f32(pj.T.reshape(NH, DH, D)[HPERM].reshape(D, D))
    w["frout"] = f32(np.asarray(inputs["L0_ffn_router_weight"], np.float32).T)
    w["fw1"] = f32(inputs["L0_ffn_moe_w1"])
    w["fw2"] = f32(inputs["L0_ffn_moe_w2"])
    w["fln1w"] = col8(inputs["L0_ffn_ln1_weight"])
    w["fln1b"] = col8(inputs["L0_ffn_ln1_bias"])
    w["fln2w"] = col8(inputs["L0_ffn_ln2_weight"])
    w["fln2b"] = col8(inputs["L0_ffn_ln2_bias"])

    for nm, key in (("Wq", "bwq"), ("Wk", "bwk"), ("Wv", "bwv"),
                    ("Wout", "bwo"), ("Wpos", "bwp")):
        w[key] = f32(np.asarray(inputs[f"L0_attn_b_{nm}"], np.float32).T)
    w["bbq"] = col8(inputs["L0_attn_b_bq"])
    w["bbk"] = col8(inputs["L0_attn_b_bk"])
    w["bbout"] = col8(inputs["L0_attn_b_bout"])
    w["bg"] = col8(inputs["L0_attn_b_ln_gamma"])
    w["bb"] = col8(inputs["L0_attn_b_ln_beta"])
    w["bbv_row"] = f32(np.asarray(inputs["L0_attn_b_bv"], np.float32).reshape(1, D))
    w["bu"] = col8(np.asarray(inputs["L0_attn_b_u_bias"], np.float32).reshape(D))
    w["bv"] = col8(np.asarray(inputs["L0_attn_b_v_bias"], np.float32).reshape(D))

    pos = np.arange(T, dtype=np.float64)[:, None]
    div = np.exp(np.arange(0, D, 2, dtype=np.float64) * (-math.log(10000.0) / D))
    ang = pos * div
    pe_pos = np.stack([np.sin(ang), np.cos(ang)], -1).reshape(T, D)
    pe_neg = np.stack([np.sin(-ang), np.cos(-ang)], -1).reshape(T, D)
    pe = np.concatenate([pe_pos[::-1], pe_neg[1:]], 0)
    pet = np.zeros((D, PPAD), np.float64)
    pet[:, :2 * T - 1] = pe.T
    w["pet"] = f32(pet)

    m16 = np.zeros((128, 8, 16), np.float32)
    for c in range(8):
        m16[0:64, c, 2 * c] = 0.125
        m16[64:128, c, 2 * c + 1] = 0.125
    w["mask16"] = f32(m16)
    s16 = np.zeros((16, 8, 128), np.float32)
    for c in range(8):
        s16[2 * c, c, 0:64] = 1.0
        s16[2 * c + 1, c, 64:128] = 1.0
    w["sel16"] = f32(s16)
    d8 = np.zeros((8, 8, 128), np.float32)
    for e in range(8):
        d8[e, e, :] = 1.0
    w["delta8"] = f32(d8)
    return w


def _core_inputs(shared, inputs, core):
    d = dict(shared)
    x = np.asarray(inputs["L0_attn_a_x"], np.float32)[core]
    d["x"] = np.ascontiguousarray(x.T)
    tm = np.zeros((16, 8), np.float32)
    tm[:, :core + 1] = 1.0
    d["tmask8"] = tm
    return d



class _Step:
    __slots__ = ("eng", "fn", "after", "tgt", "kind")

    def __init__(self, eng, fn, after, kind="op"):
        self.eng = eng
        self.fn = fn
        self.after = list(after)
        self.tgt = None
        self.kind = kind


class _StepCtx:

    def __init__(self, sched, eng):
        self.sched = sched
        self.eng = eng
        self.ndma = 0

    def dma(self, inst):
        self.sched.dcount[self.eng] += 16
        inst.then_inc(self.sched.dsem[self.eng], 16)
        self.ndma += 1
        return inst

    def sync(self, e, inst):
        self.sched.count[self.eng] += 1
        inst.then_inc(self.sched.sem[self.eng], 1)
        e.wait_ge(self.sched.sem[self.eng], self.sched.count[self.eng])
        return inst


class Sched:
    def __init__(self, nc):
        self.nc = nc
        self.sem = {e: nc.alloc_semaphore("ec_" + e) for e in ENGS}
        self.dsem = {e: nc.alloc_semaphore("ed_" + e) for e in ENGS}
        self.count = {e: 0 for e in ENGS}
        self.dcount = {e: 0 for e in ENGS}
        self.pending = []

    def step(self, eng, fn, after=(), kind="op"):
        st = _Step(eng, fn, after, kind)
        self.pending.append(st)
        return st

    def barrier(self):
        self.flush()
        snapc = dict(self.count)
        snapd = dict(self.dcount)
        toks = []
        for e in ENGS:
            waits = []
            for f in ENGS:
                if f != e:
                    waits.append((self.sem[f], snapc[f]))
                    waits.append((self.dsem[f], snapd[f]))
            waits.append((self.dsem[e], snapd[e]))
            toks.append(self.step(e, None, after=waits, kind="bar"))
        self.flush()
        return toks

    def flush(self):
        if not self.pending:
            return
        steps = self.pending
        self.pending = []
        with self.nc.Block() as blk:
            bodies = {"sy": blk.sync, "gp": blk.gpsimd, "pe": blk.tensor,
                      "ac": blk.scalar, "ve": blk.vector}
            i = 0
            while i < len(steps):
                j = i
                eng = steps[i].eng
                while j < len(steps) and steps[j].eng == eng:
                    j += 1
                group = steps[i:j]

                def run(e, group=group):
                    for st in group:
                        for a in st.after:
                            if isinstance(a, tuple):
                                sh, val = a
                            else:
                                assert a.tgt is not None, "dep on later step"
                                sh, val = a.tgt
                            if val > 0:
                                e.wait_ge(sh, val)
                        ctx = _StepCtx(self, st.eng)
                        if st.kind == "bar":
                            last = e.nop()
                        else:
                            last = st.fn(e, ctx)
                        if last is not None:
                            assert ctx.ndma == 0, "step mixes dma and compute"
                            self.count[st.eng] += 1
                            last.then_inc(self.sem[st.eng], 1)
                            st.tgt = (self.sem[st.eng], self.count[st.eng])
                        elif ctx.ndma > 0:
                            st.tgt = (self.dsem[st.eng], self.dcount[st.eng])
                        else:
                            self.count[st.eng] += 1
                            e.sem_inc(self.sem[st.eng], 1)
                            st.tgt = (self.sem[st.eng], self.count[st.eng])
                bodies[eng](run)
                i = j



def build(stage=4, sub=99):
    nc = bass.Bass("TRN2", target_bir_lowering=False, debug=False,
                   detect_race_conditions=False)
    sch = Sched(nc)
    g = {}
    sb = {}
    ps = {}

    for name, shape in (
        ("x", (D, T)), ("ident", (128, 128)),
        ("aq", (D, D)), ("ak", (D, D)), ("av", (D, D)), ("ao", (D, D)),
        ("acent", (128, 8, 64)), ("als", (16, 64)), ("aamp", (16, 64)),
        ("fqkv", (D, 1536)), ("fproj", (D, D)), ("frout", (D, 8)),
        ("fw1", (8, D, FF)), ("fw2", (8, FF, D)),
        ("fln1w", (128, 8)), ("fln1b", (128, 8)),
        ("fln2w", (128, 8)), ("fln2b", (128, 8)),
        ("bwq", (D, D)), ("bwk", (D, D)), ("bwv", (D, D)), ("bwo", (D, D)),
        ("bwp", (D, D)),
        ("bbq", (128, 8)), ("bbk", (128, 8)), ("bbout", (128, 8)),
        ("bg", (128, 8)), ("bb", (128, 8)), ("bbv_row", (1, D)),
        ("bu", (128, 8)), ("bv", (128, 8)),
        ("pet", (D, PPAD)),
        ("mask16", (128, 8, 16)), ("sel16", (16, 8, 128)),
        ("delta8", (8, 8, 128)), ("tmask8", (16, 8)),
    ):
        g[name] = nc.dram_tensor(name, list(shape), F32, kind="ExternalInput").ap()
    y_d = nc.dram_tensor("y", [D, T], F32, kind="ExternalOutput").ap()

    g["xspill"] = nc.dram_tensor("xspill", [D, T], F32, kind="Internal").ap()
    g["ps_scr"] = nc.dram_tensor("ps_scr", [16, 16, 128, PW], BF16, kind="Internal").ap()
    g["p_scr"] = nc.dram_tensor("p_scr", [16, T, T], BF16, kind="Internal").ap()
    g["pd"] = nc.dram_tensor("pd", [8, 128, PTRIM], BF16, kind="Internal").ap()
    g["agin"] = nc.dram_tensor("agin", [512, T], BF16, kind="Internal")
    g["agout"] = nc.dram_tensor("agout", [8 * 512, T], BF16, kind="Internal")

    def asb(name, shape, dt=F32):
        sb[name] = nc.alloc_sbuf_tensor("s_" + name, list(shape), dt)
        return sb[name]

    asb("xb", (128, NCH, T), BF16)
    asb("xb_stat", (128, NCH, JC), BF16)
    asb("lnt", (128, JC))
    asb("rowA", (1, T)); asb("rowB", (1, T)); asb("rowC", (1, T))
    asb("lnw", (128, 8)); asb("lnb", (128, 8))
    asb("ones_col", (128, 1), BF16)
    asb("onescol_f", (128, 1))
    asb("ones_row", (1, 128), BF16)
    asb("ones_rowf", (1, 128), F32)
    asb("ones512", (1, JC), BF16)
    asb("identf", (128, 128), F32)
    asb("cdb", (1, 4))
    nc.const_aps.register(F32, 1e-5, sb["cdb"][0:1, 0:1])
    nc.const_aps.register(F32, 1e-6, sb["cdb"][0:1, 1:2])
    asb("mask16b", (128, 8, 16), BF16)
    asb("tmask", (16, 8))
    asb("sel16b", (16, 8, 128), BF16)
    asb("nmcol", (128, 1)); asb("sumcol", (128, 1))
    asb("accA", (128, 1)); asb("accB", (128, 1))
    asb("bias1", (128, 8)); asb("bias2", (128, 8))

    arena = nc.alloc_sbuf_tensor("s_arena", [128, 130 * K], mybir.dt.uint8)
    AR0 = nc.lookup_mloc(arena).addr
    _pl = {}

    def place(name, shape, dt, off):
        key = (name, off)
        if key not in _pl:
            _pl[key] = nc.alloc_sbuf_tensor_at(
                "a_" + name, list(shape), dt, offset=AR0 + off)
        return _pl[key]

    x_sb = place("x", (128, NCH, T), F32, 0)

    ps["A"] = nc.alloc_psum_tensor("psA", [128, 2048], F32)
    ps["B"] = nc.alloc_psum_tensor("psB", [128, 512], F32)
    ps["C"] = nc.alloc_psum_tensor("psC", [128, 512], F32)
    ps["D"] = nc.alloc_psum_tensor("psD", [128, 512], F32)
    ps["E"] = nc.alloc_psum_tensor("psE", [128, 512], F32)
    psA4 = [ps["A"][:, i * JC:(i + 1) * JC] for i in range(4)]

    step = sch.step
    bar = sch.barrier
    _gate_store = [None]

    def _init(e, st):
        st.dma(e.dma_start(out=sb["identf"][:, :], in_=g["ident"][:, :]))
        st.dma(e.dma_start(out=x_sb[:, :, :],
                           in_=g["x"].rearrange("(c p) t -> p c t", p=128)))
        st.dma(e.dma_start(out=sb["tmask"][:, :], in_=g["tmask8"][:, :]))
        return None
    ld0 = step("sy", _init)

    def _ldb16(e, st):
        st.dma(e.dma_start(out=sb["mask16b"][:, :, :], in_=g["mask16"][:, :, :]))
        st.dma(e.dma_start(out=sb["sel16b"][:, :, :], in_=g["sel16"][:, :, :]))
        return None
    ldb = step("gp", _ldb16)

    def _consts(e, st):
        e.memset(sb["cdb"][0:1, 0:1], 1e-5)
        e.memset(sb["cdb"][0:1, 1:2], 1e-6)
        e.memset(sb["ones_col"][:, :], 1.0)
        e.memset(sb["onescol_f"][:, :], 1.0)
        e.memset(sb["ones_row"][:, :], 1.0)
        e.memset(sb["ones_rowf"][:, :], 1.0)
        return e.memset(sb["ones512"][:, :], 1.0)
    c0 = step("ve", _consts)

    def emit_ln(src3, out3, kind="plain", eps=1e-5, dep=()):
        stats, evacs = [], []
        prev_cast = None
        for jc in range(NJC):
            def _cast(e, st, jc=jc):
                last = None
                for c in range(NCH):
                    last = e.tensor_copy(sb["xb_stat"][:, c, :],
                                         src3[:, c, jc * JC:(jc + 1) * JC])
                return last
            depc = list(dep) if jc == 0 else [stats[jc - 1][1]]
            t_cast = step("ve", _cast, after=depc)

            def _sum(e, st, jc=jc):
                last = None
                for c in range(NCH):
                    last = e.matmul(ps["B"][0:1, :], sb["ones_col"][:, :],
                                    sb["xb_stat"][:, c, :],
                                    start=(c == 0), stop=(c == NCH - 1))
                return last
            deps = [t_cast] if jc == 0 else [t_cast, evacs[jc - 1]]
            t_sum = step("pe", _sum, after=deps)

            def _sq(e, st, jc=jc):
                return e.activation(sb["xb_stat"][:, :, :], sb["xb_stat"][:, :, :],
                                    AF.Square)
            t_sq = step("ac", _sq, after=[t_sum])

            def _sum2(e, st, jc=jc):
                last = None
                for c in range(NCH):
                    last = e.matmul(ps["C"][0:1, :], sb["ones_col"][:, :],
                                    sb["xb_stat"][:, c, :],
                                    start=(c == 0), stop=(c == NCH - 1))
                return last
            t_sum2 = step("pe", _sum2, after=[t_sq])
            stats.append((t_sum, t_sum2))

            def _evac(e, st, jc=jc):
                e.tensor_copy(sb["rowA"][0:1, jc * JC:(jc + 1) * JC], ps["B"][0:1, :])
                return e.tensor_copy(sb["rowB"][0:1, jc * JC:(jc + 1) * JC],
                                     ps["C"][0:1, :])
            evacs.append(step("ve", _evac, after=[t_sum2]))

        def _rows1(e, st):
            e.tensor_scalar(sb["rowA"][0:1, :], sb["rowA"][0:1, :], 1.0 / D, None, ALU.mult)
            e.tensor_scalar(sb["rowB"][0:1, :], sb["rowB"][0:1, :], 1.0 / D, None, ALU.mult)
            e.tensor_mul(sb["rowC"][0:1, :], sb["rowA"][0:1, :], sb["rowA"][0:1, :])
            return e.tensor_sub(sb["rowB"][0:1, :], sb["rowB"][0:1, :], sb["rowC"][0:1, :])
        rows1 = step("ve", _rows1, after=[evacs[-1]])

        def _rows2(e, st):
            e.activation(sb["rowB"][0:1, :], sb["rowB"][0:1, :], AF.Ln, bias=eps)
            return e.activation(sb["rowB"][0:1, :], sb["rowB"][0:1, :], AF.Exp, scale=-0.5)
        rows2 = step("ac", _rows2, after=[rows1])

        def _rows3(e, st):
            return e.scalar_tensor_tensor(sb["rowA"][0:1, :], sb["rowA"][0:1, :], -1.0,
                                          sb["rowB"][0:1, :], ALU.mult, ALU.mult)
        rows3 = step("ve", _rows3, after=[rows2])

        prev_app = None
        for jc in range(NJC):
            def _bc(e, st, jc=jc):
                e.matmul(ps["D"][:, :], sb["ones_rowf"][:, :],
                         sb["rowB"][0:1, jc * JC:(jc + 1) * JC], start=True, stop=True)
                return e.matmul(ps["E"][:, :], sb["ones_rowf"][:, :],
                                sb["rowA"][0:1, jc * JC:(jc + 1) * JC], start=True, stop=True)
            depb = [rows3] if prev_app is None else [rows3, prev_app]
            t_bc = step("pe", _bc, after=depb)

            def _app(e, st, jc=jc):
                last = None
                for c in range(NCH):
                    xs = src3[:, c, jc * JC:(jc + 1) * JC]
                    ov = out3[:, c, jc * JC:(jc + 1) * JC]
                    e.tensor_mul(sb["lnt"][:, :], xs, ps["D"][:, :])
                    if kind == "plain":
                        last = e.tensor_add(ov, sb["lnt"][:, :], ps["E"][:, :])
                    else:
                        e.tensor_add(sb["lnt"][:, :], sb["lnt"][:, :], ps["E"][:, :])
                        last = e.tensor_scalar(ov, sb["lnt"][:, :],
                                               sb["lnw"][:, c:c + 1],
                                               sb["lnb"][:, c:c + 1], ALU.mult, ALU.add)
                return last
            prev_app = step("ve", _app, after=[t_bc])
        return prev_app

    def spill_x(dep):
        def _sp(e, st):
            st.dma(e.dma_start(out=g["xspill"].rearrange("(c p) t -> p c t", p=128),
                               in_=x_sb[:, :, :]))
            return None
        return step("sy", _sp, after=dep)

    def reload_x(dep):
        def _rl(e, st):
            st.dma(e.dma_start(out=x_sb[:, :, :],
                               in_=g["xspill"].rearrange("(c p) t -> p c t", p=128)))
            return None
        return step("sy", _rl, after=dep)

    def attn_a():
        pre_o = place("pre_o", (128, 8, T), BF16, 66 * K)
        q_pair = place("q_pair", (128, T), BF16, 99 * K)
        k_pair = place("k_pair", (128, T), BF16, 103 * K)
        v_pair = place("v_pair", (128, 16, 128), BF16, 107 * K)
        eq_pair = place("eq_pair", (128, T), BF16, 111 * K)
        ek_pair = place("ek_pair", (128, 16, 128), BF16, 115 * K)
        aq_p = place("aq_p", (128, NCH, 128), BF16, 119 * K)
        ak_p = place("ak_p", (128, NCH, 128), BF16, 121 * K)
        av_p = place("av_p", (128, NCH, 128), BF16, 123 * K)
        cent_b = place("cent_b", (128, 8, 64), BF16, 125 * K)
        csq = place("csq", (128, 8, 64), F32, 126 * K)
        lsamp = place("lsamp", (16, 128), F32, 128 * K)
        inv_sc = place("inv_sc", (16, 64), F32, int(128.5 * K))
        amp2_sc = place("amp2_sc", (16, 64), F32, int(128.75 * K))
        amp2T = place("amp2T", (128, 16), F32, 129 * K)
        vagg_b = place("vagg_b", (128, 64), BF16, int(129.25 * K))
        ks_col = place("ks_col", (128, 2), BF16, int(129.5 * K))
        invrow = place("invrow", (1, D), F32, 0)
        amp2row = place("amp2row", (1, D), F32, 4 * K)
        c2row = place("c2row", (1, D), F32, 8 * K)
        negc2i = place("negc2i", (1, D), BF16, 12 * K)
        neginvrow = place("neginvrow", (1, D), BF16, 14 * K)
        q2row = place("q2row", (1, 2, T), BF16, 16 * K)
        k2row = place("k2row", (1, 2, T), BF16, 24 * K)
        denA = place("denA", (1, T), F32, 32 * K)
        denB = place("denB", (1, T), F32, 40 * K)
        num_sb = place("num_sb", (128, JC), F32, 48 * K)
        ksrow = place("ksrow", (1, 128), F32, 50 * K)
        invb = place("invb", (128, 64), BF16, 51 * K)
        awo = place("awo", (128, NCH, D), BF16, 99 * K)

        t_ln = emit_ln(x_sb[:, :, :], sb["xb"][:, :, :], "plain", dep=[ld0, c0])
        t_spill = spill_x([t_ln])
        if sub < 1:
            reload_x([t_spill])
            return

        def _ldsc(e, st):
            st.dma(e.dma_start(out=lsamp[:, 0:64], in_=g["als"][:, :]))
            st.dma(e.dma_start(out=lsamp[:, 64:128], in_=g["aamp"][:, :]))
            st.dma(e.dma_start(out=cent_b[:, :, :], in_=g["acent"][:, :, :]))
            return None
        t_ld = step("gp", _ldsc, after=[t_spill])

        t_exp = step("ac", lambda e, st: e.activation(inv_sc[:, :], lsamp[:, 0:64],
                                                      AF.Exp), after=[t_ld])

        def _inv(e, st):
            e.tensor_scalar(inv_sc[:, :], inv_sc[:, :], 1e-3, None, ALU.max)
            e.tensor_mul(inv_sc[:, :], inv_sc[:, :], inv_sc[:, :])
            st.sync(e, e.reciprocal(inv_sc[:, :], inv_sc[:, :]))
            e.tensor_scalar(inv_sc[:, :], inv_sc[:, :], 0.5, None, ALU.mult)
            return e.tensor_mul(amp2_sc[:, :], lsamp[:, 64:128], lsamp[:, 64:128])
        t_inv = step("ve", _inv, after=[t_exp])
        if sub == 11:
            reload_x([t_inv])
            return

        def _rowize(e, st):
            st.dma(e.dma_start(out=invrow[0:1, :], in_=inv_sc[:, :]))
            st.dma(e.dma_start(out=amp2row[0:1, :], in_=amp2_sc[:, :]))
            return None
        t_rz = step("gp", _rowize, after=[t_inv])
        if sub == 12:
            reload_x([t_rz])
            return

        t_a2t = step("pe", lambda e, st: e.matmul(ps["B"][0:64, 0:16], amp2_sc[:, :],
                                                  sb["identf"][0:16, 0:16],
                                                  start=True, stop=True),
                     after=[t_inv])
        t_a2te = step("ve", lambda e, st: e.tensor_copy(amp2T[0:64, :],
                                                        ps["B"][0:64, 0:16]),
                      after=[t_a2t])
        t_csq = step("ac", lambda e, st: e.activation(csq[:, :, :], cent_b[:, :, :],
                                                      AF.Square), after=[t_ld])
        if sub == 13:
            reload_x([t_csq, t_a2te])
            return

        prev = None
        for p in range(8):
            def _c2(e, st, p=p):
                e.matmul(ps["C"][0:1, 0:64], sb["onescol_f"][0:64, 0:1],
                         csq[0:64, p, :], start=True, stop=True)
                return e.matmul(ps["D"][0:1, 0:64], sb["onescol_f"][64:128, 0:1],
                                csq[64:128, p, :], start=True, stop=True)
            t_c2 = step("pe", _c2, after=[t_csq] if prev is None else [prev])

            def _c2e(e, st, p=p):
                e.tensor_copy(c2row[0:1, p * 128:p * 128 + 64], ps["C"][0:1, 0:64])
                return e.tensor_copy(c2row[0:1, p * 128 + 64:(p + 1) * 128],
                                     ps["D"][0:1, 0:64])
            prev = step("ve", _c2e, after=[t_c2])
            if sub == 130 + p:
                reload_x([prev])
                return

        def _fin(e, st):
            e.tensor_mul(c2row[0:1, :], c2row[0:1, :], invrow[0:1, :])
            e.tensor_scalar(negc2i[0:1, :], c2row[0:1, :], -1.0, None, ALU.mult)
            return e.tensor_scalar(neginvrow[0:1, :], invrow[0:1, :], -1.0, None, ALU.mult)
        t_fin = step("ve", _fin, after=[prev, t_rz, t_a2te])
        if sub == 14:
            reload_x([t_fin])
            return

        prev = t_fin
        for p in range(8):
            def _ib(e, st, p=p):
                e.matmul(ps["D"][0:64, 0:64], sb["ones_rowf"][0:1, 0:64],
                         invrow[0:1, (2 * p) * 64:(2 * p) * 64 + 64], start=True, stop=True)
                return e.matmul(ps["D"][64:128, 0:64], sb["ones_rowf"][0:1, 0:64],
                                invrow[0:1, (2 * p + 1) * 64:(2 * p + 1) * 64 + 64],
                                start=True, stop=True, tile_position=(0, 64))
            t_ib = step("pe", _ib, after=[prev])

            def _ibe(e, st, p=p):
                e.tensor_copy(invb[:, :], ps["D"][:, 0:64])
                return e.scalar_tensor_tensor(cent_b[:, p, :], invb[:, :], 2.0,
                                              cent_b[:, p, :], ALU.mult, ALU.mult)
            prev = step("ve", _ibe, after=[t_ib])
        bar()
        if sub < 2:
            reload_x([])
            return

        for p in range(8):
            if (30 <= sub <= 40) or (130 <= sub <= 140) or (330 <= sub <= 340):
                _attn_a_pair(p, q_pair, k_pair, v_pair, eq_pair, ek_pair,
                             aq_p, ak_p, av_p, cent_b, invrow, amp2row, c2row,
                             negc2i, neginvrow, q2row, k2row, denA, denB,
                             num_sb, ksrow, vagg_b, ks_col, amp2T, pre_o, sub)
                bar()
                reload_x([])
                return
            if p >= max(1, sub - 2):
                step("ve", lambda e, st, p=p: e.memset(pre_o[:, p, :], 0.0))
                continue
            _attn_a_pair(p, q_pair, k_pair, v_pair, eq_pair, ek_pair,
                         aq_p, ak_p, av_p, cent_b, invrow, amp2row, c2row,
                         negc2i, neginvrow, q2row, k2row, denA, denB,
                         num_sb, ksrow, vagg_b, ks_col, amp2T, pre_o, 99)
            bar()
        bar()

        t_rx = reload_x([])
        t_wo = step("gp", lambda e, st: st.dma(e.dma_start(
            out=awo[:, :, :], in_=g["ao"].rearrange("(c p) m -> p c m", p=128))) and None)

        adds = [None, None]
        k = 0
        for jc in range(NJC):
            for dc in range(NCH):
                bk = ps["B"] if k % 2 == 0 else ps["C"]

                def _mm(e, st, jc=jc, dc=dc, bk=bk):
                    last = None
                    for pc in range(NCH):
                        last = e.matmul(bk[:, :], awo[:, pc, dc * 128:(dc + 1) * 128],
                                        pre_o[:, pc, jc * JC:(jc + 1) * JC],
                                        start=(pc == 0), stop=(pc == NCH - 1))
                    return last
                depm = [t_wo] if adds[k % 2] is None else [t_wo, adds[k % 2]]
                t_mm = step("pe", _mm, after=depm)

                def _add(e, st, jc=jc, dc=dc, bk=bk):
                    xv = x_sb[:, dc, jc * JC:(jc + 1) * JC]
                    return e.tensor_add(xv, bk[:, :], xv)
                adds[k % 2] = step("ve", _add, after=[t_mm, t_rx])
                k += 1
        bar()

    def _attn_a_pair(p, q_pair, k_pair, v_pair, eq_pair, ek_pair,
                     aq_p, ak_p, av_p, cent_b, invrow, amp2row, c2row,
                     negc2i, neginvrow, q2row, k2row, denA, denB,
                     num_sb, ksrow, vagg_b, ks_col, amp2T, pre_o, sub=99):
        hA, hB = 2 * p, 2 * p + 1

        def _ldw(e, st, p=p):
            for dst, wsrc in ((aq_p, "aq"), (ak_p, "ak"), (av_p, "av")):
                st.dma(e.dma_start(
                    out=dst[:, :, :],
                    in_=g[wsrc][:, p * 128:(p + 1) * 128]
                        .rearrange("(c q) m -> q c m", q=128)))
            return None
        t_ldw = step("gp", _ldw)

        def _qmm(e, st):
            last = None
            for jc in range(NJC):
                for c in range(NCH):
                    last = e.matmul(psA4[jc], aq_p[:, c, :],
                                    sb["xb"][:, c, jc * JC:(jc + 1) * JC],
                                    start=(c == 0), stop=(c == NCH - 1))
            return last
        t_qmm = step("pe", _qmm, after=[t_ldw])
        t_qev = step("ve", lambda e, st: e.tensor_copy(q_pair[:, :], ps["A"][:, :]),
                     after=[t_qmm])
        if sub == 31:
            return

        def _kmm(e, st):
            last = None
            for jc in range(NJC):
                for c in range(NCH):
                    last = e.matmul(psA4[jc], ak_p[:, c, :],
                                    sb["xb"][:, c, jc * JC:(jc + 1) * JC],
                                    start=(c == 0), stop=(c == NCH - 1))
            return last
        t_kmm = step("pe", _kmm, after=[t_qev])
        t_kev = step("ve", lambda e, st: e.tensor_copy(k_pair[:, :], ps["A"][:, :]),
                     after=[t_kmm])

        vprev = None
        for tc2 in range(8):
            def _vmm(e, st, tc2=tc2):
                last = None
                for i, tt in enumerate((2 * tc2, 2 * tc2 + 1)):
                    bank = ps["B"] if i == 0 else ps["C"]
                    for c in range(NCH):
                        last = e.matmul(bank[:, 0:128],
                                        sb["xb"][:, c, tt * 128:(tt + 1) * 128],
                                        av_p[:, c, :], start=(c == 0), stop=(c == NCH - 1))
                return last
            t_vmm = step("pe", _vmm, after=[t_ldw] if vprev is None else [vprev])

            def _vev(e, st, tc2=tc2):
                e.tensor_copy(v_pair[:, 2 * tc2, :], ps["B"][:, 0:128])
                return e.tensor_copy(v_pair[:, 2 * tc2 + 1, :], ps["C"][:, 0:128])
            vprev = step("ve", _vev, after=[t_vmm])
        t_vdone = vprev
        if sub == 32:
            return

        t_sqq = step("ac", lambda e, st: e.activation(eq_pair[:, :], q_pair[:, :],
                                                      AF.Square), after=[t_qev])
        prev = t_sqq
        for jc in range(NJC):
            def _q2jc(e, st, jc=jc):
                e.matmul(ps["D"][0:1, 0:JC], sb["ones_col"][0:64, 0:1],
                         eq_pair[0:64, jc * JC:(jc + 1) * JC], start=True, stop=True)
                return e.matmul(ps["E"][0:1, 0:JC], sb["ones_col"][64:128, 0:1],
                                eq_pair[64:128, jc * JC:(jc + 1) * JC],
                                start=True, stop=True)
            t_q2 = step("pe", _q2jc, after=[prev])

            def _q2ev(e, st, jc=jc):
                e.tensor_copy(q2row[0:1, 0, jc * JC:(jc + 1) * JC], ps["D"][0:1, 0:JC])
                return e.tensor_copy(q2row[0:1, 1, jc * JC:(jc + 1) * JC],
                                     ps["E"][0:1, 0:JC])
            prev = step("ve", _q2ev, after=[t_q2])
        t_q2done = prev
        if sub == 331:
            return

        t_sqk = step("ac", lambda e, st: e.activation(
            eq_pair[:, :], k_pair[:, :], AF.Square), after=[t_kev, t_q2done])
        prev = t_sqk
        for jc in range(NJC):
            def _k2jc(e, st, jc=jc):
                e.matmul(ps["D"][0:1, 0:JC], sb["ones_col"][0:64, 0:1],
                         eq_pair[0:64, jc * JC:(jc + 1) * JC], start=True, stop=True)
                return e.matmul(ps["E"][0:1, 0:JC], sb["ones_col"][64:128, 0:1],
                                eq_pair[64:128, jc * JC:(jc + 1) * JC],
                                start=True, stop=True)
            t_k2 = step("pe", _k2jc, after=[prev])

            def _k2ev(e, st, jc=jc):
                e.tensor_copy(k2row[0:1, 0, jc * JC:(jc + 1) * JC], ps["D"][0:1, 0:JC])
                return e.tensor_copy(k2row[0:1, 1, jc * JC:(jc + 1) * JC],
                                     ps["E"][0:1, 0:JC])
            prev = step("ve", _k2ev, after=[t_k2])
        t_k2done = prev
        if sub == 33:
            return

        eqprev = t_k2done
        for jc in range(NJC):
            def _eq(e, st, jc=jc, p=p, hA=hA, hB=hB):
                sl = slice(jc * JC, (jc + 1) * JC)
                e.matmul(ps["B"][0:64, :], cent_b[0:64, p, :], q_pair[0:64, sl],
                         start=True, stop=False)
                e.matmul(ps["B"][0:64, :], neginvrow[0:1, hA * 64:hA * 64 + 64],
                         q2row[0:1, 0, sl], start=False, stop=False)
                e.matmul(ps["B"][0:64, :], negc2i[0:1, hA * 64:hA * 64 + 64],
                         sb["ones512"][0:1, :], start=False, stop=True)
                e.matmul(ps["B"][64:128, :], cent_b[64:128, p, :], q_pair[64:128, sl],
                         start=True, stop=False, tile_position=(64, 64))
                e.matmul(ps["B"][64:128, :], neginvrow[0:1, hB * 64:hB * 64 + 64],
                         q2row[0:1, 1, sl], start=False, stop=False,
                         tile_position=(0, 64))
                return e.matmul(ps["B"][64:128, :], negc2i[0:1, hB * 64:hB * 64 + 64],
                                sb["ones512"][0:1, :], start=False, stop=True,
                                tile_position=(0, 64))
            t_eq = step("pe", _eq, after=[eqprev])
            eqprev = step("ac", lambda e, st, jc=jc: e.activation(
                eq_pair[:, jc * JC:(jc + 1) * JC], ps["B"][:, :], AF.Exp), after=[t_eq])
        t_eqmin = step("ve", lambda e, st: e.tensor_scalar(eq_pair[:, :], eq_pair[:, :],
                                                           1.0, None, ALU.min),
                       after=[eqprev])
        if sub == 34:
            return

        ekprev = t_k2done
        for tc in range(16):
            def _ek(e, st, tc=tc, p=p, hA=hA, hB=hB):
                tl = slice(tc * 128, (tc + 1) * 128)
                e.matmul(ps["C"][:, 0:64], k_pair[0:64, tl], cent_b[0:64, p, :],
                         start=True, stop=False)
                e.matmul(ps["C"][:, 0:64], k2row[0:1, 0, tl],
                         neginvrow[0:1, hA * 64:hA * 64 + 64], start=False, stop=False)
                e.matmul(ps["C"][:, 0:64], sb["ones_row"][0:1, :],
                         negc2i[0:1, hA * 64:hA * 64 + 64], start=False, stop=True)
                e.matmul(ps["D"][:, 0:64], k_pair[64:128, tl], cent_b[64:128, p, :],
                         start=True, stop=False, tile_position=(64, 0))
                e.matmul(ps["D"][:, 0:64], k2row[0:1, 1, tl],
                         neginvrow[0:1, hB * 64:hB * 64 + 64], start=False, stop=False)
                return e.matmul(ps["D"][:, 0:64], sb["ones_row"][0:1, :],
                                negc2i[0:1, hB * 64:hB * 64 + 64], start=False, stop=True)
            t_ek = step("pe", _ek, after=[ekprev, t_eqmin] if tc == 0 else [ekprev])

            def _eke(e, st, tc=tc):
                e.activation(ek_pair[:, tc, 0:64], ps["C"][:, 0:64], AF.Exp)
                return e.activation(ek_pair[:, tc, 64:128], ps["D"][:, 0:64], AF.Exp)
            ekprev = step("ac", _eke, after=[t_ek])
        t_ekmin = step("ve", lambda e, st: e.tensor_scalar(ek_pair[:, :, :],
                                                           ek_pair[:, :, :], 1.0,
                                                           None, ALU.min),
                       after=[ekprev])
        if sub == 35:
            return

        def _vagg(e, st):
            for tc in range(16):
                e.matmul(ps["B"][0:64, 0:64], ek_pair[:, tc, 0:64], v_pair[:, tc, 0:64],
                         start=(tc == 0), stop=(tc == 15))
            for tc in range(16):
                e.matmul(ps["B"][64:128, 0:64], ek_pair[:, tc, 64:128],
                         v_pair[:, tc, 64:128], start=(tc == 0), stop=(tc == 15),
                         tile_position=(0, 64))
            last = None
            for tc in range(16):
                last = e.matmul(ps["E"][0:1, 0:128], sb["ones_col"][:, :],
                                ek_pair[:, tc, :], start=(tc == 0), stop=(tc == 15))
            return last
        t_vagg = step("pe", _vagg, after=[t_ekmin, t_vdone])

        def _vev2(e, st, hA=hA, hB=hB):
            e.tensor_scalar(vagg_b[0:64, :], ps["B"][0:64, 0:64],
                            amp2T[0:64, hA:hA + 1], None, ALU.mult)
            e.tensor_scalar(vagg_b[64:128, :], ps["B"][64:128, 0:64],
                            amp2T[0:64, hB:hB + 1], None, ALU.mult)
            e.tensor_copy(ksrow[0:1, :], ps["E"][0:1, 0:128])
            return e.tensor_mul(ksrow[0:1, :], ksrow[0:1, :],
                                amp2row[0:1, hA * 64:hA * 64 + 128])
        t_vev2 = step("ve", _vev2, after=[t_vagg])

        t_kst = step("pe", lambda e, st: e.matmul(ps["C"][0:128, 0:1], ksrow[0:1, :],
                                                  sb["onescol_f"][0:1, 0:1],
                                                  start=True, stop=True),
                     after=[t_vev2])

        def _kscol(e, st):
            e.memset(ks_col[:, :], 0.0)
            e.tensor_copy(ks_col[0:64, 0:1], ps["C"][0:64, 0:1])
            return e.tensor_copy(ks_col[64:128, 1:2], ps["C"][64:128, 0:1])
        t_kscol = step("ve", _kscol, after=[t_kst])
        if sub == 36:
            return

        prev = t_kscol
        for jc in range(NJC):
            def _den(e, st, jc=jc):
                sl = slice(jc * JC, (jc + 1) * JC)
                e.matmul(ps["E"][0:1, 0:JC], ks_col[:, 0:1], eq_pair[:, sl],
                         start=True, stop=True)
                return e.matmul(ps["D"][0:1, 0:JC], ks_col[:, 1:2], eq_pair[:, sl],
                                start=True, stop=True)
            t_den = step("pe", _den, after=[prev])

            def _denev(e, st, jc=jc):
                sl = slice(jc * JC, (jc + 1) * JC)
                e.tensor_copy(denA[0:1, sl], ps["E"][0:1, 0:JC])
                return e.tensor_copy(denB[0:1, sl], ps["D"][0:1, 0:JC])
            prev = step("ve", _denev, after=[t_den])

        def _recip(e, st):
            e.tensor_scalar(denA[0:1, :], denA[0:1, :], 1e-6, None, ALU.add)
            e.tensor_scalar(denB[0:1, :], denB[0:1, :], 1e-6, None, ALU.add)
            e.reciprocal(denA[0:1, :], denA[0:1, :])
            return e.reciprocal(denB[0:1, :], denB[0:1, :])
        t_rec = step("ve", _recip, after=[prev])
        if sub == 37:
            return

        prev = t_rec
        for jc in range(NJC):
            def _num2(e, st, jc=jc):
                sl = slice(jc * JC, (jc + 1) * JC)
                e.matmul(ps["B"][0:64, :], vagg_b[0:64, :], eq_pair[0:64, sl],
                         start=True, stop=True)
                e.matmul(ps["B"][64:128, :], vagg_b[64:128, :], eq_pair[64:128, sl],
                         start=True, stop=True, tile_position=(64, 64))
                e.matmul(ps["D"][0:64, 0:JC], sb["ones_rowf"][0:1, 0:64],
                         denA[0:1, sl], start=True, stop=True)
                return e.matmul(ps["D"][64:128, 0:JC], sb["ones_rowf"][0:1, 0:64],
                                denB[0:1, sl], start=True, stop=True,
                                tile_position=(0, 64))
            t_n2 = step("pe", _num2, after=[prev])

            def _preo(e, st, jc=jc, p=p):
                e.tensor_copy(num_sb[:, :], ps["B"][:, :])
                return e.tensor_mul(pre_o[:, p, jc * JC:(jc + 1) * JC],
                                    num_sb[:, :], ps["D"][:, :])
            prev = step("ve", _preo, after=[t_n2])

    def ffn(call, sub=99):
        h = place("h", (128, NCH, T), F32, 66 * K)
        q_b = place("q_b", (128, NCH, T), BF16, 0)
        ctx_acc = place("ctx_acc", (128, NCH, JC), F32, 32 * K)
        sc8 = place("sc8", (16, 8, JC), BF16, 48 * K)
        kv_t = place("kv_t", (128, 4, JC), BF16, 56 * K)
        fpr_dc = place("fpr_dc", (128, NCH, 128), BF16, 56 * K)
        mx16 = place("mx16", (16, JC), F32, 61 * K)
        s16t = place("s16t", (16, JC), F32, 63 * K)
        prod4 = place("prod4", (128, 4, JC), BF16, int(61 * K))
        glog = place("glog", (128, 16, 8), F32, 65 * K)
        gsc1 = place("gsc1", (128, 16, 8), F32, int(65.5 * K))
        gsc2 = place("gsc2", (128, 16, 8), F32, 60 * K)
        grow = place("grow", (128, 16), F32, int(60.5 * K))
        w1h = place("w1h", (128, NCH, K), BF16, 0)
        w2h = place("w2h", (128, NCH, K), BF16, 16 * K)
        h1g = place("h1g", (128, NCH, JC), BF16, 32 * K)
        bc_sb = place("bc_sb", (128, JC), F32, 40 * K)
        tmp2 = place("tmp2", (128, JC), F32, 42 * K)
        wT = place("wT", (8, T), F32, 44 * K)
        d8_sb = place("d8_sb", (8, 8, 128), F32, 52 * K)
        rt_sb = place("rt_sb", (128, NCH, 8), BF16, 56 * K)

        t_h = emit_ln(x_sb[:, :, :], h[:, :, :], "plain", dep=[])
        t_spill = spill_x([t_h])
        if sub == 201:
            _gate_store[0] = h
            return None

        def _ldln1(e, st):
            st.dma(e.dma_start(out=sb["lnw"][:, :], in_=g["fln1w"][:, :]))
            st.dma(e.dma_start(out=sb["lnb"][:, :], in_=g["fln1b"][:, :]))
            return None
        t_l1 = step("gp", _ldln1)
        t_ln1 = emit_ln(h[:, :, :], sb["xb"][:, :, :], "affine", dep=[t_l1])
        if sub == 202:
            _gate_store[0] = h
            return None

        qkv_w = place("qkv_w", (128, NCH, 128), BF16, 56 * K)
        agv = g["agin"].ap().rearrange("(c p) t -> p c t", p=128)
        kvev = place("kvev", (128, JC), BF16, 60 * K)
        prev = t_ln1
        ag_deps = []
        for oc in range(12):
            def _lw(e, st, oc=oc):
                st.dma(e.dma_start(
                    out=qkv_w[:, :, :],
                    in_=g["fqkv"][:, oc * 128:(oc + 1) * 128]
                        .rearrange("(c q) m -> q c m", q=128)))
                return None
            t_lw = step("gp", _lw, after=[prev])

            def _mm(e, st, oc=oc):
                last = None
                for jc in range(NJC):
                    for c in range(NCH):
                        last = e.matmul(psA4[jc], qkv_w[:, c, :],
                                        sb["xb"][:, c, jc * JC:(jc + 1) * JC],
                                        start=(c == 0), stop=(c == NCH - 1))
                return last
            t_mm = step("pe", _mm, after=[t_lw])

            if oc < 8:
                prev = step("ve", lambda e, st, oc=oc: e.tensor_copy(
                    q_b[:, oc, :], ps["A"][:, :]), after=[t_mm])
            else:
                for jc in range(NJC):
                    t_ev = step("ve", lambda e, st, oc=oc, jc=jc: e.tensor_copy(
                        kvev[:, :], psA4[jc]), after=[t_mm] if jc == 0 else [prev])

                    def _ship(e, st, oc=oc, jc=jc):
                        st.dma(e.dma_start(out=agv[:, oc - 8, jc * JC:(jc + 1) * JC],
                                           in_=kvev[:, :]))
                        return None
                    t_sh = step("sy", _ship, after=[t_ev])
                    prev = t_sh
                ag_deps.append(prev)

        if sub == 203:
            _gate_store[0] = h
            return None

        def _cc(e, st):
            return e.collective_compute(
                "AllGather", ALU.bypass, replica_groups=[list(range(NCORES))],
                ins=[g["agin"].ap().opt()], outs=[g["agout"].ap().opt()])
        t_cc = step("gp", _cc, after=[prev])
        if sub == 204:
            _gate_store[0] = h
            return None

        agov = g["agout"].ap().rearrange("(t c p) u -> t p c u", p=128, c=4)
        prev_jc = t_cc
        for jc in range(NJC):
            sl = slice(jc * JC, (jc + 1) * JC)
            sc_deps = []
            prev_t = prev_jc
            for t in range(8):
                def _lkv(e, st, t=t, sl=sl):
                    st.dma(e.dma_start(out=kv_t[:, :, :], in_=agov[t, :, :, sl]))
                    return None
                t_kv = step("sy", _lkv, after=[prev_t])

                half_toks = []
                for half in range(2):
                    def _prod(e, st, half=half, sl=sl):
                        last = None
                        for i in range(4):
                            c = half * 4 + i
                            last = e.tensor_mul(prod4[:, i, :], q_b[:, c, sl],
                                                kv_t[:, 0 if c < 4 else 1, :])
                        return last
                    dp = [t_kv] if half == 0 else [half_toks[-1]]
                    t_pr = step("ve", _prod, after=dp)

                    def _smm(e, st, half=half, t=t):
                        last = None
                        for i in range(4):
                            c = half * 4 + i
                            last = e.matmul(ps["B"][0:16, :], sb["mask16b"][:, c, :],
                                            prod4[:, i, :],
                                            start=(c == 0), stop=(c == 7))
                        return last
                    t_sm = step("pe", _smm, after=[t_pr])
                    half_toks.append(t_sm)

                t_sev = step("ve", lambda e, st, t=t: e.tensor_copy(
                    sc8[:, t, :], ps["B"][0:16, :]), after=[half_toks[-1]])
                sc_deps.append(t_sev)
                prev_t = t_sev

            def _mx(e, st):
                e.tensor_max(mx16[:, :], sc8[:, 0, :], sc8[:, 1, :])
                for t in range(2, 8):
                    e.tensor_max(mx16[:, :], mx16[:, :], sc8[:, t, :])
                last = None
                for t in range(8):
                    last = e.tensor_sub(sc8[:, t, :], sc8[:, t, :], mx16[:, :])
                return last
            t_mx = step("ve", _mx, after=[sc_deps[-1]])

            t_ex = step("ac", lambda e, st: e.activation(sc8[:, :, :], sc8[:, :, :],
                                                         AF.Exp), after=[t_mx])

            def _norm(e, st):
                for t in range(8):
                    e.tensor_scalar(sc8[:, t, :], sc8[:, t, :],
                                    sb["tmask"][:, t:t + 1], None, ALU.mult)
                e.tensor_add(s16t[:, :], sc8[:, 0, :], sc8[:, 1, :])
                for t in range(2, 8):
                    e.tensor_add(s16t[:, :], s16t[:, :], sc8[:, t, :])
                st.sync(e, e.reciprocal(s16t[:, :], s16t[:, :]))
                last = None
                for t in range(8):
                    last = e.tensor_mul(sc8[:, t, :], sc8[:, t, :], s16t[:, :])
                return last
            t_nm = step("ve", _norm, after=[t_ex])

            t_z = step("ve", lambda e, st: e.memset(ctx_acc[:, :, :], 0.0), after=[t_nm])
            prev_c = t_z
            for t in range(8):
                def _lkv2(e, st, t=t, sl=sl):
                    st.dma(e.dma_start(out=kv_t[:, :, :], in_=agov[t, :, :, sl]))
                    return None
                t_kv2 = step("sy", _lkv2, after=[prev_c])
                for half in range(2):
                    def _bmm(e, st, half=half, t=t):
                        last = None
                        for i in range(4):
                            c = half * 4 + i
                            last = e.matmul(psA4[i], sb["sel16b"][:, c, :],
                                            sc8[:, t, :], start=True, stop=True)
                        return last
                    dp = [t_kv2, prev_c] if half == 0 else [prev_c]
                    t_bm = step("pe", _bmm, after=dp)

                    def _acc(e, st, half=half, t=t):
                        last = None
                        for i in range(4):
                            c = half * 4 + i
                            e.tensor_mul(sb["lnt"][:, :], psA4[i],
                                         kv_t[:, 2 + (0 if c < 4 else 1), :])
                            last = e.tensor_add(ctx_acc[:, c, :], ctx_acc[:, c, :],
                                                sb["lnt"][:, :])
                        return last
                    prev_c = step("ve", _acc, after=[t_bm])

            prev_jc = _ffn_proj(jc, prev_c, h, ctx_acc, fpr_dc)
            if sub == 205:
                _gate_store[0] = h
                return None
        if sub == 206:
            _gate_store[0] = h
            return None
        return _ffn_tail(call, h, q_b, glog, gsc1, gsc2, grow, wT, d8_sb, rt_sb,
                         w1h, w2h, h1g, bc_sb, tmp2, prev_jc, sub)

    def _ffn_proj(jc, dep, h, ctx_acc, fpr_dc):
        ctx_b8 = place("ctx_b8", (128, NCH, JC), BF16, 48 * K)

        def _cast(e, st):
            last = None
            for pc in range(NCH):
                last = e.tensor_copy(ctx_b8[:, pc, :], ctx_acc[:, pc, :])
            return last
        fpr2 = place("fpr2", (128, NCH, 128), BF16, 58 * K)
        t_cast = step("ve", _cast, after=[dep])
        hadds = [t_cast, t_cast]
        lfs = [None, None]
        pjs = [None, None]
        for dc in range(NCH):
            par = dc % 2
            buf = fpr_dc if par == 0 else fpr2
            bk = ps["B"] if par == 0 else ps["C"]

            def _lf(e, st, dc=dc, buf=buf):
                st.dma(e.dma_start(
                    out=buf[:, :, :],
                    in_=g["fproj"][:, dc * 128:(dc + 1) * 128]
                        .rearrange("(c q) m -> q c m", q=128)))
                return None
            deplf = [dep] if pjs[par] is None else [pjs[par]]
            t_lf = step("gp", _lf, after=deplf)
            lfs[par] = t_lf

            def _pj(e, st, dc=dc, buf=buf, bk=bk):
                last = None
                for pc in range(NCH):
                    last = e.matmul(bk[:, :], buf[:, pc, :], ctx_b8[:, pc, :],
                                    start=(pc == 0), stop=(pc == NCH - 1))
                return last
            dpj = [t_lf, t_cast] if hadds[par] is None else [t_lf, hadds[par]]
            t_pj = step("pe", _pj, after=dpj)
            pjs[par] = t_pj

            def _hadd(e, st, dc=dc, jc=jc, bk=bk):
                hv = h[:, dc, jc * JC:(jc + 1) * JC]
                return e.tensor_add(hv, bk[:, :], hv)
            hadds[par] = step("ve", _hadd, after=[t_pj])
        def _join(e, st):
            return e.nop()
        return step("ve", _join, after=[hadds[0], hadds[1]])

    def _ffn_tail(call, h, q_b, glog, gsc1, gsc2, grow, wT, d8_sb, rt_sb,
                  w1h, w2h, h1g, bc_sb, tmp2, dep, sub=99):
        def _ldln2(e, st):
            st.dma(e.dma_start(out=sb["lnw"][:, :], in_=g["fln2w"][:, :]))
            st.dma(e.dma_start(out=sb["lnb"][:, :], in_=g["fln2b"][:, :]))
            st.dma(e.dma_start(out=rt_sb[:, :, :],
                               in_=g["frout"].rearrange("(c p) m -> p c m", p=128)))
            st.dma(e.dma_start(out=d8_sb[:, :, :], in_=g["delta8"][:, :, :]))
            return None
        t_l2 = step("gp", _ldln2, after=[dep])
        t_ln2 = emit_ln(h[:, :, :], sb["xb"][:, :, :], "affine", dep=[t_l2])
        if sub == 207:
            _gate_store[0] = h
            return None
        if sub == 211:
            _gate_store[0] = ("bf16", sb["xb"])
            return None

        prev = t_ln2
        for js in range(16):
            def _rmm(e, st, js=js):
                last = None
                for c in range(NCH):
                    last = e.matmul(ps["B"][0:128, 0:8],
                                    sb["xb"][:, c, js * 128:(js + 1) * 128],
                                    rt_sb[:, c, :], start=(c == 0), stop=(c == NCH - 1))
                return last
            t_rm = step("pe", _rmm, after=[prev])
            prev = step("ve", lambda e, st, js=js: e.tensor_copy(
                glog[:, js, :], ps["B"][0:128, 0:8]), after=[t_rm])

        if sub == 213:
            _gate_store[0] = ("p128", glog)
            return None

        def _gate1(e, st):
            st.sync(e, e.reduce_max(grow[:, :], glog[:, :, :], axis=AX))
            g3 = grow[:, :].rearrange("p (a b) -> p a b", b=1).broadcast_to([128, 16, 8])
            return e.tensor_sub(glog[:, :, :], glog[:, :, :], g3)
        t_g1 = step("ve", _gate1, after=[prev])
        t_g2 = step("ac", lambda e, st: e.activation(glog[:, :, :], glog[:, :, :], AF.Exp),
                    after=[t_g1])

        def _gate3(e, st):
            st.sync(e, e.reduce_sum(grow[:, :], glog[:, :, :], axis=AX))
            st.sync(e, e.reciprocal(grow[:, :], grow[:, :]))
            g3 = grow[:, :].rearrange("p (a b) -> p a b", b=1).broadcast_to([128, 16, 8])
            e.tensor_mul(glog[:, :, :], glog[:, :, :], g3)
            st.sync(e, e.reduce_max(grow[:, :], glog[:, :, :], axis=AX))
            m3 = grow[:, :].rearrange("p (a b) -> p a b", b=1).broadcast_to([128, 16, 8])
            e.tensor_tensor(gsc1[:, :, :], glog[:, :, :], m3, ALU.is_equal)
            e.scalar_tensor_tensor(gsc2[:, :, :], gsc1[:, :, :], -2.0, glog[:, :, :],
                                   ALU.mult, ALU.add)
            st.sync(e, e.reduce_max(grow[:, :], gsc2[:, :, :], axis=AX))
            m3 = grow[:, :].rearrange("p (a b) -> p a b", b=1).broadcast_to([128, 16, 8])
            e.tensor_tensor(gsc1[:, :, :], glog[:, :, :], m3, ALU.is_ge)
            return e.tensor_mul(glog[:, :, :], glog[:, :, :], gsc1[:, :, :])
        t_g3 = step("ve", _gate3, after=[t_g2])
        if sub == 214:
            _gate_store[0] = ("p128", glog)
            return None
        if sub == 208:
            _gate_store[0] = h
            return None

        prev = t_g3
        for js in range(16):
            t_tr = step("pe", lambda e, st, js=js: e.transpose(
                ps["B"][0:8, 0:128], glog[:, js, :], sb["identf"][:, :]), after=[prev])
            prev = step("ve", lambda e, st, js=js: e.tensor_copy(
                wT[0:8, js * 128:(js + 1) * 128], ps["B"][0:8, 0:128]), after=[t_tr])
        t_wT = prev
        if sub == 209:
            _gate_store[0] = h
            return None
        if sub == 212:
            _gate_store[0] = ("wt", wT)
            return None

        fw1v = g["fw1"].rearrange("e (c p) f -> e p c f", p=128)
        fw2v = g["fw2"].rearrange("e (c p) f -> e p c f", p=128)
        prev_acc = t_wT
        for ei in range(8):
            for fh in range(2):
                def _lw(e, st, ei=ei, fh=fh):
                    st.dma(e.dma_start(out=w1h[:, :, :],
                                       in_=fw1v[ei, :, :, fh * K:(fh + 1) * K]))
                    st.dma(e.dma_start(out=w2h[:, :, :],
                                       in_=fw2v[ei, :, fh * NCH:(fh + 1) * NCH, :]))
                    return None
                t_lwm = step("gp", _lw, after=[prev_acc])

                for jc in range(NJC):
                    sl = slice(jc * JC, (jc + 1) * JC)

                    def _h1a(e, st, sl=sl):
                        last = None
                        for fc in range(4):
                            for c in range(NCH):
                                last = e.matmul(psA4[fc], w1h[:, c, fc * 128:(fc + 1) * 128],
                                                sb["xb"][:, c, sl],
                                                start=(c == 0), stop=(c == NCH - 1))
                        return last
                    t_h1a = step("pe", _h1a, after=[t_lwm] if jc == 0 else [t_lwm, prev_acc])

                    def _gel1(e, st, jc=jc):
                        last = None
                        for fc in range(4):
                            last = e.activation(h1g[:, fc, :], psA4[fc], AF.Gelu_apprx_tanh)
                        return last
                    t_g1m = step("ac", _gel1, after=[t_h1a])

                    def _h1b(e, st, sl=sl):
                        last = None
                        for fc in range(4):
                            for c in range(NCH):
                                last = e.matmul(psA4[fc],
                                                w1h[:, c, (4 + fc) * 128:(5 + fc) * 128],
                                                sb["xb"][:, c, sl],
                                                start=(c == 0), stop=(c == NCH - 1))
                        return last
                    t_h1b = step("pe", _h1b, after=[t_g1m])

                    def _gel2(e, st, jc=jc):
                        last = None
                        for fc in range(4):
                            last = e.activation(h1g[:, 4 + fc, :], psA4[fc],
                                                AF.Gelu_apprx_tanh)
                        return last
                    t_g2m = step("ac", _gel2, after=[t_h1b])

                    def _bcm(e, st, ei=ei, sl=sl):
                        return e.matmul(ps["E"][:, :], d8_sb[:, ei, :], wT[0:8, sl],
                                        start=True, stop=True)
                    t_bcm = step("pe", _bcm, after=[t_g2m])
                    t_bce = step("ve", lambda e, st: e.tensor_copy(bc_sb[:, :], ps["E"][:, :]),
                                 after=[t_bcm])

                    accs = [t_bce, t_bce]
                    for dc2 in range(4):
                        bk0 = ps["B"] if dc2 % 2 == 0 else ps["D"]
                        bk1 = ps["C"] if dc2 % 2 == 0 else ps["E"]

                        def _eo(e, st, dc2=dc2, bk0=bk0, bk1=bk1):
                            for c in range(NCH):
                                e.matmul(bk0[:, :], w2h[:, c, (2 * dc2) * 128:(2 * dc2 + 1) * 128],
                                         h1g[:, c, :], start=(c == 0), stop=(c == NCH - 1))
                            last = None
                            for c in range(NCH):
                                last = e.matmul(bk1[:, :],
                                                w2h[:, c, (2 * dc2 + 1) * 128:(2 * dc2 + 2) * 128],
                                                h1g[:, c, :], start=(c == 0), stop=(c == NCH - 1))
                            return last
                        t_eo = step("pe", _eo, after=[accs[dc2 % 2]])

                        def _accm(e, st, dc2=dc2, jc=jc, bk0=bk0, bk1=bk1):
                            e.tensor_mul(tmp2[:, :], bk0[:, :], bc_sb[:, :])
                            hv = h[:, 2 * dc2, jc * JC:(jc + 1) * JC]
                            e.tensor_add(hv, hv, tmp2[:, :])
                            e.tensor_mul(tmp2[:, :], bk1[:, :], bc_sb[:, :])
                            hv = h[:, 2 * dc2 + 1, jc * JC:(jc + 1) * JC]
                            return e.tensor_add(hv, hv, tmp2[:, :])
                        accs[dc2 % 2] = step("ve", _accm, after=[t_eo])
                    prev_acc = accs[1]
            if sub == 210:
                _gate_store[0] = h
                break

        t_rx = reload_x([prev_acc])
        prev = prev_acc
        for dc in range(NCH):
            def _xadd(e, st, dc=dc):
                last = None
                for jc in range(NJC):
                    xv = x_sb[:, dc, jc * JC:(jc + 1) * JC]
                    last = e.tensor_add(xv, xv, h[:, dc, jc * JC:(jc + 1) * JC])
                return last
            prev = step("ve", _xadd, after=[t_rx, prev])
        return prev

    def attn_b():
        tmp_h = place("tmph", (128, NCH, T), F32, 66 * K)
        pe_bf = place("pe_bf", (128, NCH, PPAD), BF16, 0)
        pbuf = place("pbuf", (128, PPAD), BF16, 72 * K)
        p_pair = place("p_pair", (128, PTRIM), BF16, 0)
        q_u = place("q_u", (128, T), BF16, 9 * K)
        q_v = place("q_v", (128, T), BF16, 13 * K)
        kb_pair = place("kb_pair", (128, T), BF16, 17 * K)
        vb_pair = place("vb_pair", (128, 16, 128), BF16, 21 * K)
        S_A = place("S_A", (128, T), F32, 25 * K)
        S_B = place("S_B", (128, T), F32, 33 * K)
        P_A = place("P_A", (128, T), BF16, 41 * K)
        P_B = place("P_B", (128, T), BF16, 45 * K)
        PT_A = place("PT_A", (128, 2, T), BF16, 49 * K)
        PT_B = place("PT_B", (128, 2, T), BF16, 57 * K)
        posbuf = place("posbuf", (128, 2, JC), BF16, 65 * K)
        wq_sl = place("wq_sl", (128, NCH, 128), BF16, 67 * K)
        wk_sl = place("wk_sl", (128, NCH, 128), BF16, 69 * K)
        wv_sl = place("wv_sl", (128, NCH, 128), BF16, 71 * K)
        bvrow = place("bvrow", (1, D), F32, 73 * K)
        ctx_all = place("ctx_all", (128, NCH, T), BF16, 77 * K)
        bwo_sb = place("bwo_sb", (128, NCH, K), BF16, 110 * K)

        t_ln = emit_ln(x_sb[:, :, :], tmp_h[:, :, :], "plain", dep=[])

        def _ldg(e, st):
            st.dma(e.dma_start(out=sb["lnw"][:, :], in_=g["bg"][:, :]))
            st.dma(e.dma_start(out=sb["lnb"][:, :], in_=g["bb"][:, :]))
            st.dma(e.dma_start(out=sb["bias1"][:, :], in_=g["bbq"][:, :]))
            st.dma(e.dma_start(out=sb["bias2"][:, :], in_=g["bbk"][:, :]))
            return None
        t_lg = step("gp", _ldg, after=[t_ln])
        t_ln2 = emit_ln(tmp_h[:, :, :], sb["xb"][:, :, :], "affine", dep=[t_lg])
        t_spill = spill_x([t_ln2])

        ubias = place("ubias", (128, 8), F32, 109 * K)
        vbias = place("vbias", (128, 8), F32, int(109.25 * K))
        kbias = place("kbias", (128, 8), F32, int(109.5 * K))

        def _ldpe(e, st):
            st.dma(e.dma_start(out=pe_bf[:, :, :],
                               in_=g["pet"].rearrange("(c p) t -> p c t", p=128)))
            return None
        t_lpe = step("gp", _ldpe, after=[t_spill])

        wp_sl = place("wp_sl", (128, NCH, 128), BF16, 82 * K)
        prev = t_lpe
        for p in range(8):
            def _lbwp(e, st, p=p):
                st.dma(e.dma_start(
                    out=wp_sl[:, :, :],
                    in_=g["bwp"][:, p * 128:(p + 1) * 128].rearrange("(c q) m -> q c m", q=128)))
                return None
            t_lw = step("gp", _lbwp, after=[prev])
            evs = []
            for wch in range(9):
                def _pmm(e, st, wch=wch):
                    bank = ps["B"] if wch % 2 == 0 else ps["C"]
                    last = None
                    for c in range(NCH):
                        last = e.matmul(bank[:, :], wp_sl[:, c, :],
                                        pe_bf[:, c, wch * JC:(wch + 1) * JC],
                                        start=(c == 0), stop=(c == NCH - 1))
                    return last
                dpp = [t_lw] if wch < 2 else [t_lw, evs[wch - 2]]
                t_pm = step("pe", _pmm, after=dpp)
                t_pe = step("ac", lambda e, st, wch=wch: e.activation(
                    pbuf[:, wch * JC:(wch + 1) * JC],
                    (ps["B"] if wch % 2 == 0 else ps["C"])[:, :], AF.Identity),
                    after=[t_pm])
                evs.append(t_pe)

            def _shipp(e, st, p=p):
                st.dma(e.dma_start(out=g["pd"][p, :, :], in_=pbuf[:, 0:PTRIM]))
                return None
            prev = step("sy", _shipp, after=[evs[-1]])
        bar()

        def _ldg2(e, st):
            st.dma(e.dma_start(out=ubias[:, :], in_=g["bu"][:, :]))
            st.dma(e.dma_start(out=vbias[:, :], in_=g["bv"][:, :]))
            st.dma(e.dma_start(out=bvrow[0:1, :], in_=g["bbv_row"][:, :]))
            return None
        t_lg2 = step("gp", _ldg2)

        def _bias(e, st):
            e.tensor_add(ubias[:, :], ubias[:, :], sb["bias1"][:, :])
            e.tensor_add(vbias[:, :], vbias[:, :], sb["bias1"][:, :])
            return e.tensor_copy(kbias[:, :], sb["bias2"][:, :])
        t_bias = step("ve", _bias, after=[t_lg2])
        bar()

        for p in range(8):
            _attn_b_pair(p, p_pair, q_u, q_v, kb_pair, vb_pair, S_A, S_B,
                         P_A, P_B, PT_A, PT_B, posbuf, wq_sl, wk_sl, wv_sl,
                         bvrow, ubias, vbias, kbias, ctx_all)
            bar()

        t_rx = reload_x([])

        def _ldwo(e, st):
            st.dma(e.dma_start(out=bwo_sb[:, :, :],
                               in_=g["bwo"].rearrange("(c p) m -> p c m", p=128)))
            st.dma(e.dma_start(out=sb["bias1"][:, :], in_=g["bbout"][:, :]))
            return None
        t_wo = step("gp", _ldwo)

        adds = [None, None]
        k = 0
        for jc in range(NJC):
            for dc in range(NCH):
                bk = ps["B"] if k % 2 == 0 else ps["C"]

                def _mm(e, st, jc=jc, dc=dc, bk=bk):
                    last = None
                    for pc in range(NCH):
                        last = e.matmul(bk[:, :], bwo_sb[:, pc, dc * 128:(dc + 1) * 128],
                                        ctx_all[:, pc, jc * JC:(jc + 1) * JC],
                                        start=(pc == 0), stop=(pc == NCH - 1))
                    return last
                depm = [t_wo] if adds[k % 2] is None else [t_wo, adds[k % 2]]
                t_mm = step("pe", _mm, after=depm)

                def _add(e, st, jc=jc, dc=dc, bk=bk):
                    xv = x_sb[:, dc, jc * JC:(jc + 1) * JC]
                    return e.scalar_tensor_tensor(xv, bk[:, :],
                                                  sb["bias1"][:, dc:dc + 1], xv,
                                                  ALU.add, ALU.add)
                adds[k % 2] = step("ve", _add, after=[t_mm, t_rx])
                k += 1
        bar()

    def _attn_b_pair(p, p_pair, q_u, q_v, kb_pair, vb_pair, S_A, S_B,
                     P_A, P_B, PT_A, PT_B, posbuf, wq_sl, wk_sl, wv_sl,
                     bvrow, ubias, vbias, kbias, ctx_all):
        hA, hB = 2 * p, 2 * p + 1

        def _ldw(e, st, p=p):
            st.dma(e.dma_start(out=p_pair[:, :], in_=g["pd"][p, :, :]))
            for dst, wsrc in ((wq_sl, "bwq"), (wk_sl, "bwk"), (wv_sl, "bwv")):
                st.dma(e.dma_start(
                    out=dst[:, :, :],
                    in_=g[wsrc][:, p * 128:(p + 1) * 128]
                        .rearrange("(c q) m -> q c m", q=128)))
            return None
        t_ldw = step("gp", _ldw)

        def _qmm(e, st):
            last = None
            for jc in range(NJC):
                for c in range(NCH):
                    last = e.matmul(psA4[jc], wq_sl[:, c, :],
                                    sb["xb"][:, c, jc * JC:(jc + 1) * JC],
                                    start=(c == 0), stop=(c == NCH - 1))
            return last
        t_qmm = step("pe", _qmm, after=[t_ldw])

        def _qev(e, st, p=p):
            e.activation(q_u[:, :], ps["A"][:, :], AF.Identity, bias=ubias[:, p:p + 1])
            return e.activation(q_v[:, :], ps["A"][:, :], AF.Identity,
                                bias=vbias[:, p:p + 1])
        t_qev = step("ac", _qev, after=[t_qmm])

        def _kmm(e, st):
            last = None
            for jc in range(NJC):
                for c in range(NCH):
                    last = e.matmul(psA4[jc], wk_sl[:, c, :],
                                    sb["xb"][:, c, jc * JC:(jc + 1) * JC],
                                    start=(c == 0), stop=(c == NCH - 1))
            return last
        t_kmm = step("pe", _kmm, after=[t_qev])
        t_kev = step("ac", lambda e, st, p=p: e.activation(
            kb_pair[:, :], ps["A"][:, :], AF.Identity, bias=kbias[:, p:p + 1]),
            after=[t_kmm])

        vprev = t_kev
        for tc2 in range(8):
            def _vmm(e, st, tc2=tc2, p=p):
                last = None
                for i, tt in enumerate((2 * tc2, 2 * tc2 + 1)):
                    bank = ps["B"] if i == 0 else ps["C"]
                    for c in range(NCH):
                        e.matmul(bank[:, 0:128], sb["xb"][:, c, tt * 128:(tt + 1) * 128],
                                 wv_sl[:, c, :], start=(c == 0), stop=False)
                    last = e.matmul(bank[:, 0:128], sb["ones_rowf"][0:1, :],
                                    bvrow[0:1, p * 128:(p + 1) * 128],
                                    start=False, stop=True)
                return last
            t_vmm = step("pe", _vmm, after=[vprev])

            def _vev(e, st, tc2=tc2):
                e.tensor_copy(vb_pair[:, 2 * tc2, :], ps["B"][:, 0:128])
                return e.tensor_copy(vb_pair[:, 2 * tc2 + 1, :], ps["C"][:, 0:128])
            vprev = step("ve", _vev, after=[t_vmm])
        t_vdone = vprev

        pprev = t_vdone
        for qt in range(16):
            w0 = 128 * (15 - qt)
            qsl = slice(qt * 128, (qt + 1) * 128)

            def _cmm(e, st, qt=qt, qsl=qsl):
                banksB = (ps["B"], ps["C"], ps["D"], ps["E"])
                last = None
                for ss in range(4):
                    e.matmul(psA4[ss], q_u[0:64, qsl],
                             kb_pair[0:64, ss * JC:(ss + 1) * JC], start=True, stop=True)
                    last = e.matmul(banksB[ss][:, :], q_u[64:128, qsl],
                                    kb_pair[64:128, ss * JC:(ss + 1) * JC],
                                    start=True, stop=True, tile_position=(64, 0))
                return last
            t_cm = step("pe", _cmm, after=[pprev])

            def _cevA(e, st, qt=qt):
                last = None
                for ss in range(4):
                    last = e.activation(S_A[:, ss * JC:(ss + 1) * JC], psA4[ss], AF.Identity)
                return last
            t_ca = step("ac", _cevA, after=[t_cm])

            def _cevB(e, st, qt=qt):
                last = None
                for ss, bank in enumerate((ps["B"], ps["C"], ps["D"], ps["E"])):
                    last = e.tensor_copy(S_B[:, ss * JC:(ss + 1) * JC], bank[:, :])
                return last
            t_cb = step("ve", _cevB, after=[t_cm])

            posev = []
            for w in range(5):
                def _pmm(e, st, w=w, w0=w0, qsl=qsl):
                    e.matmul(ps["B"][:, :], q_v[0:64, qsl],
                             p_pair[0:64, w0 + w * JC:w0 + (w + 1) * JC],
                             start=True, stop=True)
                    return e.matmul(ps["C"][:, :], q_v[64:128, qsl],
                                    p_pair[64:128, w0 + w * JC:w0 + (w + 1) * JC],
                                    start=True, stop=True, tile_position=(64, 0))
                dpp = [t_cb] if w == 0 else [posev[-1]]
                t_pm = step("pe", _pmm, after=dpp)

                def _pev(e, st, w=w):
                    e.tensor_copy(posbuf[:, 0, :], ps["B"][:, :])
                    return e.tensor_copy(posbuf[:, 1, :], ps["C"][:, :])
                t_pv = step("ve", _pev, after=[t_pm])

                def _pship(e, st, w=w, qt=qt, hA=hA, hB=hB):
                    st.dma(e.dma_start(out=g["ps_scr"][hA, qt, :, w * JC:(w + 1) * JC],
                                       in_=posbuf[:, 0, :]))
                    st.dma(e.dma_start(out=g["ps_scr"][hB, qt, :, w * JC:(w + 1) * JC],
                                       in_=posbuf[:, 1, :]))
                    return None
                t_ps = step("sy", _pship, after=[t_pv])
                posev.append(t_ps)

            pst = g["ps_scr"].tensor

            def _shear(e, st, qt=qt, hA=hA, hB=hB):
                baseA = ((hA * 16 + qt) * 128) * PW + 127
                baseB = ((hB * 16 + qt) * 128) * PW + 127
                st.dma(e.dma_start(out=S_A[:, :], in_=AP(pst, baseA, [[PW - 1, 128], [1, T]]),
                                   accum_op=ALU.add))
                st.dma(e.dma_start(out=S_B[:, :], in_=AP(pst, baseB, [[PW - 1, 128], [1, T]]),
                                   accum_op=ALU.add))
                return None
            t_sh = step("gp", _shear, after=[posev[-1], t_ca, t_cb])

            def _sm1(e, st):
                st.sync(e, e.reduce_max(sb["nmcol"][:, :], S_A[:, :], axis=AX, negate=True))
                e.tensor_scalar(sb["nmcol"][:, :], sb["nmcol"][:, :], 0.125, None, ALU.mult)
                st.sync(e, e.reduce_max(sb["sumcol"][:, :], S_B[:, :], axis=AX, negate=True))
                return e.tensor_scalar(sb["sumcol"][:, :], sb["sumcol"][:, :], 0.125,
                                       None, ALU.mult)
            t_s1 = step("ve", _sm1, after=[t_sh])

            def _sm2(e, st):
                e.activation(P_A[:, :], S_A[:, :], AF.Exp, bias=sb["nmcol"][:, :],
                             scale=0.125, accum_out=sb["accA"][:, :])
                return e.activation(P_B[:, :], S_B[:, :], AF.Exp, bias=sb["sumcol"][:, :],
                                    scale=0.125, accum_out=sb["accB"][:, :])
            t_s2 = step("ac", _sm2, after=[t_s1])

            def _sm3(e, st):
                e.reciprocal(sb["accA"][:, :], sb["accA"][:, :])
                st.sync(e, e.reciprocal(sb["accB"][:, :], sb["accB"][:, :]))
                e.tensor_scalar(P_A[:, :], P_A[:, :], sb["accA"][:, :], None, ALU.mult)
                return e.tensor_scalar(P_B[:, :], P_B[:, :], sb["accB"][:, :],
                                       None, ALU.mult)
            t_s3 = step("ve", _sm3, after=[t_s2])

            def _pship2(e, st, qt=qt, hA=hA, hB=hB):
                st.dma(e.dma_start(out=g["p_scr"][hA, qt * 128:(qt + 1) * 128, :],
                                   in_=P_A[:, :]))
                st.dma(e.dma_start(out=g["p_scr"][hB, qt * 128:(qt + 1) * 128, :],
                                   in_=P_B[:, :]))
                return None
            pprev = step("sy", _pship2, after=[t_s3])

        av_prev = pprev
        ld_toks = []
        for stp in range(16):
            par = stp % 2

            def _ldpt(e, st, stp=stp, par=par, hA=hA, hB=hB):
                st.dma(e.dma_start_transpose(
                    PT_A[:, par, :], g["p_scr"][hA, :, stp * 128:(stp + 1) * 128]))
                st.dma(e.dma_start_transpose(
                    PT_B[:, par, :], g["p_scr"][hB, :, stp * 128:(stp + 1) * 128]))
                return None
            dpl = [av_prev] if stp < 2 else [av_prev, ld_toks[stp - 2][1]]
            t_lp = step("ac", _ldpt, after=dpl)

            def _avmm(e, st, stp=stp, par=par):
                banksB = (ps["B"], ps["C"], ps["D"], ps["E"])
                last = None
                for qc in range(4):
                    e.matmul(ps["A"][0:64, qc * JC:(qc + 1) * JC],
                             vb_pair[:, stp, 0:64], PT_A[:, par, qc * JC:(qc + 1) * JC],
                             start=(stp == 0), stop=(stp == 15))
                    last = e.matmul(banksB[qc][64:128, :],
                                    vb_pair[:, stp, 64:128],
                                    PT_B[:, par, qc * JC:(qc + 1) * JC],
                                    start=(stp == 0), stop=(stp == 15),
                                    tile_position=(0, 64))
                return last
            t_av = step("pe", _avmm, after=[t_lp] if stp == 0 else [t_lp, ld_toks[-1][1]])
            ld_toks.append((t_lp, t_av))

        def _ctxev(e, st, p=p):
            banksB = (ps["B"], ps["C"], ps["D"], ps["E"])
            e.tensor_copy(ctx_all[0:64, p, :], ps["A"][0:64, :])
            last = None
            for qc in range(4):
                last = e.tensor_copy(ctx_all[64:128, p, qc * JC:(qc + 1) * JC],
                                     banksB[qc][64:128, :])
            return last
        step("ve", _ctxev, after=[ld_toks[-1][1]])

    bar()
    attn_a()
    if sub < 99 and stage == 1:
        pass
    if stage >= 2:
        _r = ffn(0, sub)
        if _r is None:
            bar()
            src = _gate_store[0] if _gate_store[0] is not None else x_sb
            if isinstance(src, tuple):
                kindg, tile = src
                if kindg == "bf16":
                    def _gstore(e, st, tile=tile):
                        st.dma(e.dma_start(out=y_d.rearrange("(c p) t -> p c t", p=128),
                                           in_=tile[:, :, :]))
                        return None
                    tg = step("gp", _gstore)
                elif kindg == "p128":
                    def _gstore(e, st, tile=tile):
                        st.dma(e.dma_start(
                            out=y_d[0:128, 0:128],
                            in_=tile[:, :, :].rearrange("p a b -> p (a b)")))
                        return None
                    tg = step("sy", _gstore)
                else:
                    def _gstore(e, st, tile=tile):
                        st.dma(e.dma_start(out=y_d[0:8, :], in_=tile[0:8, :]))
                        return None
                    tg = step("sy", _gstore)
            else:
                def _gstore(e, st, src=src):
                    st.dma(e.dma_start(out=y_d.rearrange("(c p) t -> p c t", p=128),
                                       in_=src[:, :, :]))
                    return None
                tg = step("sy", _gstore)
            step("sy", lambda e, st: e.nop(), after=[tg])
            sch.flush()
            return nc
        bar()
    if stage >= 3:
        attn_b()
    if stage >= 4:
        ffn(1)
        bar()

    def _store(e, st):
        st.dma(e.dma_start(out=y_d.rearrange("(c p) t -> p c t", p=128),
                           in_=x_sb[:, :, :]))
        return None
    t_out = step("sy", _store)
    step("sy", lambda e, st: e.nop(), after=[t_out])
    sch.flush()
    return nc



def kernel(**inputs):
    shared = _host_layout(inputs)
    nc = build(4)
    in_maps = [_core_inputs(shared, inputs, c) for c in range(NCORES)]
    res = run_bass_kernel_spmd(nc, in_maps, core_ids=list(range(NCORES)))
    out = np.zeros((NCORES, T, D), np.float32)
    for c in range(NCORES):
        out[c] = np.asarray(res.results[c]["y"], np.float32).T
    return out


# version 03 (final)
# speedup vs baseline: 1.1300x; 1.0707x over previous
import math
import numpy as np

import concourse.bass as bass
import concourse.mybir as mybir
from concourse.bass_utils import run_bass_kernel_spmd
from concourse.bass_types import AP

F32 = mybir.dt.float32
BF16 = mybir.dt.bfloat16
AF = mybir.ActivationFunctionType
ALU = mybir.AluOpType
AX = mybir.AxisListType.X

NCORES = 8
D = 1024
T = 2048
NH, DH = 16, 64
JC = 512
NJC = T // JC
NCH = D // 128
FF = 2048
PW = 2560
PPAD = 4608
PTRIM = 4480

HPERM = []
for _c in range(8):
    HPERM += [_c, _c + 8]
GORDER = [0, 2, 1, 3]

ENGS = ("sy", "gp", "pe", "ac", "ve")
K = 1024



def _host_layout(inputs):
    f32 = lambda a: np.ascontiguousarray(np.asarray(a), dtype=np.float32)
    col8 = lambda a: f32(np.asarray(a, np.float32).reshape(8, 128).T)
    w = {}
    w["ident"] = np.eye(128, dtype=np.float32)

    for nm, key in (("W_q", "aq"), ("W_k", "ak"), ("W_v", "av"), ("W_o", "ao")):
        w[key] = f32(inputs[f"L0_attn_a_{nm}"])
    cent = np.asarray(inputs["L0_attn_a_centers"], np.float32)
    cp = np.zeros((128, 8, 64), np.float32)
    for p in range(8):
        cp[0:64, p, :] = cent[2 * p].T
        cp[64:128, p, :] = cent[2 * p + 1].T
    w["acent"] = f32(cp)
    w["als"] = f32(inputs["L0_attn_a_log_scales"])
    w["aamp"] = f32(inputs["L0_attn_a_amplitudes"])

    qkv = np.asarray(inputs["L0_ffn_qkv_weight"], np.float32)
    qw = qkv[:NH * DH].reshape(NH, DH, D)[HPERM].reshape(NH * DH, D)
    kw = qkv[NH * DH:(NH + 4) * DH].reshape(4, DH, D)[GORDER].reshape(4 * DH, D)
    vw = qkv[(NH + 4) * DH:].reshape(4, DH, D)[GORDER].reshape(4 * DH, D)
    w["fqkv"] = f32(np.concatenate([qw, kw, vw], 0).T)
    pj = np.asarray(inputs["L0_ffn_proj_weight"], np.float32)
    w["fproj"] = f32(pj.T.reshape(NH, DH, D)[HPERM].reshape(D, D))
    w["frout"] = f32(np.asarray(inputs["L0_ffn_router_weight"], np.float32).T)
    w["fw1"] = f32(inputs["L0_ffn_moe_w1"])
    w["fw2"] = f32(inputs["L0_ffn_moe_w2"])
    w["fln1w"] = col8(inputs["L0_ffn_ln1_weight"])
    w["fln1b"] = col8(inputs["L0_ffn_ln1_bias"])
    w["fln2w"] = col8(inputs["L0_ffn_ln2_weight"])
    w["fln2b"] = col8(inputs["L0_ffn_ln2_bias"])

    for nm, key in (("Wq", "bwq"), ("Wk", "bwk"), ("Wv", "bwv"),
                    ("Wout", "bwo"), ("Wpos", "bwp")):
        w[key] = f32(np.asarray(inputs[f"L0_attn_b_{nm}"], np.float32).T)
    w["bbq"] = col8(inputs["L0_attn_b_bq"])
    w["bbk"] = col8(inputs["L0_attn_b_bk"])
    w["bbout"] = col8(inputs["L0_attn_b_bout"])
    w["bg"] = col8(inputs["L0_attn_b_ln_gamma"])
    w["bb"] = col8(inputs["L0_attn_b_ln_beta"])
    w["bbv_row"] = f32(np.asarray(inputs["L0_attn_b_bv"], np.float32).reshape(1, D))
    w["bu"] = col8(np.asarray(inputs["L0_attn_b_u_bias"], np.float32).reshape(D))
    w["bv"] = col8(np.asarray(inputs["L0_attn_b_v_bias"], np.float32).reshape(D))

    pos = np.arange(T, dtype=np.float64)[:, None]
    div = np.exp(np.arange(0, D, 2, dtype=np.float64) * (-math.log(10000.0) / D))
    ang = pos * div
    pe_pos = np.stack([np.sin(ang), np.cos(ang)], -1).reshape(T, D)
    pe_neg = np.stack([np.sin(-ang), np.cos(-ang)], -1).reshape(T, D)
    pe = np.concatenate([pe_pos[::-1], pe_neg[1:]], 0)
    pet = np.zeros((D, PPAD), np.float64)
    pet[:, :2 * T - 1] = pe.T
    w["pet"] = f32(pet)

    m16 = np.zeros((128, 8, 16), np.float32)
    for c in range(8):
        m16[0:64, c, 2 * c] = 0.125
        m16[64:128, c, 2 * c + 1] = 0.125
    w["mask16"] = f32(m16)
    s16 = np.zeros((16, 8, 128), np.float32)
    for c in range(8):
        s16[2 * c, c, 0:64] = 1.0
        s16[2 * c + 1, c, 64:128] = 1.0
    w["sel16"] = f32(s16)
    d8 = np.zeros((8, 8, 128), np.float32)
    for e in range(8):
        d8[e, e, :] = 1.0
    w["delta8"] = f32(d8)
    return w


def _core_inputs(shared, inputs, core):
    d = dict(shared)
    x = np.asarray(inputs["L0_attn_a_x"], np.float32)[core]
    d["x"] = np.ascontiguousarray(x.T)
    tm = np.zeros((16, 8), np.float32)
    tm[:, :core + 1] = 1.0
    d["tmask8"] = tm
    return d



class _Step:
    __slots__ = ("eng", "fn", "after", "tgt", "kind")

    def __init__(self, eng, fn, after, kind="op"):
        self.eng = eng
        self.fn = fn
        self.after = list(after)
        self.tgt = None
        self.kind = kind


class _StepCtx:

    def __init__(self, sched, eng):
        self.sched = sched
        self.eng = eng
        self.ndma = 0

    def dma(self, inst):
        self.sched.dcount[self.eng] += 16
        inst.then_inc(self.sched.dsem[self.eng], 16)
        self.ndma += 1
        return inst

    def sync(self, e, inst):
        self.sched.count[self.eng] += 1
        inst.then_inc(self.sched.sem[self.eng], 1)
        e.wait_ge(self.sched.sem[self.eng], self.sched.count[self.eng])
        return inst


class Sched:
    def __init__(self, nc):
        self.nc = nc
        self.sem = {e: nc.alloc_semaphore("ec_" + e) for e in ENGS}
        self.dsem = {e: nc.alloc_semaphore("ed_" + e) for e in ENGS}
        self.count = {e: 0 for e in ENGS}
        self.dcount = {e: 0 for e in ENGS}
        self.pending = []

    def step(self, eng, fn, after=(), kind="op"):
        st = _Step(eng, fn, after, kind)
        self.pending.append(st)
        return st

    def barrier(self):
        self.flush()
        snapc = dict(self.count)
        snapd = dict(self.dcount)
        toks = []
        for e in ENGS:
            waits = []
            for f in ENGS:
                if f != e:
                    waits.append((self.sem[f], snapc[f]))
                    waits.append((self.dsem[f], snapd[f]))
            waits.append((self.dsem[e], snapd[e]))
            toks.append(self.step(e, None, after=waits, kind="bar"))
        self.flush()
        return toks

    def flush(self):
        if not self.pending:
            return
        steps = self.pending
        self.pending = []
        with self.nc.Block() as blk:
            bodies = {"sy": blk.sync, "gp": blk.gpsimd, "pe": blk.tensor,
                      "ac": blk.scalar, "ve": blk.vector}
            i = 0
            while i < len(steps):
                j = i
                eng = steps[i].eng
                while j < len(steps) and steps[j].eng == eng:
                    j += 1
                group = steps[i:j]

                def run(e, group=group):
                    for st in group:
                        for a in st.after:
                            if isinstance(a, tuple):
                                sh, val = a
                            else:
                                assert a.tgt is not None, "dep on later step"
                                sh, val = a.tgt
                            if val > 0:
                                e.wait_ge(sh, val)
                        ctx = _StepCtx(self, st.eng)
                        if st.kind == "bar":
                            last = e.nop()
                        else:
                            last = st.fn(e, ctx)
                        if last is not None:
                            assert ctx.ndma == 0, "step mixes dma and compute"
                            self.count[st.eng] += 1
                            last.then_inc(self.sem[st.eng], 1)
                            st.tgt = (self.sem[st.eng], self.count[st.eng])
                        elif ctx.ndma > 0:
                            st.tgt = (self.dsem[st.eng], self.dcount[st.eng])
                        else:
                            self.count[st.eng] += 1
                            e.sem_inc(self.sem[st.eng], 1)
                            st.tgt = (self.sem[st.eng], self.count[st.eng])
                bodies[eng](run)
                i = j



def build(stage=4, sub=99):
    nc = bass.Bass("TRN2", target_bir_lowering=False, debug=False,
                   detect_race_conditions=False)
    sch = Sched(nc)
    g = {}
    sb = {}
    ps = {}

    for name, shape in (
        ("x", (D, T)), ("ident", (128, 128)),
        ("aq", (D, D)), ("ak", (D, D)), ("av", (D, D)), ("ao", (D, D)),
        ("acent", (128, 8, 64)), ("als", (16, 64)), ("aamp", (16, 64)),
        ("fqkv", (D, 1536)), ("fproj", (D, D)), ("frout", (D, 8)),
        ("fw1", (8, D, FF)), ("fw2", (8, FF, D)),
        ("fln1w", (128, 8)), ("fln1b", (128, 8)),
        ("fln2w", (128, 8)), ("fln2b", (128, 8)),
        ("bwq", (D, D)), ("bwk", (D, D)), ("bwv", (D, D)), ("bwo", (D, D)),
        ("bwp", (D, D)),
        ("bbq", (128, 8)), ("bbk", (128, 8)), ("bbout", (128, 8)),
        ("bg", (128, 8)), ("bb", (128, 8)), ("bbv_row", (1, D)),
        ("bu", (128, 8)), ("bv", (128, 8)),
        ("pet", (D, PPAD)),
        ("mask16", (128, 8, 16)), ("sel16", (16, 8, 128)),
        ("delta8", (8, 8, 128)), ("tmask8", (16, 8)),
    ):
        g[name] = nc.dram_tensor(name, list(shape), F32, kind="ExternalInput").ap()
    y_d = nc.dram_tensor("y", [D, T], F32, kind="ExternalOutput").ap()

    g["xspill"] = nc.dram_tensor("xspill", [D, T], F32, kind="Internal").ap()
    g["ps_scr"] = nc.dram_tensor("ps_scr", [16, 16, 128, PW], BF16, kind="Internal").ap()
    g["p_scr"] = nc.dram_tensor("p_scr", [16, T, T], BF16, kind="Internal").ap()
    g["pd"] = nc.dram_tensor("pd", [8, 128, PTRIM], BF16, kind="Internal").ap()
    g["agin"] = nc.dram_tensor("agin", [512, T], BF16, kind="Internal")
    g["agout"] = nc.dram_tensor("agout", [8 * 512, T], BF16, kind="Internal")

    def asb(name, shape, dt=F32):
        sb[name] = nc.alloc_sbuf_tensor("s_" + name, list(shape), dt)
        return sb[name]

    asb("xb", (128, NCH, T), BF16)
    asb("xb_stat", (128, NCH, JC), BF16)
    asb("lnt", (128, JC))
    asb("rowA", (1, T)); asb("rowB", (1, T)); asb("rowC", (1, T))
    asb("lnw", (128, 8)); asb("lnb", (128, 8))
    asb("ones_col", (128, 1), BF16)
    asb("onescol_f", (128, 1))
    asb("ones_row", (1, 128), BF16)
    asb("ones_rowf", (1, 128), F32)
    asb("ones512", (1, JC), BF16)
    asb("identf", (128, 128), F32)
    asb("cdb", (1, 4))
    nc.const_aps.register(F32, 1e-5, sb["cdb"][0:1, 0:1])
    nc.const_aps.register(F32, 1e-6, sb["cdb"][0:1, 1:2])
    asb("mask16b", (128, 8, 16), BF16)
    asb("tmask", (16, 8))
    asb("sel16b", (16, 8, 128), BF16)
    asb("nmcol", (128, 1)); asb("sumcol", (128, 1))
    asb("accA", (128, 1)); asb("accB", (128, 1))
    asb("bias1", (128, 8)); asb("bias2", (128, 8))

    arena = nc.alloc_sbuf_tensor("s_arena", [128, 130 * K], mybir.dt.uint8)
    AR0 = nc.lookup_mloc(arena).addr
    _pl = {}

    def place(name, shape, dt, off):
        key = (name, off)
        if key not in _pl:
            _pl[key] = nc.alloc_sbuf_tensor_at(
                "a_" + name, list(shape), dt, offset=AR0 + off)
        return _pl[key]

    x_sb = place("x", (128, NCH, T), F32, 0)

    ps["A"] = nc.alloc_psum_tensor("psA", [128, 2048], F32)
    ps["B"] = nc.alloc_psum_tensor("psB", [128, 512], F32)
    ps["C"] = nc.alloc_psum_tensor("psC", [128, 512], F32)
    ps["D"] = nc.alloc_psum_tensor("psD", [128, 512], F32)
    ps["E"] = nc.alloc_psum_tensor("psE", [128, 512], F32)
    psA4 = [ps["A"][:, i * JC:(i + 1) * JC] for i in range(4)]

    step = sch.step
    bar = sch.barrier
    _gate_store = [None]

    def _init(e, st):
        st.dma(e.dma_start(out=sb["identf"][:, :], in_=g["ident"][:, :]))
        st.dma(e.dma_start(out=x_sb[:, :, :],
                           in_=g["x"].rearrange("(c p) t -> p c t", p=128)))
        st.dma(e.dma_start(out=sb["tmask"][:, :], in_=g["tmask8"][:, :]))
        return None
    ld0 = step("sy", _init)

    def _ldb16(e, st):
        st.dma(e.dma_start(out=sb["mask16b"][:, :, :], in_=g["mask16"][:, :, :]))
        st.dma(e.dma_start(out=sb["sel16b"][:, :, :], in_=g["sel16"][:, :, :]))
        return None
    ldb = step("gp", _ldb16)

    def _consts(e, st):
        e.memset(sb["cdb"][0:1, 0:1], 1e-5)
        e.memset(sb["cdb"][0:1, 1:2], 1e-6)
        e.memset(sb["ones_col"][:, :], 1.0)
        e.memset(sb["onescol_f"][:, :], 1.0)
        e.memset(sb["ones_row"][:, :], 1.0)
        e.memset(sb["ones_rowf"][:, :], 1.0)
        return e.memset(sb["ones512"][:, :], 1.0)
    c0 = step("ve", _consts)

    def emit_ln(src3, out3, kind="plain", eps=1e-5, dep=()):
        stats, evacs = [], []
        prev_cast = None
        for jc in range(NJC):
            def _cast(e, st, jc=jc):
                last = None
                for c in range(NCH):
                    last = e.tensor_copy(sb["xb_stat"][:, c, :],
                                         src3[:, c, jc * JC:(jc + 1) * JC])
                return last
            depc = list(dep) if jc == 0 else [stats[jc - 1][1]]
            t_cast = step("ve", _cast, after=depc)

            def _sum(e, st, jc=jc):
                last = None
                for c in range(NCH):
                    last = e.matmul(ps["B"][0:1, :], sb["ones_col"][:, :],
                                    sb["xb_stat"][:, c, :],
                                    start=(c == 0), stop=(c == NCH - 1))
                return last
            deps = [t_cast] if jc == 0 else [t_cast, evacs[jc - 1]]
            t_sum = step("pe", _sum, after=deps)

            def _sq(e, st, jc=jc):
                return e.activation(sb["xb_stat"][:, :, :], sb["xb_stat"][:, :, :],
                                    AF.Square)
            t_sq = step("ac", _sq, after=[t_sum])

            def _sum2(e, st, jc=jc):
                last = None
                for c in range(NCH):
                    last = e.matmul(ps["C"][0:1, :], sb["ones_col"][:, :],
                                    sb["xb_stat"][:, c, :],
                                    start=(c == 0), stop=(c == NCH - 1))
                return last
            t_sum2 = step("pe", _sum2, after=[t_sq])
            stats.append((t_sum, t_sum2))

            def _evac(e, st, jc=jc):
                e.tensor_copy(sb["rowA"][0:1, jc * JC:(jc + 1) * JC], ps["B"][0:1, :])
                return e.tensor_copy(sb["rowB"][0:1, jc * JC:(jc + 1) * JC],
                                     ps["C"][0:1, :])
            evacs.append(step("ve", _evac, after=[t_sum2]))

        def _rows1(e, st):
            e.tensor_scalar(sb["rowA"][0:1, :], sb["rowA"][0:1, :], 1.0 / D, None, ALU.mult)
            e.tensor_scalar(sb["rowB"][0:1, :], sb["rowB"][0:1, :], 1.0 / D, None, ALU.mult)
            e.tensor_mul(sb["rowC"][0:1, :], sb["rowA"][0:1, :], sb["rowA"][0:1, :])
            return e.tensor_sub(sb["rowB"][0:1, :], sb["rowB"][0:1, :], sb["rowC"][0:1, :])
        rows1 = step("ve", _rows1, after=[evacs[-1]])

        def _rows2(e, st):
            e.activation(sb["rowB"][0:1, :], sb["rowB"][0:1, :], AF.Ln, bias=eps)
            return e.activation(sb["rowB"][0:1, :], sb["rowB"][0:1, :], AF.Exp, scale=-0.5)
        rows2 = step("ac", _rows2, after=[rows1])

        def _rows3(e, st):
            return e.scalar_tensor_tensor(sb["rowA"][0:1, :], sb["rowA"][0:1, :], -1.0,
                                          sb["rowB"][0:1, :], ALU.mult, ALU.mult)
        rows3 = step("ve", _rows3, after=[rows2])

        prev_app = None
        for jc in range(NJC):
            def _bc(e, st, jc=jc):
                e.matmul(ps["D"][:, :], sb["ones_rowf"][:, :],
                         sb["rowB"][0:1, jc * JC:(jc + 1) * JC], start=True, stop=True)
                return e.matmul(ps["E"][:, :], sb["ones_rowf"][:, :],
                                sb["rowA"][0:1, jc * JC:(jc + 1) * JC], start=True, stop=True)
            depb = [rows3] if prev_app is None else [rows3, prev_app]
            t_bc = step("pe", _bc, after=depb)

            def _app(e, st, jc=jc):
                last = None
                for c in range(NCH):
                    xs = src3[:, c, jc * JC:(jc + 1) * JC]
                    ov = out3[:, c, jc * JC:(jc + 1) * JC]
                    e.tensor_mul(sb["lnt"][:, :], xs, ps["D"][:, :])
                    if kind == "plain":
                        last = e.tensor_add(ov, sb["lnt"][:, :], ps["E"][:, :])
                    else:
                        e.tensor_add(sb["lnt"][:, :], sb["lnt"][:, :], ps["E"][:, :])
                        last = e.tensor_scalar(ov, sb["lnt"][:, :],
                                               sb["lnw"][:, c:c + 1],
                                               sb["lnb"][:, c:c + 1], ALU.mult, ALU.add)
                return last
            prev_app = step("ve", _app, after=[t_bc])
        return prev_app

    def spill_x(dep):
        def _sp(e, st):
            st.dma(e.dma_start(out=g["xspill"].rearrange("(c p) t -> p c t", p=128),
                               in_=x_sb[:, :, :]))
            return None
        return step("sy", _sp, after=dep)

    def reload_x(dep):
        def _rl(e, st):
            st.dma(e.dma_start(out=x_sb[:, :, :],
                               in_=g["xspill"].rearrange("(c p) t -> p c t", p=128)))
            return None
        return step("sy", _rl, after=dep)

    def attn_a():
        pre_o = place("pre_o", (128, 8, T), BF16, 66 * K)
        q_pair = place("q_pair", (128, T), BF16, 99 * K)
        k_pair = place("k_pair", (128, T), BF16, 103 * K)
        v_pair = place("v_pair", (128, 16, 128), BF16, 107 * K)
        eq_pair = place("eq_pair", (128, T), BF16, 111 * K)
        ek_pair = place("ek_pair", (128, 16, 128), BF16, 115 * K)
        aq_p = place("aq_p", (128, NCH, 128), BF16, 119 * K)
        ak_p = place("ak_p", (128, NCH, 128), BF16, 121 * K)
        av_p = place("av_p", (128, NCH, 128), BF16, 123 * K)
        cent_b = place("cent_b", (128, 8, 64), BF16, 125 * K)
        csq = place("csq", (128, 8, 64), F32, 126 * K)
        lsamp = place("lsamp", (16, 128), F32, 128 * K)
        inv_sc = place("inv_sc", (16, 64), F32, int(128.5 * K))
        amp2_sc = place("amp2_sc", (16, 64), F32, int(128.75 * K))
        amp2T = place("amp2T", (128, 16), F32, 129 * K)
        vagg_b = place("vagg_b", (128, 64), BF16, int(129.25 * K))
        ks_col = place("ks_col", (128, 2), BF16, int(129.5 * K))
        invrow = place("invrow", (1, D), F32, 0)
        amp2row = place("amp2row", (1, D), F32, 4 * K)
        c2row = place("c2row", (1, D), F32, 8 * K)
        negc2i = place("negc2i", (1, D), BF16, 12 * K)
        neginvrow = place("neginvrow", (1, D), BF16, 14 * K)
        q2row = place("q2row", (1, 2, T), BF16, 16 * K)
        k2row = place("k2row", (1, 2, T), BF16, 24 * K)
        denA = place("denA", (1, T), F32, 32 * K)
        denB = place("denB", (1, T), F32, 40 * K)
        num_sb = place("num_sb", (128, JC), F32, 48 * K)
        ksrow = place("ksrow", (1, 128), F32, 50 * K)
        invb = place("invb", (128, 64), BF16, 51 * K)
        awo = place("awo", (128, NCH, D), BF16, 99 * K)

        t_ln = emit_ln(x_sb[:, :, :], sb["xb"][:, :, :], "plain", dep=[ld0, c0])
        t_spill = spill_x([t_ln])
        if sub < 1:
            reload_x([t_spill])
            return

        def _ldsc(e, st):
            st.dma(e.dma_start(out=lsamp[:, 0:64], in_=g["als"][:, :]))
            st.dma(e.dma_start(out=lsamp[:, 64:128], in_=g["aamp"][:, :]))
            st.dma(e.dma_start(out=cent_b[:, :, :], in_=g["acent"][:, :, :]))
            return None
        t_ld = step("gp", _ldsc, after=[t_spill])

        t_exp = step("ac", lambda e, st: e.activation(inv_sc[:, :], lsamp[:, 0:64],
                                                      AF.Exp), after=[t_ld])

        def _inv(e, st):
            e.tensor_scalar(inv_sc[:, :], inv_sc[:, :], 1e-3, None, ALU.max)
            e.tensor_mul(inv_sc[:, :], inv_sc[:, :], inv_sc[:, :])
            st.sync(e, e.reciprocal(inv_sc[:, :], inv_sc[:, :]))
            e.tensor_scalar(inv_sc[:, :], inv_sc[:, :], 0.5, None, ALU.mult)
            return e.tensor_mul(amp2_sc[:, :], lsamp[:, 64:128], lsamp[:, 64:128])
        t_inv = step("ve", _inv, after=[t_exp])
        if sub == 11:
            reload_x([t_inv])
            return

        def _rowize(e, st):
            st.dma(e.dma_start(out=invrow[0:1, :], in_=inv_sc[:, :]))
            st.dma(e.dma_start(out=amp2row[0:1, :], in_=amp2_sc[:, :]))
            return None
        t_rz = step("gp", _rowize, after=[t_inv])
        if sub == 12:
            reload_x([t_rz])
            return

        t_a2t = step("pe", lambda e, st: e.matmul(ps["B"][0:64, 0:16], amp2_sc[:, :],
                                                  sb["identf"][0:16, 0:16],
                                                  start=True, stop=True),
                     after=[t_inv])
        t_a2te = step("ve", lambda e, st: e.tensor_copy(amp2T[0:64, :],
                                                        ps["B"][0:64, 0:16]),
                      after=[t_a2t])
        t_csq = step("ac", lambda e, st: e.activation(csq[:, :, :], cent_b[:, :, :],
                                                      AF.Square), after=[t_ld])
        if sub == 13:
            reload_x([t_csq, t_a2te])
            return

        prev = None
        for p in range(8):
            def _c2(e, st, p=p):
                e.matmul(ps["C"][0:1, 0:64], sb["onescol_f"][0:64, 0:1],
                         csq[0:64, p, :], start=True, stop=True)
                return e.matmul(ps["D"][0:1, 0:64], sb["onescol_f"][64:128, 0:1],
                                csq[64:128, p, :], start=True, stop=True)
            t_c2 = step("pe", _c2, after=[t_csq] if prev is None else [prev])

            def _c2e(e, st, p=p):
                e.tensor_copy(c2row[0:1, p * 128:p * 128 + 64], ps["C"][0:1, 0:64])
                return e.tensor_copy(c2row[0:1, p * 128 + 64:(p + 1) * 128],
                                     ps["D"][0:1, 0:64])
            prev = step("ve", _c2e, after=[t_c2])
            if sub == 130 + p:
                reload_x([prev])
                return

        def _fin(e, st):
            e.tensor_mul(c2row[0:1, :], c2row[0:1, :], invrow[0:1, :])
            e.tensor_scalar(negc2i[0:1, :], c2row[0:1, :], -1.0, None, ALU.mult)
            return e.tensor_scalar(neginvrow[0:1, :], invrow[0:1, :], -1.0, None, ALU.mult)
        t_fin = step("ve", _fin, after=[prev, t_rz, t_a2te])
        if sub == 14:
            reload_x([t_fin])
            return

        prev = t_fin
        for p in range(8):
            def _ib(e, st, p=p):
                e.matmul(ps["D"][0:64, 0:64], sb["ones_rowf"][0:1, 0:64],
                         invrow[0:1, (2 * p) * 64:(2 * p) * 64 + 64], start=True, stop=True)
                return e.matmul(ps["D"][64:128, 0:64], sb["ones_rowf"][0:1, 0:64],
                                invrow[0:1, (2 * p + 1) * 64:(2 * p + 1) * 64 + 64],
                                start=True, stop=True, tile_position=(0, 64))
            t_ib = step("pe", _ib, after=[prev])

            def _ibe(e, st, p=p):
                e.tensor_copy(invb[:, :], ps["D"][:, 0:64])
                return e.scalar_tensor_tensor(cent_b[:, p, :], invb[:, :], 2.0,
                                              cent_b[:, p, :], ALU.mult, ALU.mult)
            prev = step("ve", _ibe, after=[t_ib])
        bar()
        if sub < 2:
            reload_x([])
            return

        for p in range(8):
            if (30 <= sub <= 40) or (130 <= sub <= 140) or (330 <= sub <= 340):
                _attn_a_pair(p, q_pair, k_pair, v_pair, eq_pair, ek_pair,
                             aq_p, ak_p, av_p, cent_b, invrow, amp2row, c2row,
                             negc2i, neginvrow, q2row, k2row, denA, denB,
                             num_sb, ksrow, vagg_b, ks_col, amp2T, pre_o, sub)
                bar()
                reload_x([])
                return
            if p >= max(1, sub - 2):
                step("ve", lambda e, st, p=p: e.memset(pre_o[:, p, :], 0.0))
                continue
            _attn_a_pair(p, q_pair, k_pair, v_pair, eq_pair, ek_pair,
                         aq_p, ak_p, av_p, cent_b, invrow, amp2row, c2row,
                         negc2i, neginvrow, q2row, k2row, denA, denB,
                         num_sb, ksrow, vagg_b, ks_col, amp2T, pre_o, 99)
            bar()
        bar()

        t_rx = reload_x([])
        t_wo = step("gp", lambda e, st: st.dma(e.dma_start(
            out=awo[:, :, :], in_=g["ao"].rearrange("(c p) m -> p c m", p=128))) and None)

        adds = [None, None]
        k = 0
        for jc in range(NJC):
            for dc in range(NCH):
                bk = ps["B"] if k % 2 == 0 else ps["C"]

                def _mm(e, st, jc=jc, dc=dc, bk=bk):
                    last = None
                    for pc in range(NCH):
                        last = e.matmul(bk[:, :], awo[:, pc, dc * 128:(dc + 1) * 128],
                                        pre_o[:, pc, jc * JC:(jc + 1) * JC],
                                        start=(pc == 0), stop=(pc == NCH - 1))
                    return last
                depm = [t_wo] if adds[k % 2] is None else [t_wo, adds[k % 2]]
                t_mm = step("pe", _mm, after=depm)

                def _add(e, st, jc=jc, dc=dc, bk=bk):
                    xv = x_sb[:, dc, jc * JC:(jc + 1) * JC]
                    return e.tensor_add(xv, bk[:, :], xv)
                adds[k % 2] = step("ve", _add, after=[t_mm, t_rx])
                k += 1
        bar()

    def _attn_a_pair(p, q_pair, k_pair, v_pair, eq_pair, ek_pair,
                     aq_p, ak_p, av_p, cent_b, invrow, amp2row, c2row,
                     negc2i, neginvrow, q2row, k2row, denA, denB,
                     num_sb, ksrow, vagg_b, ks_col, amp2T, pre_o, sub=99):
        hA, hB = 2 * p, 2 * p + 1

        def _ldw(e, st, p=p):
            for dst, wsrc in ((aq_p, "aq"), (ak_p, "ak"), (av_p, "av")):
                st.dma(e.dma_start(
                    out=dst[:, :, :],
                    in_=g[wsrc][:, p * 128:(p + 1) * 128]
                        .rearrange("(c q) m -> q c m", q=128)))
            return None
        t_ldw = step("gp", _ldw)

        def _qmm(e, st):
            last = None
            for jc in range(NJC):
                for c in range(NCH):
                    last = e.matmul(psA4[jc], aq_p[:, c, :],
                                    sb["xb"][:, c, jc * JC:(jc + 1) * JC],
                                    start=(c == 0), stop=(c == NCH - 1))
            return last
        t_qmm = step("pe", _qmm, after=[t_ldw])
        t_qev = step("ve", lambda e, st: e.tensor_copy(q_pair[:, :], ps["A"][:, :]),
                     after=[t_qmm])
        if sub == 31:
            return

        def _kmm(e, st):
            last = None
            for jc in range(NJC):
                for c in range(NCH):
                    last = e.matmul(psA4[jc], ak_p[:, c, :],
                                    sb["xb"][:, c, jc * JC:(jc + 1) * JC],
                                    start=(c == 0), stop=(c == NCH - 1))
            return last
        t_kmm = step("pe", _kmm, after=[t_qev])
        t_kev = step("ve", lambda e, st: e.tensor_copy(k_pair[:, :], ps["A"][:, :]),
                     after=[t_kmm])

        vprev = None
        for tc2 in range(8):
            def _vmm(e, st, tc2=tc2):
                last = None
                for i, tt in enumerate((2 * tc2, 2 * tc2 + 1)):
                    bank = ps["B"] if i == 0 else ps["C"]
                    for c in range(NCH):
                        last = e.matmul(bank[:, 0:128],
                                        sb["xb"][:, c, tt * 128:(tt + 1) * 128],
                                        av_p[:, c, :], start=(c == 0), stop=(c == NCH - 1))
                return last
            t_vmm = step("pe", _vmm, after=[t_ldw] if vprev is None else [vprev])

            def _vev(e, st, tc2=tc2):
                e.tensor_copy(v_pair[:, 2 * tc2, :], ps["B"][:, 0:128])
                return e.tensor_copy(v_pair[:, 2 * tc2 + 1, :], ps["C"][:, 0:128])
            vprev = step("ve", _vev, after=[t_vmm])
        t_vdone = vprev
        if sub == 32:
            return

        t_sqq = step("ac", lambda e, st: e.activation(eq_pair[:, :], q_pair[:, :],
                                                      AF.Square), after=[t_qev])
        prev = t_sqq
        for jc in range(NJC):
            def _q2jc(e, st, jc=jc):
                e.matmul(ps["D"][0:1, 0:JC], sb["ones_col"][0:64, 0:1],
                         eq_pair[0:64, jc * JC:(jc + 1) * JC], start=True, stop=True)
                return e.matmul(ps["E"][0:1, 0:JC], sb["ones_col"][64:128, 0:1],
                                eq_pair[64:128, jc * JC:(jc + 1) * JC],
                                start=True, stop=True)
            t_q2 = step("pe", _q2jc, after=[prev])

            def _q2ev(e, st, jc=jc):
                e.tensor_copy(q2row[0:1, 0, jc * JC:(jc + 1) * JC], ps["D"][0:1, 0:JC])
                return e.tensor_copy(q2row[0:1, 1, jc * JC:(jc + 1) * JC],
                                     ps["E"][0:1, 0:JC])
            prev = step("ve", _q2ev, after=[t_q2])
        t_q2done = prev
        if sub == 331:
            return

        t_sqk = step("ac", lambda e, st: e.activation(
            eq_pair[:, :], k_pair[:, :], AF.Square), after=[t_kev, t_q2done])
        prev = t_sqk
        for jc in range(NJC):
            def _k2jc(e, st, jc=jc):
                e.matmul(ps["D"][0:1, 0:JC], sb["ones_col"][0:64, 0:1],
                         eq_pair[0:64, jc * JC:(jc + 1) * JC], start=True, stop=True)
                return e.matmul(ps["E"][0:1, 0:JC], sb["ones_col"][64:128, 0:1],
                                eq_pair[64:128, jc * JC:(jc + 1) * JC],
                                start=True, stop=True)
            t_k2 = step("pe", _k2jc, after=[prev])

            def _k2ev(e, st, jc=jc):
                e.tensor_copy(k2row[0:1, 0, jc * JC:(jc + 1) * JC], ps["D"][0:1, 0:JC])
                return e.tensor_copy(k2row[0:1, 1, jc * JC:(jc + 1) * JC],
                                     ps["E"][0:1, 0:JC])
            prev = step("ve", _k2ev, after=[t_k2])
        t_k2done = prev
        if sub == 33:
            return

        eqprev = t_k2done
        for jc in range(NJC):
            def _eq(e, st, jc=jc, p=p, hA=hA, hB=hB):
                sl = slice(jc * JC, (jc + 1) * JC)
                e.matmul(ps["B"][0:64, :], cent_b[0:64, p, :], q_pair[0:64, sl],
                         start=True, stop=False)
                e.matmul(ps["B"][0:64, :], neginvrow[0:1, hA * 64:hA * 64 + 64],
                         q2row[0:1, 0, sl], start=False, stop=False)
                e.matmul(ps["B"][0:64, :], negc2i[0:1, hA * 64:hA * 64 + 64],
                         sb["ones512"][0:1, :], start=False, stop=True)
                e.matmul(ps["B"][64:128, :], cent_b[64:128, p, :], q_pair[64:128, sl],
                         start=True, stop=False, tile_position=(64, 64))
                e.matmul(ps["B"][64:128, :], neginvrow[0:1, hB * 64:hB * 64 + 64],
                         q2row[0:1, 1, sl], start=False, stop=False,
                         tile_position=(0, 64))
                return e.matmul(ps["B"][64:128, :], negc2i[0:1, hB * 64:hB * 64 + 64],
                                sb["ones512"][0:1, :], start=False, stop=True,
                                tile_position=(0, 64))
            t_eq = step("pe", _eq, after=[eqprev])
            eqprev = step("ac", lambda e, st, jc=jc: e.activation(
                eq_pair[:, jc * JC:(jc + 1) * JC], ps["B"][:, :], AF.Exp), after=[t_eq])
        t_eqmin = step("ve", lambda e, st: e.tensor_scalar(eq_pair[:, :], eq_pair[:, :],
                                                           1.0, None, ALU.min),
                       after=[eqprev])
        if sub == 34:
            return

        ekprev = t_k2done
        for tc in range(16):
            def _ek(e, st, tc=tc, p=p, hA=hA, hB=hB):
                tl = slice(tc * 128, (tc + 1) * 128)
                e.matmul(ps["C"][:, 0:64], k_pair[0:64, tl], cent_b[0:64, p, :],
                         start=True, stop=False)
                e.matmul(ps["C"][:, 0:64], k2row[0:1, 0, tl],
                         neginvrow[0:1, hA * 64:hA * 64 + 64], start=False, stop=False)
                e.matmul(ps["C"][:, 0:64], sb["ones_row"][0:1, :],
                         negc2i[0:1, hA * 64:hA * 64 + 64], start=False, stop=True)
                e.matmul(ps["D"][:, 0:64], k_pair[64:128, tl], cent_b[64:128, p, :],
                         start=True, stop=False, tile_position=(64, 0))
                e.matmul(ps["D"][:, 0:64], k2row[0:1, 1, tl],
                         neginvrow[0:1, hB * 64:hB * 64 + 64], start=False, stop=False)
                return e.matmul(ps["D"][:, 0:64], sb["ones_row"][0:1, :],
                                negc2i[0:1, hB * 64:hB * 64 + 64], start=False, stop=True)
            t_ek = step("pe", _ek, after=[ekprev, t_eqmin] if tc == 0 else [ekprev])

            def _eke(e, st, tc=tc):
                e.activation(ek_pair[:, tc, 0:64], ps["C"][:, 0:64], AF.Exp)
                return e.activation(ek_pair[:, tc, 64:128], ps["D"][:, 0:64], AF.Exp)
            ekprev = step("ac", _eke, after=[t_ek])
        t_ekmin = step("ve", lambda e, st: e.tensor_scalar(ek_pair[:, :, :],
                                                           ek_pair[:, :, :], 1.0,
                                                           None, ALU.min),
                       after=[ekprev])
        if sub == 35:
            return

        def _vagg(e, st):
            for tc in range(16):
                e.matmul(ps["B"][0:64, 0:64], ek_pair[:, tc, 0:64], v_pair[:, tc, 0:64],
                         start=(tc == 0), stop=(tc == 15))
            for tc in range(16):
                e.matmul(ps["B"][64:128, 0:64], ek_pair[:, tc, 64:128],
                         v_pair[:, tc, 64:128], start=(tc == 0), stop=(tc == 15),
                         tile_position=(0, 64))
            last = None
            for tc in range(16):
                last = e.matmul(ps["E"][0:1, 0:128], sb["ones_col"][:, :],
                                ek_pair[:, tc, :], start=(tc == 0), stop=(tc == 15))
            return last
        t_vagg = step("pe", _vagg, after=[t_ekmin, t_vdone])

        def _vev2(e, st, hA=hA, hB=hB):
            e.tensor_scalar(vagg_b[0:64, :], ps["B"][0:64, 0:64],
                            amp2T[0:64, hA:hA + 1], None, ALU.mult)
            e.tensor_scalar(vagg_b[64:128, :], ps["B"][64:128, 0:64],
                            amp2T[0:64, hB:hB + 1], None, ALU.mult)
            e.tensor_copy(ksrow[0:1, :], ps["E"][0:1, 0:128])
            return e.tensor_mul(ksrow[0:1, :], ksrow[0:1, :],
                                amp2row[0:1, hA * 64:hA * 64 + 128])
        t_vev2 = step("ve", _vev2, after=[t_vagg])

        t_kst = step("pe", lambda e, st: e.matmul(ps["C"][0:128, 0:1], ksrow[0:1, :],
                                                  sb["onescol_f"][0:1, 0:1],
                                                  start=True, stop=True),
                     after=[t_vev2])

        def _kscol(e, st):
            e.memset(ks_col[:, :], 0.0)
            e.tensor_copy(ks_col[0:64, 0:1], ps["C"][0:64, 0:1])
            return e.tensor_copy(ks_col[64:128, 1:2], ps["C"][64:128, 0:1])
        t_kscol = step("ve", _kscol, after=[t_kst])
        if sub == 36:
            return

        prev = t_kscol
        for jc in range(NJC):
            def _den(e, st, jc=jc):
                sl = slice(jc * JC, (jc + 1) * JC)
                e.matmul(ps["E"][0:1, 0:JC], ks_col[:, 0:1], eq_pair[:, sl],
                         start=True, stop=True)
                return e.matmul(ps["D"][0:1, 0:JC], ks_col[:, 1:2], eq_pair[:, sl],
                                start=True, stop=True)
            t_den = step("pe", _den, after=[prev])

            def _denev(e, st, jc=jc):
                sl = slice(jc * JC, (jc + 1) * JC)
                e.tensor_copy(denA[0:1, sl], ps["E"][0:1, 0:JC])
                return e.tensor_copy(denB[0:1, sl], ps["D"][0:1, 0:JC])
            prev = step("ve", _denev, after=[t_den])

        def _recip(e, st):
            e.tensor_scalar(denA[0:1, :], denA[0:1, :], 1e-6, None, ALU.add)
            e.tensor_scalar(denB[0:1, :], denB[0:1, :], 1e-6, None, ALU.add)
            e.reciprocal(denA[0:1, :], denA[0:1, :])
            return e.reciprocal(denB[0:1, :], denB[0:1, :])
        t_rec = step("ve", _recip, after=[prev])
        if sub == 37:
            return

        prev = t_rec
        for jc in range(NJC):
            def _num2(e, st, jc=jc):
                sl = slice(jc * JC, (jc + 1) * JC)
                e.matmul(ps["B"][0:64, :], vagg_b[0:64, :], eq_pair[0:64, sl],
                         start=True, stop=True)
                e.matmul(ps["B"][64:128, :], vagg_b[64:128, :], eq_pair[64:128, sl],
                         start=True, stop=True, tile_position=(64, 64))
                e.matmul(ps["D"][0:64, 0:JC], sb["ones_rowf"][0:1, 0:64],
                         denA[0:1, sl], start=True, stop=True)
                return e.matmul(ps["D"][64:128, 0:JC], sb["ones_rowf"][0:1, 0:64],
                                denB[0:1, sl], start=True, stop=True,
                                tile_position=(0, 64))
            t_n2 = step("pe", _num2, after=[prev])

            def _preo(e, st, jc=jc, p=p):
                e.tensor_copy(num_sb[:, :], ps["B"][:, :])
                return e.tensor_mul(pre_o[:, p, jc * JC:(jc + 1) * JC],
                                    num_sb[:, :], ps["D"][:, :])
            prev = step("ve", _preo, after=[t_n2])

    def ffn(call, sub=99):
        h = place("h", (128, NCH, T), F32, 66 * K)
        q_b = place("q_b", (128, NCH, T), BF16, 0)
        ctx_acc = place("ctx_acc", (128, NCH, JC), F32, 32 * K)
        sc8 = place("sc8", (16, 8, JC), BF16, 48 * K)
        kv_t = place("kv_t", (128, 4, JC), BF16, 56 * K)
        fpr_dc = place("fpr_dc", (128, NCH, 128), BF16, 56 * K)
        mx16 = place("mx16", (16, JC), F32, 61 * K)
        s16t = place("s16t", (16, JC), F32, 63 * K)
        prod4 = place("prod4", (128, 4, JC), BF16, int(61 * K))
        glog = place("glog", (128, 16, 8), F32, 65 * K)
        gsc1 = place("gsc1", (128, 16, 8), F32, int(65.5 * K))
        gsc2 = place("gsc2", (128, 16, 8), F32, 60 * K)
        grow = place("grow", (128, 16), F32, int(60.5 * K))
        w1h = place("w1h", (128, NCH, K), BF16, 0)
        w2h = place("w2h", (128, NCH, K), BF16, 16 * K)
        h1g = place("h1g", (128, NCH, JC), BF16, 32 * K)
        bc_sb = place("bc_sb", (128, JC), F32, 40 * K)
        tmp2 = place("tmp2", (128, JC), F32, 42 * K)
        wT = place("wT", (8, T), F32, 44 * K)
        d8_sb = place("d8_sb", (8, 8, 128), F32, 52 * K)
        rt_sb = place("rt_sb", (128, NCH, 8), BF16, 56 * K)

        t_h = emit_ln(x_sb[:, :, :], h[:, :, :], "plain", dep=[])
        t_spill = spill_x([t_h])
        if sub == 201:
            _gate_store[0] = h
            return None

        def _ldln1(e, st):
            st.dma(e.dma_start(out=sb["lnw"][:, :], in_=g["fln1w"][:, :]))
            st.dma(e.dma_start(out=sb["lnb"][:, :], in_=g["fln1b"][:, :]))
            return None
        t_l1 = step("gp", _ldln1)
        t_ln1 = emit_ln(h[:, :, :], sb["xb"][:, :, :], "affine", dep=[t_l1])
        if sub == 202:
            _gate_store[0] = h
            return None

        qkv_w = place("qkv_w", (128, NCH, 128), BF16, 56 * K)
        agv = g["agin"].ap().rearrange("(c p) t -> p c t", p=128)
        kvev = place("kvev", (128, JC), BF16, 60 * K)
        prev = t_ln1
        ag_deps = []
        for oc in range(12):
            def _lw(e, st, oc=oc):
                st.dma(e.dma_start(
                    out=qkv_w[:, :, :],
                    in_=g["fqkv"][:, oc * 128:(oc + 1) * 128]
                        .rearrange("(c q) m -> q c m", q=128)))
                return None
            t_lw = step("gp", _lw, after=[prev])

            def _mm(e, st, oc=oc):
                last = None
                for jc in range(NJC):
                    for c in range(NCH):
                        last = e.matmul(psA4[jc], qkv_w[:, c, :],
                                        sb["xb"][:, c, jc * JC:(jc + 1) * JC],
                                        start=(c == 0), stop=(c == NCH - 1))
                return last
            t_mm = step("pe", _mm, after=[t_lw])

            if oc < 8:
                prev = step("ve", lambda e, st, oc=oc: e.tensor_copy(
                    q_b[:, oc, :], ps["A"][:, :]), after=[t_mm])
            else:
                for jc in range(NJC):
                    t_ev = step("ve", lambda e, st, oc=oc, jc=jc: e.tensor_copy(
                        kvev[:, :], psA4[jc]), after=[t_mm] if jc == 0 else [prev])

                    def _ship(e, st, oc=oc, jc=jc):
                        st.dma(e.dma_start(out=agv[:, oc - 8, jc * JC:(jc + 1) * JC],
                                           in_=kvev[:, :]))
                        return None
                    t_sh = step("sy", _ship, after=[t_ev])
                    prev = t_sh
                ag_deps.append(prev)

        if sub == 203:
            _gate_store[0] = h
            return None

        def _cc(e, st):
            return e.collective_compute(
                "AllGather", ALU.bypass, replica_groups=[list(range(NCORES))],
                ins=[g["agin"].ap().opt()], outs=[g["agout"].ap().opt()])
        t_cc = step("gp", _cc, after=[prev])
        if sub == 204:
            _gate_store[0] = h
            return None

        agov = g["agout"].ap().rearrange("(t c p) u -> t p c u", p=128, c=4)
        prev_jc = t_cc
        for jc in range(NJC):
            sl = slice(jc * JC, (jc + 1) * JC)
            sc_deps = []
            prev_t = prev_jc
            for t in range(8):
                def _lkv(e, st, t=t, sl=sl):
                    st.dma(e.dma_start(out=kv_t[:, :, :], in_=agov[t, :, :, sl]))
                    return None
                t_kv = step("sy", _lkv, after=[prev_t])

                half_toks = []
                for half in range(2):
                    def _prod(e, st, half=half, sl=sl):
                        last = None
                        for i in range(4):
                            c = half * 4 + i
                            last = e.tensor_mul(prod4[:, i, :], q_b[:, c, sl],
                                                kv_t[:, 0 if c < 4 else 1, :])
                        return last
                    dp = [t_kv] if half == 0 else [half_toks[-1]]
                    t_pr = step("ve", _prod, after=dp)

                    def _smm(e, st, half=half, t=t):
                        last = None
                        for i in range(4):
                            c = half * 4 + i
                            last = e.matmul(ps["B"][0:16, :], sb["mask16b"][:, c, :],
                                            prod4[:, i, :],
                                            start=(c == 0), stop=(c == 7))
                        return last
                    t_sm = step("pe", _smm, after=[t_pr])
                    half_toks.append(t_sm)

                t_sev = step("ve", lambda e, st, t=t: e.tensor_copy(
                    sc8[:, t, :], ps["B"][0:16, :]), after=[half_toks[-1]])
                sc_deps.append(t_sev)
                prev_t = t_sev

            def _mx(e, st):
                e.tensor_max(mx16[:, :], sc8[:, 0, :], sc8[:, 1, :])
                for t in range(2, 8):
                    e.tensor_max(mx16[:, :], mx16[:, :], sc8[:, t, :])
                last = None
                for t in range(8):
                    last = e.tensor_sub(sc8[:, t, :], sc8[:, t, :], mx16[:, :])
                return last
            t_mx = step("ve", _mx, after=[sc_deps[-1]])

            t_ex = step("ac", lambda e, st: e.activation(sc8[:, :, :], sc8[:, :, :],
                                                         AF.Exp), after=[t_mx])

            def _norm(e, st):
                for t in range(8):
                    e.tensor_scalar(sc8[:, t, :], sc8[:, t, :],
                                    sb["tmask"][:, t:t + 1], None, ALU.mult)
                e.tensor_add(s16t[:, :], sc8[:, 0, :], sc8[:, 1, :])
                for t in range(2, 8):
                    e.tensor_add(s16t[:, :], s16t[:, :], sc8[:, t, :])
                st.sync(e, e.reciprocal(s16t[:, :], s16t[:, :]))
                last = None
                for t in range(8):
                    last = e.tensor_mul(sc8[:, t, :], sc8[:, t, :], s16t[:, :])
                return last
            t_nm = step("ve", _norm, after=[t_ex])

            t_z = step("ve", lambda e, st: e.memset(ctx_acc[:, :, :], 0.0), after=[t_nm])
            prev_c = t_z
            for t in range(8):
                def _lkv2(e, st, t=t, sl=sl):
                    st.dma(e.dma_start(out=kv_t[:, :, :], in_=agov[t, :, :, sl]))
                    return None
                t_kv2 = step("sy", _lkv2, after=[prev_c])
                for half in range(2):
                    def _bmm(e, st, half=half, t=t):
                        last = None
                        for i in range(4):
                            c = half * 4 + i
                            last = e.matmul(psA4[i], sb["sel16b"][:, c, :],
                                            sc8[:, t, :], start=True, stop=True)
                        return last
                    dp = [t_kv2, prev_c] if half == 0 else [prev_c]
                    t_bm = step("pe", _bmm, after=dp)

                    def _acc(e, st, half=half, t=t):
                        last = None
                        for i in range(4):
                            c = half * 4 + i
                            e.tensor_mul(sb["lnt"][:, :], psA4[i],
                                         kv_t[:, 2 + (0 if c < 4 else 1), :])
                            last = e.tensor_add(ctx_acc[:, c, :], ctx_acc[:, c, :],
                                                sb["lnt"][:, :])
                        return last
                    prev_c = step("ve", _acc, after=[t_bm])

            prev_jc = _ffn_proj(jc, prev_c, h, ctx_acc, fpr_dc)
            if sub == 205:
                _gate_store[0] = h
                return None
        if sub == 206:
            _gate_store[0] = h
            return None
        return _ffn_tail(call, h, q_b, glog, gsc1, gsc2, grow, wT, d8_sb, rt_sb,
                         w1h, w2h, h1g, bc_sb, tmp2, prev_jc, sub)

    def _ffn_proj(jc, dep, h, ctx_acc, fpr_dc):
        ctx_b8 = place("ctx_b8", (128, NCH, JC), BF16, 48 * K)

        def _cast(e, st):
            last = None
            for pc in range(NCH):
                last = e.tensor_copy(ctx_b8[:, pc, :], ctx_acc[:, pc, :])
            return last
        fpr2 = place("fpr2", (128, NCH, 128), BF16, 58 * K)
        t_cast = step("ve", _cast, after=[dep])
        hadds = [t_cast, t_cast]
        lfs = [None, None]
        pjs = [None, None]
        for dc in range(NCH):
            par = dc % 2
            buf = fpr_dc if par == 0 else fpr2
            bk = ps["B"] if par == 0 else ps["C"]

            def _lf(e, st, dc=dc, buf=buf):
                st.dma(e.dma_start(
                    out=buf[:, :, :],
                    in_=g["fproj"][:, dc * 128:(dc + 1) * 128]
                        .rearrange("(c q) m -> q c m", q=128)))
                return None
            deplf = [dep] if pjs[par] is None else [pjs[par]]
            t_lf = step("gp", _lf, after=deplf)
            lfs[par] = t_lf

            def _pj(e, st, dc=dc, buf=buf, bk=bk):
                last = None
                for pc in range(NCH):
                    last = e.matmul(bk[:, :], buf[:, pc, :], ctx_b8[:, pc, :],
                                    start=(pc == 0), stop=(pc == NCH - 1))
                return last
            dpj = [t_lf, t_cast] if hadds[par] is None else [t_lf, hadds[par]]
            t_pj = step("pe", _pj, after=dpj)
            pjs[par] = t_pj

            def _hadd(e, st, dc=dc, jc=jc, bk=bk):
                hv = h[:, dc, jc * JC:(jc + 1) * JC]
                return e.tensor_add(hv, bk[:, :], hv)
            hadds[par] = step("ve", _hadd, after=[t_pj])
        def _join(e, st):
            return e.nop()
        return step("ve", _join, after=[hadds[0], hadds[1]])

    def _ffn_tail(call, h, q_b, glog, gsc1, gsc2, grow, wT, d8_sb, rt_sb,
                  w1h, w2h, h1g, bc_sb, tmp2, dep, sub=99):
        def _ldln2(e, st):
            st.dma(e.dma_start(out=sb["lnw"][:, :], in_=g["fln2w"][:, :]))
            st.dma(e.dma_start(out=sb["lnb"][:, :], in_=g["fln2b"][:, :]))
            st.dma(e.dma_start(out=rt_sb[:, :, :],
                               in_=g["frout"].rearrange("(c p) m -> p c m", p=128)))
            st.dma(e.dma_start(out=d8_sb[:, :, :], in_=g["delta8"][:, :, :]))
            return None
        t_l2 = step("gp", _ldln2, after=[dep])
        t_ln2 = emit_ln(h[:, :, :], sb["xb"][:, :, :], "affine", dep=[t_l2])
        if sub == 207:
            _gate_store[0] = h
            return None
        if sub == 211:
            _gate_store[0] = ("bf16", sb["xb"])
            return None

        prev = t_ln2
        for js in range(16):
            def _rmm(e, st, js=js):
                last = None
                for c in range(NCH):
                    last = e.matmul(ps["B"][0:128, 0:8],
                                    sb["xb"][:, c, js * 128:(js + 1) * 128],
                                    rt_sb[:, c, :], start=(c == 0), stop=(c == NCH - 1))
                return last
            t_rm = step("pe", _rmm, after=[prev])
            prev = step("ve", lambda e, st, js=js: e.tensor_copy(
                glog[:, js, :], ps["B"][0:128, 0:8]), after=[t_rm])

        if sub == 213:
            _gate_store[0] = ("p128", glog)
            return None

        def _gate1(e, st):
            st.sync(e, e.reduce_max(grow[:, :], glog[:, :, :], axis=AX))
            g3 = grow[:, :].rearrange("p (a b) -> p a b", b=1).broadcast_to([128, 16, 8])
            return e.tensor_sub(glog[:, :, :], glog[:, :, :], g3)
        t_g1 = step("ve", _gate1, after=[prev])
        t_g2 = step("ac", lambda e, st: e.activation(glog[:, :, :], glog[:, :, :], AF.Exp),
                    after=[t_g1])

        def _gate3(e, st):
            st.sync(e, e.reduce_sum(grow[:, :], glog[:, :, :], axis=AX))
            st.sync(e, e.reciprocal(grow[:, :], grow[:, :]))
            g3 = grow[:, :].rearrange("p (a b) -> p a b", b=1).broadcast_to([128, 16, 8])
            e.tensor_mul(glog[:, :, :], glog[:, :, :], g3)
            st.sync(e, e.reduce_max(grow[:, :], glog[:, :, :], axis=AX))
            m3 = grow[:, :].rearrange("p (a b) -> p a b", b=1).broadcast_to([128, 16, 8])
            e.tensor_tensor(gsc1[:, :, :], glog[:, :, :], m3, ALU.is_equal)
            e.scalar_tensor_tensor(gsc2[:, :, :], gsc1[:, :, :], -2.0, glog[:, :, :],
                                   ALU.mult, ALU.add)
            st.sync(e, e.reduce_max(grow[:, :], gsc2[:, :, :], axis=AX))
            m3 = grow[:, :].rearrange("p (a b) -> p a b", b=1).broadcast_to([128, 16, 8])
            e.tensor_tensor(gsc1[:, :, :], glog[:, :, :], m3, ALU.is_ge)
            return e.tensor_mul(glog[:, :, :], glog[:, :, :], gsc1[:, :, :])
        t_g3 = step("ve", _gate3, after=[t_g2])
        if sub == 214:
            _gate_store[0] = ("p128", glog)
            return None
        if sub == 208:
            _gate_store[0] = h
            return None

        prev = t_g3
        for js in range(16):
            t_tr = step("pe", lambda e, st, js=js: e.transpose(
                ps["B"][0:8, 0:128], glog[:, js, :], sb["identf"][:, :]), after=[prev])
            prev = step("ve", lambda e, st, js=js: e.tensor_copy(
                wT[0:8, js * 128:(js + 1) * 128], ps["B"][0:8, 0:128]), after=[t_tr])
        t_wT = prev
        if sub == 209:
            _gate_store[0] = h
            return None
        if sub == 212:
            _gate_store[0] = ("wt", wT)
            return None

        fw1v = g["fw1"].rearrange("e (c p) f -> e p c f", p=128)
        fw2v = g["fw2"].rearrange("e (c p) f -> e p c f", p=128)
        prev_acc = t_wT
        for ei in range(8):
            for fh in range(2):
                def _lw1(e, st, ei=ei, fh=fh):
                    st.dma(e.dma_start(out=w1h[:, :, :],
                                       in_=fw1v[ei, :, :, fh * K:(fh + 1) * K]))
                    return None
                t_lwm = step("gp", _lw1, after=[prev_acc])

                def _lw2(e, st, ei=ei, fh=fh):
                    st.dma(e.dma_start(out=w2h[:, :, :],
                                       in_=fw2v[ei, :, fh * NCH:(fh + 1) * NCH, :]))
                    return None
                t_lw2 = step("gp", _lw2, after=[prev_acc])

                for jc in range(NJC):
                    sl = slice(jc * JC, (jc + 1) * JC)

                    def _h1a(e, st, sl=sl):
                        last = None
                        for fc in range(4):
                            for c in range(NCH):
                                last = e.matmul(psA4[fc], w1h[:, c, fc * 128:(fc + 1) * 128],
                                                sb["xb"][:, c, sl],
                                                start=(c == 0), stop=(c == NCH - 1))
                        return last
                    t_h1a = step("pe", _h1a, after=[t_lwm] if jc == 0 else [t_lwm, prev_acc])

                    def _gel1(e, st, jc=jc):
                        last = None
                        for fc in range(4):
                            last = e.activation(h1g[:, fc, :], psA4[fc], AF.Gelu_apprx_tanh)
                        return last
                    t_g1m = step("ac", _gel1, after=[t_h1a])

                    def _h1b(e, st, sl=sl):
                        last = None
                        for fc in range(4):
                            for c in range(NCH):
                                last = e.matmul(psA4[fc],
                                                w1h[:, c, (4 + fc) * 128:(5 + fc) * 128],
                                                sb["xb"][:, c, sl],
                                                start=(c == 0), stop=(c == NCH - 1))
                        return last
                    t_h1b = step("pe", _h1b, after=[t_g1m])

                    def _gel2(e, st, jc=jc):
                        last = None
                        for fc in range(4):
                            last = e.activation(h1g[:, 4 + fc, :], psA4[fc],
                                                AF.Gelu_apprx_tanh)
                        return last
                    t_g2m = step("ac", _gel2, after=[t_h1b])

                    def _bcm(e, st, ei=ei, sl=sl):
                        return e.matmul(ps["E"][:, :], d8_sb[:, ei, :], wT[0:8, sl],
                                        start=True, stop=True)
                    dbcm = [t_g2m, t_lw2] if jc == 0 else [t_g2m]
                    t_bcm = step("pe", _bcm, after=dbcm)
                    t_bce = step("ve", lambda e, st: e.tensor_copy(bc_sb[:, :], ps["E"][:, :]),
                                 after=[t_bcm])

                    accs = [t_bce, t_bce]
                    for dc2 in range(4):
                        bk0 = ps["B"] if dc2 % 2 == 0 else ps["D"]
                        bk1 = ps["C"] if dc2 % 2 == 0 else ps["E"]

                        def _eo(e, st, dc2=dc2, bk0=bk0, bk1=bk1):
                            for c in range(NCH):
                                e.matmul(bk0[:, :], w2h[:, c, (2 * dc2) * 128:(2 * dc2 + 1) * 128],
                                         h1g[:, c, :], start=(c == 0), stop=(c == NCH - 1))
                            last = None
                            for c in range(NCH):
                                last = e.matmul(bk1[:, :],
                                                w2h[:, c, (2 * dc2 + 1) * 128:(2 * dc2 + 2) * 128],
                                                h1g[:, c, :], start=(c == 0), stop=(c == NCH - 1))
                            return last
                        t_eo = step("pe", _eo, after=[accs[dc2 % 2]])

                        def _accm(e, st, dc2=dc2, jc=jc, bk0=bk0, bk1=bk1):
                            e.tensor_mul(tmp2[:, :], bk0[:, :], bc_sb[:, :])
                            hv = h[:, 2 * dc2, jc * JC:(jc + 1) * JC]
                            e.tensor_add(hv, hv, tmp2[:, :])
                            e.tensor_mul(tmp2[:, :], bk1[:, :], bc_sb[:, :])
                            hv = h[:, 2 * dc2 + 1, jc * JC:(jc + 1) * JC]
                            return e.tensor_add(hv, hv, tmp2[:, :])
                        accs[dc2 % 2] = step("ve", _accm, after=[t_eo])
                    prev_acc = accs[1]
            if sub == 210:
                _gate_store[0] = h
                break

        t_rx = reload_x([prev_acc])
        prev = prev_acc
        for dc in range(NCH):
            def _xadd(e, st, dc=dc):
                last = None
                for jc in range(NJC):
                    xv = x_sb[:, dc, jc * JC:(jc + 1) * JC]
                    last = e.tensor_add(xv, xv, h[:, dc, jc * JC:(jc + 1) * JC])
                return last
            prev = step("ve", _xadd, after=[t_rx, prev])
        return prev

    def attn_b():
        tmp_h = place("tmph", (128, NCH, T), F32, 66 * K)
        pe_bf = place("pe_bf", (128, NCH, PPAD), BF16, 0)
        pbuf = place("pbuf", (128, PPAD), BF16, 72 * K)
        p_pair = place("p_pair", (128, PTRIM), BF16, 0)
        q_u = place("q_u", (128, T), BF16, 9 * K)
        q_v = place("q_v", (128, T), BF16, 13 * K)
        kb_pair = place("kb_pair", (128, T), BF16, 17 * K)
        vb_pair = place("vb_pair", (128, 16, 128), BF16, 21 * K)
        S_A = place("S_A", (128, T), F32, 25 * K)
        S_B = place("S_B", (128, T), F32, 33 * K)
        P_A = place("P_A", (128, T), BF16, 41 * K)
        P_B = place("P_B", (128, T), BF16, 45 * K)
        PT_A = place("PT_A", (128, 2, T), BF16, 49 * K)
        PT_B = place("PT_B", (128, 2, T), BF16, 57 * K)
        posbuf = place("posbuf", (128, 2, JC), BF16, 65 * K)
        wq_sl = place("wq_sl", (128, NCH, 128), BF16, 67 * K)
        wk_sl = place("wk_sl", (128, NCH, 128), BF16, 69 * K)
        wv_sl = place("wv_sl", (128, NCH, 128), BF16, 71 * K)
        bvrow = place("bvrow", (1, D), F32, 73 * K)
        ctx_all = place("ctx_all", (128, NCH, T), BF16, 77 * K)
        bwo_sb = place("bwo_sb", (128, NCH, K), BF16, 110 * K)

        t_ln = emit_ln(x_sb[:, :, :], tmp_h[:, :, :], "plain", dep=[])

        def _ldg(e, st):
            st.dma(e.dma_start(out=sb["lnw"][:, :], in_=g["bg"][:, :]))
            st.dma(e.dma_start(out=sb["lnb"][:, :], in_=g["bb"][:, :]))
            st.dma(e.dma_start(out=sb["bias1"][:, :], in_=g["bbq"][:, :]))
            st.dma(e.dma_start(out=sb["bias2"][:, :], in_=g["bbk"][:, :]))
            return None
        t_lg = step("gp", _ldg, after=[t_ln])
        t_ln2 = emit_ln(tmp_h[:, :, :], sb["xb"][:, :, :], "affine", dep=[t_lg])
        t_spill = spill_x([t_ln2])

        ubias = place("ubias", (128, 8), F32, 109 * K)
        vbias = place("vbias", (128, 8), F32, int(109.25 * K))
        kbias = place("kbias", (128, 8), F32, int(109.5 * K))

        def _ldpe(e, st):
            st.dma(e.dma_start(out=pe_bf[:, :, :],
                               in_=g["pet"].rearrange("(c p) t -> p c t", p=128)))
            return None
        t_lpe = step("gp", _ldpe, after=[t_spill])

        wp_sl = place("wp_sl", (128, NCH, 128), BF16, 82 * K)
        prev = t_lpe
        for p in range(8):
            def _lbwp(e, st, p=p):
                st.dma(e.dma_start(
                    out=wp_sl[:, :, :],
                    in_=g["bwp"][:, p * 128:(p + 1) * 128].rearrange("(c q) m -> q c m", q=128)))
                return None
            t_lw = step("gp", _lbwp, after=[prev])
            evs = []
            for wch in range(9):
                def _pmm(e, st, wch=wch):
                    bank = ps["B"] if wch % 2 == 0 else ps["C"]
                    last = None
                    for c in range(NCH):
                        last = e.matmul(bank[:, :], wp_sl[:, c, :],
                                        pe_bf[:, c, wch * JC:(wch + 1) * JC],
                                        start=(c == 0), stop=(c == NCH - 1))
                    return last
                dpp = [t_lw] if wch < 2 else [t_lw, evs[wch - 2]]
                t_pm = step("pe", _pmm, after=dpp)
                t_pe = step("ac", lambda e, st, wch=wch: e.activation(
                    pbuf[:, wch * JC:(wch + 1) * JC],
                    (ps["B"] if wch % 2 == 0 else ps["C"])[:, :], AF.Identity),
                    after=[t_pm])
                evs.append(t_pe)

            def _shipp(e, st, p=p):
                st.dma(e.dma_start(out=g["pd"][p, :, :], in_=pbuf[:, 0:PTRIM]))
                return None
            prev = step("sy", _shipp, after=[evs[-1]])
        bar()

        def _ldg2(e, st):
            st.dma(e.dma_start(out=ubias[:, :], in_=g["bu"][:, :]))
            st.dma(e.dma_start(out=vbias[:, :], in_=g["bv"][:, :]))
            st.dma(e.dma_start(out=bvrow[0:1, :], in_=g["bbv_row"][:, :]))
            return None
        t_lg2 = step("gp", _ldg2)

        def _bias(e, st):
            e.tensor_add(ubias[:, :], ubias[:, :], sb["bias1"][:, :])
            e.tensor_add(vbias[:, :], vbias[:, :], sb["bias1"][:, :])
            return e.tensor_copy(kbias[:, :], sb["bias2"][:, :])
        t_bias = step("ve", _bias, after=[t_lg2])
        bar()

        for p in range(8):
            _attn_b_pair(p, p_pair, q_u, q_v, kb_pair, vb_pair, S_A, S_B,
                         P_A, P_B, PT_A, PT_B, posbuf, wq_sl, wk_sl, wv_sl,
                         bvrow, ubias, vbias, kbias, ctx_all)
            bar()

        t_rx = reload_x([])

        def _ldwo(e, st):
            st.dma(e.dma_start(out=bwo_sb[:, :, :],
                               in_=g["bwo"].rearrange("(c p) m -> p c m", p=128)))
            st.dma(e.dma_start(out=sb["bias1"][:, :], in_=g["bbout"][:, :]))
            return None
        t_wo = step("gp", _ldwo)

        adds = [None, None]
        k = 0
        for jc in range(NJC):
            for dc in range(NCH):
                bk = ps["B"] if k % 2 == 0 else ps["C"]

                def _mm(e, st, jc=jc, dc=dc, bk=bk):
                    last = None
                    for pc in range(NCH):
                        last = e.matmul(bk[:, :], bwo_sb[:, pc, dc * 128:(dc + 1) * 128],
                                        ctx_all[:, pc, jc * JC:(jc + 1) * JC],
                                        start=(pc == 0), stop=(pc == NCH - 1))
                    return last
                depm = [t_wo] if adds[k % 2] is None else [t_wo, adds[k % 2]]
                t_mm = step("pe", _mm, after=depm)

                def _add(e, st, jc=jc, dc=dc, bk=bk):
                    xv = x_sb[:, dc, jc * JC:(jc + 1) * JC]
                    return e.scalar_tensor_tensor(xv, bk[:, :],
                                                  sb["bias1"][:, dc:dc + 1], xv,
                                                  ALU.add, ALU.add)
                adds[k % 2] = step("ve", _add, after=[t_mm, t_rx])
                k += 1
        bar()

    def _attn_b_pair(p, p_pair, q_u, q_v, kb_pair, vb_pair, S_A, S_B,
                     P_A, P_B, PT_A, PT_B, posbuf, wq_sl, wk_sl, wv_sl,
                     bvrow, ubias, vbias, kbias, ctx_all):
        hA, hB = 2 * p, 2 * p + 1

        def _ldw(e, st, p=p):
            st.dma(e.dma_start(out=p_pair[:, :], in_=g["pd"][p, :, :]))
            for dst, wsrc in ((wq_sl, "bwq"), (wk_sl, "bwk"), (wv_sl, "bwv")):
                st.dma(e.dma_start(
                    out=dst[:, :, :],
                    in_=g[wsrc][:, p * 128:(p + 1) * 128]
                        .rearrange("(c q) m -> q c m", q=128)))
            return None
        t_ldw = step("gp", _ldw)

        def _qmm(e, st):
            last = None
            for jc in range(NJC):
                for c in range(NCH):
                    last = e.matmul(psA4[jc], wq_sl[:, c, :],
                                    sb["xb"][:, c, jc * JC:(jc + 1) * JC],
                                    start=(c == 0), stop=(c == NCH - 1))
            return last
        t_qmm = step("pe", _qmm, after=[t_ldw])

        def _qev(e, st, p=p):
            e.activation(q_u[:, :], ps["A"][:, :], AF.Identity, bias=ubias[:, p:p + 1])
            return e.activation(q_v[:, :], ps["A"][:, :], AF.Identity,
                                bias=vbias[:, p:p + 1])
        t_qev = step("ac", _qev, after=[t_qmm])

        def _kmm(e, st):
            last = None
            for jc in range(NJC):
                for c in range(NCH):
                    last = e.matmul(psA4[jc], wk_sl[:, c, :],
                                    sb["xb"][:, c, jc * JC:(jc + 1) * JC],
                                    start=(c == 0), stop=(c == NCH - 1))
            return last
        t_kmm = step("pe", _kmm, after=[t_qev])
        t_kev = step("ac", lambda e, st, p=p: e.activation(
            kb_pair[:, :], ps["A"][:, :], AF.Identity, bias=kbias[:, p:p + 1]),
            after=[t_kmm])

        vprev = t_kev
        for tc2 in range(8):
            def _vmm(e, st, tc2=tc2, p=p):
                last = None
                for i, tt in enumerate((2 * tc2, 2 * tc2 + 1)):
                    bank = ps["B"] if i == 0 else ps["C"]
                    for c in range(NCH):
                        e.matmul(bank[:, 0:128], sb["xb"][:, c, tt * 128:(tt + 1) * 128],
                                 wv_sl[:, c, :], start=(c == 0), stop=False)
                    last = e.matmul(bank[:, 0:128], sb["ones_rowf"][0:1, :],
                                    bvrow[0:1, p * 128:(p + 1) * 128],
                                    start=False, stop=True)
                return last
            t_vmm = step("pe", _vmm, after=[vprev])

            def _vev(e, st, tc2=tc2):
                e.tensor_copy(vb_pair[:, 2 * tc2, :], ps["B"][:, 0:128])
                return e.tensor_copy(vb_pair[:, 2 * tc2 + 1, :], ps["C"][:, 0:128])
            vprev = step("ve", _vev, after=[t_vmm])
        t_vdone = vprev

        pprev = t_vdone
        prev_cb = t_vdone
        prev_s2 = t_vdone
        last_pv = [t_vdone]
        for qt in range(16):
            w0 = 128 * (15 - qt)
            qsl = slice(qt * 128, (qt + 1) * 128)

            def _cmm(e, st, qt=qt, qsl=qsl):
                banksB = (ps["B"], ps["C"], ps["D"], ps["E"])
                last = None
                for ss in range(4):
                    e.matmul(psA4[ss], q_u[0:64, qsl],
                             kb_pair[0:64, ss * JC:(ss + 1) * JC], start=True, stop=True)
                    last = e.matmul(banksB[ss][:, :], q_u[64:128, qsl],
                                    kb_pair[64:128, ss * JC:(ss + 1) * JC],
                                    start=True, stop=True, tile_position=(64, 0))
                return last
            t_cm = step("pe", _cmm, after=[prev_cb])

            def _cevA(e, st, qt=qt):
                last = None
                for ss in range(4):
                    last = e.activation(S_A[:, ss * JC:(ss + 1) * JC], psA4[ss], AF.Identity)
                return last
            t_ca = step("ac", _cevA, after=[t_cm, prev_s2])

            def _cevB(e, st, qt=qt):
                last = None
                for ss, bank in enumerate((ps["B"], ps["C"], ps["D"], ps["E"])):
                    last = e.tensor_copy(S_B[:, ss * JC:(ss + 1) * JC], bank[:, :])
                return last
            t_cb = step("ve", _cevB, after=[t_cm, prev_s2])

            posev = []
            for w in range(5):
                def _pmm(e, st, w=w, w0=w0, qsl=qsl):
                    e.matmul(ps["B"][:, :], q_v[0:64, qsl],
                             p_pair[0:64, w0 + w * JC:w0 + (w + 1) * JC],
                             start=True, stop=True)
                    return e.matmul(ps["C"][:, :], q_v[64:128, qsl],
                                    p_pair[64:128, w0 + w * JC:w0 + (w + 1) * JC],
                                    start=True, stop=True, tile_position=(64, 0))
                dpp = [t_cb] if w == 0 else [posev[-1]]
                t_pm = step("pe", _pmm, after=dpp)

                def _pev(e, st, w=w):
                    e.tensor_copy(posbuf[:, 0, :], ps["B"][:, :])
                    return e.tensor_copy(posbuf[:, 1, :], ps["C"][:, :])
                t_pv = step("ve", _pev, after=[t_pm])

                def _pship(e, st, w=w, qt=qt, hA=hA, hB=hB):
                    st.dma(e.dma_start(out=g["ps_scr"][hA, qt, :, w * JC:(w + 1) * JC],
                                       in_=posbuf[:, 0, :]))
                    st.dma(e.dma_start(out=g["ps_scr"][hB, qt, :, w * JC:(w + 1) * JC],
                                       in_=posbuf[:, 1, :]))
                    return None
                t_ps = step("sy", _pship, after=[t_pv])
                posev.append(t_ps)
                last_pv[0] = t_pv

            pst = g["ps_scr"].tensor

            def _shear(e, st, qt=qt, hA=hA, hB=hB):
                baseA = ((hA * 16 + qt) * 128) * PW + 127
                baseB = ((hB * 16 + qt) * 128) * PW + 127
                st.dma(e.dma_start(out=S_A[:, :], in_=AP(pst, baseA, [[PW - 1, 128], [1, T]]),
                                   accum_op=ALU.add))
                st.dma(e.dma_start(out=S_B[:, :], in_=AP(pst, baseB, [[PW - 1, 128], [1, T]]),
                                   accum_op=ALU.add))
                return None
            t_sh = step("gp", _shear, after=[posev[-1], t_ca, t_cb])

            def _sm1(e, st):
                st.sync(e, e.reduce_max(sb["nmcol"][:, :], S_A[:, :], axis=AX, negate=True))
                e.tensor_scalar(sb["nmcol"][:, :], sb["nmcol"][:, :], 0.125, None, ALU.mult)
                st.sync(e, e.reduce_max(sb["sumcol"][:, :], S_B[:, :], axis=AX, negate=True))
                return e.tensor_scalar(sb["sumcol"][:, :], sb["sumcol"][:, :], 0.125,
                                       None, ALU.mult)
            t_s1 = step("ve", _sm1, after=[t_sh])

            def _sm2(e, st):
                e.activation(P_A[:, :], S_A[:, :], AF.Exp, bias=sb["nmcol"][:, :],
                             scale=0.125, accum_out=sb["accA"][:, :])
                return e.activation(P_B[:, :], S_B[:, :], AF.Exp, bias=sb["sumcol"][:, :],
                                    scale=0.125, accum_out=sb["accB"][:, :])
            t_s2 = step("ac", _sm2, after=[t_s1])

            def _sm3(e, st):
                e.reciprocal(sb["accA"][:, :], sb["accA"][:, :])
                st.sync(e, e.reciprocal(sb["accB"][:, :], sb["accB"][:, :]))
                e.tensor_scalar(P_A[:, :], P_A[:, :], sb["accA"][:, :], None, ALU.mult)
                return e.tensor_scalar(P_B[:, :], P_B[:, :], sb["accB"][:, :],
                                       None, ALU.mult)
            t_s3 = step("ve", _sm3, after=[t_s2])

            def _pship2(e, st, qt=qt, hA=hA, hB=hB):
                st.dma(e.dma_start(out=g["p_scr"][hA, qt * 128:(qt + 1) * 128, :],
                                   in_=P_A[:, :]))
                st.dma(e.dma_start(out=g["p_scr"][hB, qt * 128:(qt + 1) * 128, :],
                                   in_=P_B[:, :]))
                return None
            pprev = step("sy", _pship2, after=[t_s3])
            prev_cb = last_pv[0]
            prev_s2 = t_s2

        av_prev = pprev
        ld_toks = []
        for stp in range(16):
            par = stp % 2

            def _ldpt(e, st, stp=stp, par=par, hA=hA, hB=hB):
                st.dma(e.dma_start_transpose(
                    PT_A[:, par, :], g["p_scr"][hA, :, stp * 128:(stp + 1) * 128]))
                st.dma(e.dma_start_transpose(
                    PT_B[:, par, :], g["p_scr"][hB, :, stp * 128:(stp + 1) * 128]))
                return None
            dpl = [av_prev] if stp < 2 else [av_prev, ld_toks[stp - 2][1]]
            t_lp = step("ac", _ldpt, after=dpl)

            def _avmm(e, st, stp=stp, par=par):
                banksB = (ps["B"], ps["C"], ps["D"], ps["E"])
                last = None
                for qc in range(4):
                    e.matmul(ps["A"][0:64, qc * JC:(qc + 1) * JC],
                             vb_pair[:, stp, 0:64], PT_A[:, par, qc * JC:(qc + 1) * JC],
                             start=(stp == 0), stop=(stp == 15))
                    last = e.matmul(banksB[qc][64:128, :],
                                    vb_pair[:, stp, 64:128],
                                    PT_B[:, par, qc * JC:(qc + 1) * JC],
                                    start=(stp == 0), stop=(stp == 15),
                                    tile_position=(0, 64))
                return last
            t_av = step("pe", _avmm, after=[t_lp] if stp == 0 else [t_lp, ld_toks[-1][1]])
            ld_toks.append((t_lp, t_av))

        def _ctxev(e, st, p=p):
            banksB = (ps["B"], ps["C"], ps["D"], ps["E"])
            e.tensor_copy(ctx_all[0:64, p, :], ps["A"][0:64, :])
            last = None
            for qc in range(4):
                last = e.tensor_copy(ctx_all[64:128, p, qc * JC:(qc + 1) * JC],
                                     banksB[qc][64:128, :])
            return last
        step("ve", _ctxev, after=[ld_toks[-1][1]])

    bar()
    attn_a()
    if sub < 99 and stage == 1:
        pass
    if stage >= 2:
        _r = ffn(0, sub)
        if _r is None:
            bar()
            src = _gate_store[0] if _gate_store[0] is not None else x_sb
            if isinstance(src, tuple):
                kindg, tile = src
                if kindg == "bf16":
                    def _gstore(e, st, tile=tile):
                        st.dma(e.dma_start(out=y_d.rearrange("(c p) t -> p c t", p=128),
                                           in_=tile[:, :, :]))
                        return None
                    tg = step("gp", _gstore)
                elif kindg == "p128":
                    def _gstore(e, st, tile=tile):
                        st.dma(e.dma_start(
                            out=y_d[0:128, 0:128],
                            in_=tile[:, :, :].rearrange("p a b -> p (a b)")))
                        return None
                    tg = step("sy", _gstore)
                else:
                    def _gstore(e, st, tile=tile):
                        st.dma(e.dma_start(out=y_d[0:8, :], in_=tile[0:8, :]))
                        return None
                    tg = step("sy", _gstore)
            else:
                def _gstore(e, st, src=src):
                    st.dma(e.dma_start(out=y_d.rearrange("(c p) t -> p c t", p=128),
                                       in_=src[:, :, :]))
                    return None
                tg = step("sy", _gstore)
            step("sy", lambda e, st: e.nop(), after=[tg])
            sch.flush()
            return nc
        bar()
    if stage >= 3:
        attn_b()
    if stage >= 4:
        ffn(1)
        bar()

    def _store(e, st):
        st.dma(e.dma_start(out=y_d.rearrange("(c p) t -> p c t", p=128),
                           in_=x_sb[:, :, :]))
        return None
    t_out = step("sy", _store)
    step("sy", lambda e, st: e.nop(), after=[t_out])
    sch.flush()
    return nc



def kernel(**inputs):
    shared = _host_layout(inputs)
    nc = build(4)
    in_maps = [_core_inputs(shared, inputs, c) for c in range(NCORES)]
    res = run_bass_kernel_spmd(nc, in_maps, core_ids=list(range(NCORES)))
    out = np.zeros((NCORES, T, D), np.float32)
    for c in range(NCORES):
        out[c] = np.asarray(res.results[c]["y"], np.float32).T
    return out
